# version 19
# baseline (speedup 1.0000x reference)
import numpy as np
import ml_dtypes
from contextlib import ExitStack

import concourse.bass as bass
import concourse.mybir as mybir
from concourse.bass_utils import run_bass_kernel_spmd

F32 = mybir.dt.float32
BF16 = mybir.dt.bfloat16
I32 = mybir.dt.int32
AF = mybir.ActivationFunctionType
ALU = mybir.AluOpType
AX = mybir.AxisListType.X
BF = ml_dtypes.bfloat16

D = 1024
T = 4096
H = 16
NT = 8
TT = 512
NSUB = 4
NEXP = 18
PASSES = (4, 4, 4, 4, 2)
ENGS = ("sync", "tensor", "vector", "scalar", "gpsimd")



def _qk_perm():
    p64 = np.concatenate([np.arange(0, 64, 2), np.arange(1, 64, 2)])
    return np.concatenate([h * 64 + p64 for h in range(H)])


def host_constants():
    c = {}
    theta = 1.0 / 10000.0 ** (np.arange(32, dtype=np.float64) / 32.0)
    ang = np.arange(T, dtype=np.float64)[None, :] * theta[:, None]
    cos, sin = np.cos(ang), np.sin(ang)
    C = np.zeros((128, T), np.float64)
    S = np.zeros((128, T), np.float64)
    for blk in (0, 64):
        C[blk:blk + 32] = cos
        C[blk + 32:blk + 64] = cos
        S[blk:blk + 32] = -sin
        S[blk + 32:blk + 64] = sin
    c["crep"] = C.astype(BF)
    c["srep"] = S.astype(BF)
    P = np.zeros((128, 128), np.float32)
    for blk in (0, 64):
        for j in range(32):
            P[blk + j, blk + 32 + j] = 1.0
            P[blk + 32 + j, blk + j] = 1.0
    c["pswap"] = P.astype(BF)
    c["identb"] = np.eye(128, dtype=np.float32).astype(BF)
    c["identf"] = np.eye(128, dtype=np.float32)
    hs = np.zeros((128, 128), np.float32)
    for oc in range(8):
        hs[0:64, oc * 16 + 2 * oc] = 1.0
        hs[64:128, oc * 16 + 2 * oc + 1] = 1.0
    c["hsall"] = hs.astype(BF)
    e4 = np.zeros((128, 64), np.float32)
    for r in range(4):
        e4[:, r * 16 + r] = 1.0
    c["e4"] = e4.astype(BF)
    selp = np.zeros((128, 8 * 128), np.float32)
    for p in range(8):
        for i in range(128):
            h = 2 * p + (1 if i >= 64 else 0)
            selp[32 * (h % 4) + h // 4, p * 128 + i] = 1.0
    c["selp"] = selp
    m = np.zeros((128, 4 * 512), np.float32)
    for b in range(4):
        s = np.arange(128)[:, None] + 128 * b
        t = np.arange(512)[None, :]
        m[:, b * 512:(b + 1) * 512] = (s <= t)
    c["mask4"] = m.astype(BF)
    selr = np.zeros((16, 16 * 128), np.float32)
    for r in range(16):
        selr[r, r * 128:(r + 1) * 128] = 1.0
    c["selr"] = selr.astype(BF)
    c["ones1"] = np.ones((1, 128), np.float32)
    return c



class Buf:
    __slots__ = ("free", "ready")

    def __init__(self):
        self.free = None
        self.ready = None


class Planner:
    def __init__(self):
        self.ops = {e: [] for e in ENGS}
        self.cnt = {}
        self.sem_handles = {}
        self._gate = {e: None for e in ENGS}

    def sem(self, name):
        self.cnt.setdefault(name, 0)
        return name

    def now(self, sem):
        return (sem, self.cnt.get(sem, 0))

    def gate_all(self):
        snap = tuple((s, v) for s, v in self.cnt.items() if v > 0)
        for e in ENGS:
            self._gate[e] = snap

    def emit(self, eng, build, waits=(), inc=None, inc_amt=1, tag=""):
        waits = [w for w in waits if w is not None]
        for s, _ in waits:
            self.sem(s)
        if self._gate[eng]:
            waits = list(self._gate[eng]) + list(waits)
            self._gate[eng] = None
        val = None
        if inc is not None:
            self.sem(inc)
            self.cnt[inc] += inc_amt
            val = self.cnt[inc]
        waits = tuple((s, v) for (s, v) in waits if v > 0)

        def closure(E, waits=waits, build=build, inc=inc, inc_amt=inc_amt):
            for s, v in waits:
                E.wait_ge(self.sem_handles[s], v)
            ins = build(E)
            if inc is not None:
                if ins is None:
                    ins = E.nop()
                ins.then_inc(self.sem_handles[inc], inc_amt)
            return ins

        closure.tag = tag
        closure.waits = waits
        closure.inc = (inc, val, inc_amt) if inc else None
        self.ops[eng].append(closure)
        return self.now(inc) if inc else None

    def dma(self, eng, out_ap, in_ap, waits=(), inc=None, tag="", **kw):
        return self.emit(eng, lambda E: E.dma_start(out_ap, in_ap, **kw),
                         waits=waits, inc=inc, inc_amt=16, tag=tag)

    def simulate(self):
        idx = {e: 0 for e in ENGS}
        sem = {s: 0 for s in self.cnt}
        progress = True
        while progress:
            progress = False
            for e in ENGS:
                while idx[e] < len(self.ops[e]):
                    op = self.ops[e][idx[e]]
                    if all(sem[s] >= v for s, v in op.waits):
                        if op.inc:
                            sem[op.inc[0]] += op.inc[2]
                        idx[e] += 1
                        progress = True
                    else:
                        break
        stuck = {e: idx[e] for e in ENGS if idx[e] < len(self.ops[e])}
        if stuck:
            lines = []
            for e, i in stuck.items():
                op = self.ops[e][i]
                miss = [(s, v, sem.get(s, 0)) for s, v in op.waits
                        if sem.get(s, 0) < v]
                lines.append(f"  {e}[{i}/{len(self.ops[e])}] tag={op.tag} missing={miss}")
            raise RuntimeError("planned-program deadlock:\n" + "\n".join(lines))

    def n_ops(self):
        return {e: len(self.ops[e]) for e in ENGS}



class Build:
    def __init__(self, taps=(), phases=("W", "1a", "1b", "2"), nt2=NT,
                 p2parts=("router", "experts", "ln4")):
        self.taps = set(taps)
        self.phases = phases
        self.nt2 = nt2
        self.p2parts = set(p2parts)
        self.nc = bass.Bass("TRN2", target_bir_lowering=False)
        self.P = Planner()
        self.st = ExitStack()

    def sb(self, name, shape, dtype, scope=None):
        return (scope or self.st).enter_context(
            self.nc.sbuf_tensor(name, list(shape), dtype))

    def ps(self, name, shape, dtype=F32, scope=None):
        return (scope or self.st).enter_context(
            self.nc.psum_tensor(name, list(shape), dtype))

    def dram(self, name, shape, dtype, kind="ExternalInput"):
        return self.nc.dram_tensor(name, list(shape), dtype, kind=kind)

    def scratch(self, name, shape, dtype):
        kind = "ExternalOutput" if name in self.taps else "Internal"
        return self.nc.dram_tensor(name, list(shape), dtype, kind=kind)

    def newton_rsqrt(self, dst, src, pre=(), tag="nwt"):
        P = self.P
        p = src.shape[0]
        n = src.shape[1]
        t1 = self.nw1[0:p, 0:n]
        t2 = self.nw2[0:p, 0:n]
        th = self.nw3[0:p, 0:n]
        P.emit("vector", lambda E: E.memset(t1.bitcast(I32), 0x5F3759DF),
               waits=pre, inc="v", tag=tag + "_m")
        P.emit("vector", lambda E: E.tensor_scalar(
            t2.bitcast(I32), src.bitcast(I32), 1, None,
            op0=ALU.logical_shift_right), waits=(P.now("v"),), inc="v",
            tag=tag + "_sh")
        P.emit("vector", lambda E: E.tensor_tensor(
            dst.bitcast(I32), t1.bitcast(I32), t2.bitcast(I32),
            op=ALU.subtract), waits=(P.now("v"),), inc="v", tag=tag + "_y0")
        P.emit("vector", lambda E: E.tensor_scalar_mul(th, src, 0.5),
               waits=(P.now("v"),), inc="v", tag=tag + "_h")
        for it in range(3):
            P.emit("vector", lambda E: E.tensor_mul(t2, dst, dst),
                   waits=(P.now("v"),), inc="v", tag=tag + f"_a{it}")
            P.emit("vector", lambda E: E.tensor_mul(t2, th, t2),
                   waits=(P.now("v"),), inc="v", tag=tag + f"_b{it}")
            P.emit("vector", lambda E: E.tensor_scalar(
                t2, t2, -1.0, 1.5, op0=ALU.mult, op1=ALU.add),
                waits=(P.now("v"),), inc="v", tag=tag + f"_c{it}")
            P.emit("vector", lambda E: E.tensor_mul(dst, dst, t2),
                   waits=(P.now("v"),), inc="v", tag=tag + f"_d{it}")
        return P.now("v")

    def ln_stats(self, src_fn, xwait, tag, n=NSUB, extra_s2_wait=(),
                 serialize_ps=False):
        P = self.P
        stats = self.stats
        sw = [xwait] + list(extra_s2_wait)
        if self.stats_free.free:
            sw.append(self.stats_free.free)
        for u in range(n):
            P.emit("vector", (lambda E, o=stats[:, u:u + 1], i_=src_fn(u):
                              E.reduce_sum(o, i_, axis=AX)),
                   waits=tuple(sw) if u == 0 else (), inc="v",
                   tag=f"{tag}_s1_{u}")
            s2w = [P.now("s")] if u > 0 else list(sw)
            if serialize_ps:
                s2w.append(P.now("v"))
            P.emit("scalar", (lambda E, i_=src_fn(u), a=stats[:, 4 + u:5 + u]:
                              E.activation(self.sqdump[:, 0:i_.shape[1]], i_,
                                           AF.Square, accum_out=a)),
                   waits=tuple(s2w), inc="s",
                   tag=f"{tag}_s2_{u}")
        S1, S2 = P.now("v"), P.now("s")
        P.emit("vector", lambda E: E.tensor_scalar_mul(
            stats[:, 8:8 + n], stats[:, 0:n], 1.0 / 1024),
            waits=(S1, S2), inc="v", tag=f"{tag}_mu")
        P.emit("vector", lambda E: E.tensor_mul(
            stats[:, 16:16 + n], stats[:, 8:8 + n], stats[:, 8:8 + n]),
            waits=(P.now("v"),), inc="v", tag=f"{tag}_mu2")
        P.emit("vector", lambda E: E.tensor_scalar_mul(
            stats[:, 12:12 + n], stats[:, 4:4 + n], 1.0 / 1024),
            waits=(P.now("v"),), inc="v", tag=f"{tag}_va")
        P.emit("vector", lambda E: E.tensor_tensor(
            stats[:, 12:12 + n], stats[:, 12:12 + n], stats[:, 16:16 + n],
            op=ALU.subtract), waits=(P.now("v"),), inc="v", tag=f"{tag}_vb")
        P.emit("vector", lambda E: E.tensor_scalar_add(
            stats[:, 12:12 + n], stats[:, 12:12 + n], 1e-5),
            waits=(P.now("v"),), inc="v", tag=f"{tag}_vc")
        self.newton_rsqrt(stats[:, 20:20 + n], stats[:, 12:12 + n],
                          pre=(P.now("v"),), tag=f"{tag}_nw")
        P.emit("vector", lambda E: E.tensor_mul(
            stats[:, 24:24 + n], stats[:, 8:8 + n], stats[:, 20:20 + n]),
            waits=(P.now("v"),), inc="v", tag=f"{tag}_nm")
        P.emit("vector", lambda E: E.tensor_scalar_mul(
            stats[:, 24:24 + n], stats[:, 24:24 + n], -1.0),
            waits=(P.now("v"),), inc="v", tag=f"{tag}_nm2")
        return P.now("v")

    def build(self):
        nc, P = self.nc, self.P
        self.xin = self.dram("xin", [T, D], F32)
        self.wsrc = {n: self.dram("w" + n, [D, D], F32) for n in "qkvo"}
        self.bqkv_d = self.dram("bqkv", [128, 24], F32)
        self.bo_d = self.dram("bo_in", [1, D], F32)
        self.rw_d = self.dram("rw_in", [16, D], F32)
        self.sup_d = self.dram("sup", [2, 512, D], F32)
        self.sdn_d = self.dram("sdn", [2, D, 512], F32)
        self.rup_d = self.dram("rup", [16, 512, D], F32)
        self.rdn_d = self.dram("rdn", [16, D, 512], F32)
        cd = {}
        for n, sh, dt in (("crep", [128, T], BF16), ("srep", [128, T], BF16),
                          ("pswap", [128, 128], BF16), ("identb", [128, 128], BF16),
                          ("identf", [128, 128], F32), ("hsall", [128, 128], BF16),
                          ("e4", [128, 64], BF16), ("selp", [128, 1024], F32),
                          ("mask4", [128, 2048], BF16), ("selr", [16, 2048], BF16),
                          ("ones1", [1, 128], F32)):
            cd[n] = self.dram(n, sh, dt)
        self.cd = cd
        self.out_d = self.dram("out", [T, D], F32, kind="ExternalOutput")
        self.qf_s = self.scratch("qf_s", [NT, 128, 4096], BF16)
        self.kf_s = self.scratch("kf_s", [NT, 128, 4096], BF16)
        self.vt_s = self.scratch("vt_s", [NT, 128, 4096], BF16)
        self.x1_s = self.scratch("x1_s", [NT, 128, 4096], F32)
        self.wot_s = self.scratch("wot_s", [128, 8192], BF16)
        self.rwt_s = self.scratch("rwt_s", [128, 128], BF16)
        self.utp_s = self.scratch("utp_s", [NEXP, 128, 4096], BF16)
        self.dtp_s = self.scratch("dtp_s", [NEXP, 128, 4096], BF16)

        self.crep = [self.sb(f"crep_sb{j}", [128, 512], BF16) for j in range(2)]
        self.srep = [self.sb(f"srep_sb{j}", [128, 512], BF16) for j in range(2)]
        self.pswap = self.sb("pswap_sb", [128, 128], BF16)
        self.identb = self.sb("identb_sb", [128, 128], BF16)
        self.identf = self.sb("identf_sb", [128, 128], F32)
        self.hsall = self.sb("hsall_sb", [128, 128], BF16)
        self.ones1 = self.sb("ones1_sb", [1, 128], F32)
        self.bqkv = self.sb("bqkv_sb", [128, 24], F32)
        self.bo_sb = self.sb("bo_sb", [1, 1024], F32)
        self.runmax = self.sb("runmax_sb", [16, 16], F32)
        self.bias_ch = self.sb("bias_ch", [128, 16], F32)
        self.bias_pr = self.sb("bias_pr", [128, 8], F32)
        self.bo_rep = self.sb("bo_rep", [128, 1024], F32)
        self.stats = self.sb("stats_sb", [128, 32], F32)
        self.nw1 = self.sb("nw1", [128, 8], F32)
        self.nw2 = self.sb("nw2", [128, 8], F32)
        self.nw3 = self.sb("nw3", [128, 8], F32)
        self.sqdump = self.sb("sqdump", [128, 1024], BF16)
        self.stats_free = Buf()

        for n, t in (("pswap", self.pswap), ("identb", self.identb),
                     ("identf", self.identf), ("hsall", self.hsall),
                     ("ones1", self.ones1)):
            P.dma("sync", t[:, :], cd[n][:, :], inc="d_c", tag="c_" + n)
        P.dma("sync", self.bqkv[:, :], self.bqkv_d[:, :], inc="d_c", tag="c_bqkv")
        P.dma("sync", self.bo_sb[:, :], self.bo_d[:, :], inc="d_c", tag="c_bo")
        self.CONSTS = P.now("d_c")

        with ExitStack() as s1a:
            self.phase_w_attn(s1a)
            if "1a" in self.phases:
                self.phase_1a(s1a)
                self.phase_15()
        if "1b" in self.phases:
            P.gate_all()
            with ExitStack() as s1b:
                self.phase_1b(s1b)
        if "2" in self.phases:
            P.gate_all()
            with ExitStack() as s2:
                self.phase_2(s2)
        fin = [P.now("d_o")] if P.cnt.get("d_o") else []
        for s in ("d_spq", "d_spk", "d_spv", "d_x1w", "d_fwst0", "d_fwst1",
                  "d_woS", "d_rwS", "d_dbg"):
            if P.cnt.get(s):
                fin.append(P.now(s))
        P.emit("sync", lambda E: E.nop(), waits=tuple(fin), tag="final")
        P.simulate()
        return self

    def phase_w_attn(self, sc):
        nc, P = self.nc, self.P
        self.wstage = self.sb("wstage", [128, 8192], BF16, sc)
        self.wtrg = {n: self.sb(f"w{n}t_sb", [128, 8192], BF16, sc)
                     for n in "qkv"}
        self.fin = [self.sb(f"fw_in{j}", [128, 4096], BF16, sc) for j in range(2)]
        self.fout = [self.sb("fw_out0", [128, 4096], BF16, sc)] * 2
        self.fin_free = [Buf(), Buf()]
        _fb = Buf()
        self.fout_free = [_fb, _fb]
        self.psW = self.ps("psW", [128, 512], BF16, sc)
        self.psW2 = self.ps("psW2", [128, 512], BF16, sc)
        self.psw_free = [Buf(), Buf()]
        wstage_free = Buf()
        self.attnw_ready = {}
        for wn in "qkvo":
            dst = self.wtrg[wn] if wn != "o" else self.fout[0]
            P.dma("gpsimd", self.wstage[:, :].rearrange("p (c d) -> p c d", c=8),
                  self.wsrc[wn][:, :].rearrange("(c p) d -> p c d", p=128),
                  waits=(wstage_free.free,) if wstage_free.free else (),
                  inc="d_wA", tag=f"w{wn}_ld")
            LD = P.now("d_wA")
            for dc in range(8):
                for half in range(2):
                    k = (dc * 2 + half) % 2
                    pw = (self.psW, self.psW2)[k]
                    pb = self.psw_free[k]
                    for oq in range(4):
                        oc = half * 4 + oq
                        w = [LD, self.CONSTS] if oq == 0 else []
                        if oq == 0 and pb.free:
                            w.append(pb.free)
                        if (wn == "o" and oq == 0 and half == 0 and dc in (0, 4)
                                and self.fout_free[0].free):
                            w.append(self.fout_free[0].free)
                        P.emit("tensor",
                               (lambda E, o=pw[:, oq * 128:(oq + 1) * 128],
                                i_=self.wstage[:, oc * 1024 + dc * 128:
                                               oc * 1024 + dc * 128 + 128]:
                                E.transpose(o, i_, self.identb[:, :])),
                               waits=tuple(w), inc="t" if oq == 3 else None,
                               tag=f"w{wn}_t{dc}{half}")
                    dcol = (dc * 1024 + half * 512) if wn != "o" else \
                        ((dc % 4) * 1024 + half * 512)
                    P.emit("vector",
                           (lambda E, d=dst[:, dcol:dcol + 512],
                            sp=pw[:, :]: E.tensor_copy(d, sp)),
                           waits=(P.now("t"),), inc="v", tag=f"w{wn}_e{dc}{half}")
                    pb.free = P.now("v")
                if wn == "o" and dc in (3, 7):
                    g = dc // 4
                    P.dma("sync", self.wot_s[:, g * 4096:(g + 1) * 4096],
                          self.fout[0][:, :], waits=(P.now("v"),),
                          inc="d_woS", tag=f"wo_st{g}")
                    self.fout_free[0].free = P.now("d_woS")
            self.attnw_ready[wn] = P.now("v")
            wstage_free.free = P.now("v")

        rw_stage = self.sb("rw_stage", [16, 1024], BF16, sc)
        rwt_stage = self.sb("rwt_stage", [128, 128], BF16, sc)
        P.dma("gpsimd", rw_stage[:, :], self.rw_d[:, :],
              waits=(wstage_free.free,), inc="d_wA", tag="wrw_ld")
        RWL = P.now("d_wA")
        for g in range(2):
            pw = (self.psW, self.psW2)[g]
            pb = self.psw_free[g]
            for q in range(4):
                dc = g * 4 + q
                w = [RWL, self.CONSTS] if q == 0 else []
                if q == 0 and pb.free:
                    w.append(pb.free)
                P.emit("tensor",
                       (lambda E, o=pw[:, q * 16:(q + 1) * 16],
                        i_=rw_stage[:, dc * 128:(dc + 1) * 128]:
                        E.transpose(o, i_, self.identb[0:16, 0:16])),
                       waits=tuple(w), inc="t" if q == 3 else None,
                       tag=f"wrw_t{dc}")
            P.emit("vector", (lambda E, d=rwt_stage[:, g * 64:(g + 1) * 64],
                              sp=pw[:, 0:64]: E.tensor_copy(d, sp)),
                   waits=(P.now("t"),), inc="v", tag=f"wrw_e{g}")
            pb.free = P.now("v")
        P.dma("sync", self.rwt_s[:, :], rwt_stage[:, :], waits=(P.now("v"),),
              inc="d_rwS", tag="wrw_st")

        mats = []
        for ei in range(16):
            mats.append((self.rup_d[ei].rearrange("(c p) d -> p c d", p=128),
                         4, self.utp_s[ei], f"fu{ei}"))
            mats.append((self.rdn_d[ei].rearrange("(c p) e -> p c e", p=128),
                         8, self.dtp_s[ei], f"fd{ei}"))
        for si in range(2):
            mats.append((self.sup_d[si].rearrange("(c p) d -> p c d", p=128),
                         4, self.utp_s[16 + si], f"fu{16 + si}"))
            mats.append((self.sdn_d[si].rearrange("(c p) e -> p c e", p=128),
                         8, self.dtp_s[16 + si], f"fd{16 + si}"))
        self.ffn_mats = mats
        self.ffn_load_i = 0
        self.ffn_comp_i = 0
        self.ffn_ld_ms = {}

    def _ffn_load(self, mi):
        P = self.P
        src, nch, dst, tag = self.ffn_mats[mi]
        bi = mi % 2
        P.dma("gpsimd",
              self.fin[bi][:, :].rearrange("p (c d) -> p c d", c=nch), src,
              waits=(self.fin_free[bi].free,) if self.fin_free[bi].free else (),
              inc=f"d_f{bi}", tag=tag + "_ld")
        self.ffn_ld_ms[mi] = P.now(f"d_f{bi}")

    def _ffn_comp(self, mi):
        P = self.P
        src, nch, dst, tag = self.ffn_mats[mi]
        bi = mi % 2
        LD = self.ffn_ld_ms[mi]
        fr = 4096 // nch
        nblk = fr // 128
        ow = self.fout[bi]
        first = True
        for b in range(nblk):
            ngr = max(1, nch // 4)
            for g in range(ngr):
                k = (b * ngr + g) % 2
                pw = (self.psW, self.psW2)[k]
                pb = self.psw_free[k]
                span = min(4, nch)
                for q in range(span):
                    c_in = g * 4 + q
                    w = []
                    if q == 0:
                        w = [LD, self.CONSTS]
                        if pb.free:
                            w.append(pb.free)
                        if first and self.fout_free[bi].free:
                            w.append(self.fout_free[bi].free)
                        first = False
                    P.emit("tensor",
                           (lambda E, o=pw[:, q * 128:(q + 1) * 128],
                            i_=self.fin[bi][:, c_in * fr + b * 128:
                                            c_in * fr + b * 128 + 128]:
                            E.transpose(o, i_, self.identb[:, :])),
                           waits=tuple(w), inc="t" if q == span - 1 else None,
                           tag=tag + f"_t{b}{g}")
                P.emit("vector",
                       (lambda E, d=ow[:, b * (nch * 128) + g * 512:
                                       b * (nch * 128) + g * 512 + span * 128],
                        sp=pw[:, 0:span * 128]: E.tensor_copy(d, sp)),
                       waits=(P.now("t"),), inc="v", tag=tag + f"_e{b}{g}")
                pb.free = P.now("v")
        self.fin_free[bi].free = P.now("v")
        P.dma("sync", dst[:, :], ow[:, :], waits=(P.now("v"),),
              inc=f"d_fwst{bi}", tag=tag + "_st")
        self.fout_free[bi].free = P.now(f"d_fwst{bi}")

    def emit_ffn_prep(self, nmat, flush=False):
        n = len(self.ffn_mats)
        target = min(self.ffn_load_i + nmat, n)
        while (self.ffn_load_i < target
               or (flush and self.ffn_comp_i < n)):
            if (self.ffn_load_i < target
                    and self.ffn_load_i - self.ffn_comp_i < 2):
                self._ffn_load(self.ffn_load_i)
                self.ffn_load_i += 1
            elif self.ffn_comp_i < self.ffn_load_i:
                self._ffn_comp(self.ffn_comp_i)
                self.ffn_comp_i += 1
            else:
                break

    def phase_1a(self, sc):
        nc, P = self.nc, self.P
        x_sb = [self.sb(f"x_sb{j}", [128, 4096], F32, sc) for j in range(2)]
        xn_tok = self.sb("xn_tok", [128, 4096], BF16, sc)
        xn_f = self.sb("xn_f", [128, 4096], BF16, sc)
        proj = [self.sb(f"proj{j}", [128, 4096], BF16, sc) for j in range(3)]
        qrot = self.sb("qrot", [128, 4096], BF16, sc)
        krot = self.sb("krot", [128, 4096], BF16, sc)
        vtok = self.sb("vtok", [128, 4096], BF16, sc)
        sqn = self.sb("sqn", [128, 1024], BF16, sc)
        ttmp = self.sb("ttmp", [128, 512], F32, sc)
        psQ = self.ps("psQ", [128, 512], F32, sc)
        psQ2 = self.ps("psQ2", [128, 512], F32, sc)
        psSw = self.ps("psSw", [128, 512], F32, sc)
        psSw2 = self.ps("psSw2", [128, 512], F32, sc)
        psNorm = self.ps("psNorm", [16, 512], F32, sc)
        self.psNorm = psNorm
        self.psQ_ref = psQ

        x_free = [Buf(), Buf()]
        xnf_free = Buf()
        xntok_free = Buf()
        proj_free = [Buf(), Buf(), Buf()]
        rot_free = {"q": Buf(), "k": Buf(), "v": Buf()}
        psq_free = [Buf(), Buf()]
        self.psq_free_ref = psq_free
        pssw_free = [Buf(), Buf()]
        self.psnorm_free = Buf()
        cs_free = [Buf(), Buf()]
        sqn_free = [Buf(), Buf()]
        xn_ms = [None, None]
        csl_ms = [None, None]

        xl_ms = [None, None]

        def emit_x_load(ii):
            bi = ii % 2
            xb = x_sb[bi]
            P.dma("sync", xb[:, :].rearrange("p (u d) -> p u d", u=4),
                  self.xin[ii * TT:(ii + 1) * TT, :].rearrange(
                      "(u p) d -> p u d", p=128),
                  waits=(x_free[bi].free,) if x_free[bi].free else (),
                  inc=f"d_x{bi}", tag=f"x_ld{ii}")
            xl_ms[bi] = P.now(f"d_x{bi}")
            P.dma("sync", self.crep[bi][:, :],
                  self.cd["crep"][:, ii * TT:(ii + 1) * TT],
                  waits=(cs_free[bi].free,) if cs_free[bi].free else (),
                  inc="d_cs", tag=f"cs_c{ii}")
            P.dma("sync", self.srep[bi][:, :],
                  self.cd["srep"][:, ii * TT:(ii + 1) * TT],
                  inc="d_cs", tag=f"cs_s{ii}")
            csl_ms[bi] = P.now("d_cs")

        def emit_ln1(ii):
            bi = ii % 2
            xb = x_sb[bi]
            XL = xl_ms[bi]
            LN1 = self.ln_stats(lambda u: xb[:, u * 1024:(u + 1) * 1024],
                                XL, f"ln1_{ii}", extra_s2_wait=(self.CONSTS,))
            xw = [LN1]
            if xntok_free.free:
                xw.append(xntok_free.free)
            for u in range(NSUB):
                P.emit("scalar",
                       (lambda E, o=xn_tok[:, u * 1024:(u + 1) * 1024],
                        i_=xb[:, u * 1024:(u + 1) * 1024],
                        scv=self.stats[:, 20 + u:21 + u],
                        bv=self.stats[:, 24 + u:25 + u]:
                        E.activation(o, i_, AF.Identity, bias=bv, scale=scv)),
                       waits=tuple(xw) if u == 0 else (), inc="s",
                       tag=f"ln1n_{ii}_{u}")
            XN = P.now("s")
            self.stats_free.free = XN
            x_free[bi].free = XN
            xn_ms[bi] = XN

        emit_x_load(0)
        emit_ln1(0)
        for i in range(NT):
            bi = i % 2
            XN = xn_ms[bi]
            CSL = csl_ms[i % 2]

            tw = [XN]
            if xnf_free.free:
                tw.append(xnf_free.free)
            for c in range(8):
                k = c % 2
                pw = (psQ, psQ2)[k]
                pb = psq_free[k]
                for u in range(NSUB):
                    w = list(tw) if (c == 0 and u == 0) else []
                    if u == 0 and pb.free:
                        w.append(pb.free)
                    P.emit("tensor",
                           (lambda E, o=pw.bitcast(BF16)[:, u * 128:(u + 1) * 128],
                            i_=xn_tok[:, u * 1024 + c * 128:u * 1024 + c * 128 + 128]:
                            E.transpose(o, i_, self.identb[:, :])),
                           waits=tuple(w), inc="t" if u == 3 else None,
                           tag=f"xnt{i}_{c}{u}")
                P.emit("vector",
                       (lambda E, d=xn_f[:, c * 512:(c + 1) * 512],
                        sp=pw.bitcast(BF16)[:, 0:512]: E.tensor_copy(d, sp)),
                       waits=(P.now("t"),), inc="v", tag=f"xne{i}_{c}")
                pb.free = P.now("v")
            XNF = P.now("v")
            xntok_free.free = P.now("t")
            if i + 1 < NT:
                emit_x_load(i + 1)

            for pi, pn in enumerate("qkv"):
                wt = self.wtrg[pn]
                bcol = pi * 8
                projb = proj[pi]
                pj_w = [XNF, self.attnw_ready[pn]]
                if proj_free[pi].free:
                    pj_w.append(proj_free[pi].free)
                if pn in "qk":
                    rotb = qrot if pn == "q" else krot
                    rfree = rot_free[pn]
                    EV = {}
                    RP = {}
                    for step in range(10):
                        if step < 8:
                            oc = step
                            k = oc % 2
                            pw = (psQ, psQ2)[k]
                            pb = psq_free[k]
                            for dc in range(8):
                                w = list(pj_w) if (oc == 0 and dc == 0) else []
                                if dc == 0 and pb.free:
                                    w.append(pb.free)
                                P.emit("tensor",
                                       (lambda E, o=pw[:, :],
                                        l=wt[:, dc * 1024 + oc * 128:
                                             dc * 1024 + oc * 128 + 128],
                                        r=xn_f[:, dc * 512:(dc + 1) * 512],
                                        s0=(dc == 0), s1=(dc == 7):
                                        E.matmul(o, l, r, start=s0, stop=s1)),
                                       waits=tuple(w),
                                       inc="t" if dc == 7 else None,
                                       tag=f"{pn}mm{i}_{oc}{dc}")
                            pj_w = []
                            P.emit("scalar",
                                   (lambda E, o=projb[:, oc * 512:(oc + 1) * 512],
                                    sp=pw[:, :],
                                    bv=self.bqkv[:, bcol + oc:bcol + oc + 1]:
                                    E.activation(o, sp, AF.Identity, bias=bv)),
                                   waits=(P.now("t"),), inc="s",
                                   tag=f"{pn}ev{i}_{oc}")
                            pb.free = P.now("s")
                            EV[oc] = P.now("s")
                        if 1 <= step <= 8:
                            oc = step - 1
                            k = oc % 2
                            pw = (psSw, psSw2)[k]
                            pb = pssw_free[k]
                            w = [EV[oc], self.CONSTS]
                            if pb.free:
                                w.append(pb.free)
                            P.emit("tensor",
                                   (lambda E, o=pw[:, :],
                                    r=projb[:, oc * 512:(oc + 1) * 512]:
                                    E.matmul(o, self.pswap[:, :], r,
                                             start=True, stop=True)),
                                   waits=tuple(w), inc="t", tag=f"{pn}sw{i}_{oc}")
                            SW = P.now("t")
                            w2 = [SW, CSL]
                            if oc == 0 and rfree.free:
                                w2.append(rfree.free)
                            P.emit("vector",
                                   (lambda E, o=rotb[:, oc * 512:(oc + 1) * 512],
                                    a=projb[:, oc * 512:(oc + 1) * 512],
                                    cr=self.crep[i % 2][:, :]:
                                    E.tensor_mul(o, a, cr)),
                                   waits=tuple(w2), inc="v", tag=f"{pn}r1{i}_{oc}")
                            P.emit("vector",
                                   (lambda E, sp=pw[:, :],
                                    sr=self.srep[i % 2][:, :]:
                                    E.tensor_mul(ttmp[:, :], sp, sr)),
                                   waits=(P.now("v"),), inc="v",
                                   tag=f"{pn}r2{i}_{oc}")
                            pb.free = P.now("v")
                            P.emit("vector",
                                   (lambda E, o=rotb[:, oc * 512:(oc + 1) * 512]:
                                    E.tensor_add(o, o, ttmp[:, :])),
                                   waits=(P.now("v"),), inc="v",
                                   tag=f"{pn}r3{i}_{oc}")
                            RP[oc] = P.now("v")
                        if step >= 2:
                            oc = step - 2
                            k = oc % 2
                            w = [RP[oc]]
                            if oc == 0 and self.psnorm_free.free:
                                w.append(self.psnorm_free.free)
                            if sqn_free[k].free:
                                w.append(sqn_free[k].free)
                            P.emit("scalar",
                                   (lambda E, o=sqn[:, k * 512:(k + 1) * 512],
                                    i_=rotb[:, oc * 512:(oc + 1) * 512]:
                                    E.activation(o, i_, AF.Square)),
                                   waits=tuple(w), inc="s",
                                   tag=f"{pn}nsq{i}_{oc}")
                            P.emit("tensor",
                                   (lambda E, l=self.hsall[:, oc * 16:(oc + 1) * 16],
                                    r=sqn[:, k * 512:(k + 1) * 512],
                                    s0=(oc == 0), s1=(oc == 7):
                                    E.matmul(psNorm[:, :], l, r, start=s0, stop=s1,
                                             skip_group_check=True)),
                                   waits=(P.now("s"),), inc="t",
                                   tag=f"{pn}nmm{i}_{oc}")
                            sqn_free[k].free = P.now("t")
                    ROT = RP[7]
                    col = i if pn == "q" else 8 + i
                    P.emit("vector",
                           (lambda E, o=self.runmax[:, col:col + 1]:
                            E.reduce_max(o, psNorm[:, :], axis=AX)),
                           waits=(P.now("t"),), inc="v", tag=f"{pn}nrm{i}")
                    self.psnorm_free.free = P.now("v")
                    spsem = "d_spq" if pn == "q" else "d_spk"
                    P.dma("sync", (self.qf_s if pn == "q" else self.kf_s)[i][:, :],
                          rotb[:, :], waits=(ROT,), inc=spsem, tag=f"{pn}sp{i}")
                    rfree.free = P.now(spsem)
                    proj_free[pi].free = ROT
                    if pn == "k" and i + 1 < NT:
                        emit_ln1(i + 1)
                else:
                    for oc in range(8):
                        k = oc % 2
                        pw = (psQ, psQ2)[k]
                        pb = psq_free[k]
                        for dc in range(8):
                            w = list(pj_w) if (oc == 0 and dc == 0) else []
                            if dc == 0 and pb.free:
                                w.append(pb.free)
                            P.emit("tensor",
                                   (lambda E, o=pw[:, :],
                                    l=wt[:, dc * 1024 + oc * 128:
                                         dc * 1024 + oc * 128 + 128],
                                    r=xn_f[:, dc * 512:(dc + 1) * 512],
                                    s0=(dc == 0), s1=(dc == 7):
                                    E.matmul(o, l, r, start=s0, stop=s1)),
                                   waits=tuple(w), inc="t" if dc == 7 else None,
                                   tag=f"{pn}mm{i}_{oc}{dc}")
                        pj_w = []
                        P.emit("scalar",
                               (lambda E, o=projb[:, oc * 512:(oc + 1) * 512],
                                sp=pw[:, :],
                                bv=self.bqkv[:, bcol + oc:bcol + oc + 1]:
                                E.activation(o, sp, AF.Identity, bias=bv)),
                               waits=(P.now("t"),), inc="s", tag=f"{pn}ev{i}_{oc}")
                        pb.free = P.now("s")
                    PRJ = P.now("s")
                    vw = [PRJ]
                    if rot_free["v"].free:
                        vw.append(rot_free["v"].free)
                    for u in range(NSUB):
                        for half in range(2):
                            k = half
                            pw = (psSw, psSw2)[k]
                            pb = pssw_free[k]
                            for oq in range(4):
                                oc = half * 4 + oq
                                w = list(vw) if (u == 0 and half == 0 and oq == 0) else []
                                if oq == 0 and pb.free:
                                    w.append(pb.free)
                                P.emit("tensor",
                                       (lambda E,
                                        o=pw.bitcast(BF16)[:, oq * 128:(oq + 1) * 128],
                                        i_=projb[:, oc * 512 + u * 128:
                                                 oc * 512 + u * 128 + 128]:
                                        E.transpose(o, i_, self.identb[:, :])),
                                       waits=tuple(w),
                                       inc="t" if oq == 3 else None,
                                       tag=f"vt{i}_{u}{half}{oq}")
                            P.emit("vector",
                                   (lambda E, d=vtok[:, u * 1024 + half * 512:
                                                     u * 1024 + half * 512 + 512],
                                    sp=pw.bitcast(BF16)[:, 0:512]:
                                    E.tensor_copy(d, sp)),
                                   waits=(P.now("t"),), inc="v",
                                   tag=f"vte{i}_{u}{half}")
                            pb.free = P.now("v")
                        vw = []
                    VT = P.now("v")
                    P.dma("sync", self.vt_s[i][:, :], vtok[:, :], waits=(VT,),
                          inc="d_spv", tag=f"vsp{i}")
                    rot_free["v"].free = P.now("d_spv")
                    proj_free[pi].free = VT
            xnf_free.free = P.now("t")
            cs_free[bi].free = P.now("v")
            self.emit_ffn_prep(9 if i >= 4 else 4)
        self.emit_ffn_prep(len(self.ffn_mats), flush=True)
        self.x_free_1a = x_free

    def phase_15(self):
        P = self.P
        nw1, nw2, nw3 = self.nw1, self.nw2, self.nw3
        psNorm = self.psNorm
        P.emit("vector", lambda E: E.reduce_max(nw1[0:16, 0:1],
                                                self.runmax[:, 0:8], axis=AX),
               waits=(P.now("v"), P.now("t"), P.now("s")), inc="v", tag="ch_mq")
        P.emit("vector", lambda E: E.reduce_max(nw1[0:16, 1:2],
                                                self.runmax[:, 8:16], axis=AX),
               waits=(P.now("v"),), inc="v", tag="ch_mk")
        P.emit("vector", lambda E: E.tensor_mul(nw1[0:16, 2:3], nw1[0:16, 0:1],
                                                nw1[0:16, 1:2]),
               waits=(P.now("v"),), inc="v", tag="ch_p")
        P.emit("vector", lambda E: E.tensor_scalar_max(nw1[0:16, 2:3],
                                                       nw1[0:16, 2:3], 1e-12),
               waits=(P.now("v"),), inc="v", tag="ch_cl")
        self.newton_rsqrt(self.stats[0:16, 28:29], nw1[0:16, 2:3],
                          pre=(P.now("v"), self.stats_free.free), tag="ch_nw")
        P.emit("vector", lambda E: E.tensor_mul(nw1[0:16, 3:4], nw1[0:16, 2:3],
                                                self.stats[0:16, 28:29]),
               waits=(P.now("v"),), inc="v", tag="ch_s")
        P.emit("vector", lambda E: E.tensor_scalar_mul(nw1[0:16, 3:4],
                                                       nw1[0:16, 3:4], -0.125),
               waits=(P.now("v"),), inc="v", tag="ch_n")
        CH = P.now("v")
        P.emit("tensor", lambda E: E.transpose(psNorm[0:1, 0:16],
                                               nw1[0:16, 3:4],
                                               self.identf[0:16, 0:16]),
               waits=(CH, self.psnorm_free.free), inc="t", tag="ch_t")
        P.emit("vector", lambda E: E.tensor_copy(self.stats[0:1, 0:16],
                                                 psNorm[0:1, 0:16]),
               waits=(P.now("t"), self.stats_free.free), inc="v", tag="ch_e")
        psQ = self.psQ_ref
        pqf = self.psq_free_ref[0]
        P.emit("tensor", lambda E: E.matmul(psQ[:, 0:16], self.ones1[:, :],
                                            self.stats[0:1, 0:16],
                                            start=True, stop=True),
               waits=(P.now("v"), pqf.free), inc="t", tag="ch_b")
        P.emit("vector", lambda E: E.tensor_copy(self.bias_ch[:, :],
                                                 psQ[:, 0:16]),
               waits=(P.now("t"),), inc="v", tag="ch_be")
        for p_ in range(8):
            P.emit("vector",
                   (lambda E, o=self.bias_pr[:, p_:p_ + 1],
                    a=self.bias_ch[:, 2 * p_:2 * p_ + 1],
                    b2=self.bias_ch[:, 2 * p_ + 1:2 * p_ + 2]:
                    E.tensor_tensor(o, a, b2, op=ALU.min)),
                   waits=(P.now("v"),), inc="v", tag=f"ch_pr{p_}")
        self.psnorm_free.free = P.now("v")
        self.stats_free.free = P.now("v")
        if "dbg_bias" in self.taps:
            dbg_b = self.scratch("dbg_bias", [128, 16], F32)
            dbg_r = self.scratch("dbg_rmax", [16, 16], F32)
            P.dma("sync", dbg_b[:, :], self.bias_ch[:, :],
                  waits=(P.now("v"),), inc="d_dbg", tag="dbg_bias")
            P.dma("sync", dbg_r[:, :], self.runmax[:, :],
                  waits=(P.now("v"),), inc="d_dbg", tag="dbg_rmax")

    def phase_1b(self, sc):
        nc, P = self.nc, self.P
        wot = self.sb("wot_sb", [128, 8192], BF16, sc)
        q_sb = [self.sb(f"q_sb{j}", [128, 4096], BF16, sc) for j in range(2)]
        k_sb = [self.sb(f"k_sb{j}", [128, 4096], BF16, sc) for j in range(2)]
        v_sb = [self.sb(f"v_sb{j}", [128, 4096], BF16, sc) for j in range(2)]
        probs = self.sb("probs", [128, 4096], BF16, sc)
        attn = self.sb("attn", [128, 4096], BF16, sc)
        av32 = self.sb("av32", [128, 512], F32, sc)
        xr_sb = [self.sb(f"xr_sb{j}", [128, 4096], F32, sc) for j in range(2)]
        out1 = self.sb("out1", [128, 4096], F32, sc)
        x1t = self.sb("x1t", [128, 4096], F32, sc)
        selp_sb = self.sb("selp_sb", [128, 1024], F32, sc)
        mask_sb = self.sb("mask_sb", [128, 2048], BF16, sc)
        e4_sb = self.sb("e4_sb", [128, 64], BF16, sc)
        Ssb = self.sb("Ssb", [128, 512], F32, sc)
        recS = self.sb("recS", [128, 512], F32, sc)
        psA = self.ps("psA", [128, 2048], F32, sc)
        psAV = [self.ps(f"psAV{j}", [128, 512], F32, sc) for j in range(2)]
        psWo = self.ps("psWo", [128, 512], F32, sc)
        psWo2 = self.ps("psWo2", [128, 512], F32, sc)
        psSums = psWo
        zero128 = self.sb("zero128", [128, 128], BF16, sc)
        P.emit("vector", lambda E: E.memset(zero128[:, :], 0.0), inc="v",
               tag="zero_init")
        ZERO = P.now("v")

        if "dbg_attn" in self.taps:
            self.dbg_attn_s = self.scratch("dbg_attn", [NT, 128, 4096], BF16)
            self.dbg_s_s = self.scratch("dbg_s", [NT, 128, 512], F32)
        P.dma("sync", wot[:, :], self.wot_s[:, :],
              waits=(P.now("d_woS"),), inc="d_wo2", tag="wot_ld")
        WOT = P.now("d_wo2")
        P.dma("sync", selp_sb[:, :], self.cd["selp"][:, :], inc="d_c", tag="c_selp")
        P.dma("sync", mask_sb[:, :], self.cd["mask4"][:, :], inc="d_c", tag="c_mask")
        P.dma("sync", e4_sb[:, :], self.cd["e4"][:, :], inc="d_c", tag="c_e4")
        C1B = P.now("d_c")

        for hf in range(2):
            P.emit("tensor", (lambda E, r=self.bo_sb[:, hf * 512:(hf + 1) * 512]:
                              E.matmul(psWo[:, :], self.ones1[:, :], r,
                                       start=True, stop=True)),
                   waits=(self.CONSTS,) if hf == 0 else (P.now("v"),),
                   inc="t", tag=f"bo_b{hf}")
            P.emit("vector", (lambda E, d=self.bo_rep[:, hf * 512:(hf + 1) * 512]:
                              E.tensor_copy(d, psWo[:, :])),
                   waits=(P.now("t"),), inc="v", tag=f"bo_e{hf}")
        BOREP = P.now("v")
        pswo_free_init = BOREP

        SPILLS = [("d_spq", 16 * 8), ("d_spk", 16 * 8), ("d_spv", 16 * 8)]
        q_free = [Buf(), Buf()]
        k_free = [Buf(), Buf()]
        v_free = [Buf(), Buf()]
        xr_free = [Buf(), Buf()]
        probs_free = {g: Buf() for g in range(2)}
        ps_free = Buf()
        psav_free = [Buf(), Buf()]
        pswo_free = Buf()
        pswo_free.free = pswo_free_init
        pswo2_free = Buf()
        attn_free = Buf()
        out1_free = Buf()
        x1t_free = Buf()

        attn32 = self.sb("attn32", [128, 4096], F32, sc)
        attn32_free = Buf()
        pending_spill = []

        def flush_spill():
            while pending_spill:
                si, sX1 = pending_spill.pop(0)
                P.dma("sync", self.x1_s[si][:, :], x1t[:, :], waits=(sX1,),
                      inc="d_x1w", tag=f"x1sp{si}")
                x1t_free.free = P.now("d_x1w")
        ql_ms = [None, None]

        def emit_q_load(ii):
            qb2 = ii % 2
            P.dma("sync", q_sb[qb2][:, :], self.qf_s[ii][:, :],
                  waits=tuple((SPILLS if ii == 0 else [])
                              + ([q_free[qb2].free] if q_free[qb2].free else [])),
                  inc=f"d_q{qb2}", tag=f"q_ld{ii}")
            ql_ms[qb2] = P.now(f"d_q{qb2}")

        emit_q_load(0)
        for i in range(NT):
            qb = i % 2
            QL = ql_ms[qb]
            XR = None

            pend = []

            def emit_avsums(d):
                g, j, p = d["g"], d["j"], d["p"]
                for bb in range(2):
                    b = 2 * g + bb
                    first = (b == 0)
                    last = (b == 3)
                    for hh in range(2):
                        h = 2 * p + hh
                        w = [d["PRD"], d["VL"]] if (bb == 0 and hh == 0) else []
                        if first and bb == 0 and hh == 0 and d["avf"].free:
                            w.append(d["avf"].free)
                        P.emit("tensor",
                               (lambda E,
                                o=d["av"][64 * hh:64 * hh + 64, :],
                                l=v_sb[d["sl"]][:, b * 1024 + h * 64:
                                                b * 1024 + h * 64 + 64],
                                r=probs[:, d["pcol"] + hh * 1024 + bb * 512:
                                        d["pcol"] + hh * 1024 + bb * 512 + 512],
                                s0=first, s1=last, tp=(0, 64 * hh):
                                E.matmul(o, l, r, start=s0, stop=s1,
                                         tile_position=tp,
                                         skip_group_check=True)),
                               waits=tuple(w), inc="t",
                               tag=f"av{i}_{j}_{p}{g}{hh}{bb}")
                    for hh in range(2):
                        h = 2 * p + hh
                        P.emit("tensor",
                               (lambda E,
                                o=psSums[32 * (h % 4):32 * (h % 4) + 16, :],
                                l=e4_sb[:, (h // 4) * 16:(h // 4) * 16 + 16],
                                r=probs[:, d["pcol"] + hh * 1024 + bb * 512:
                                        d["pcol"] + hh * 1024 + bb * 512 + 512],
                                s1=d["slast"] and bb == 1 and g == 1,
                                tp=(0, 32 * (h % 4)):
                                E.matmul(o, l, r, start=False, stop=s1,
                                         tile_position=tp,
                                         skip_group_check=True)),
                               waits=(), inc="t",
                               tag=f"su{i}_{j}_{p}{g}{hh}{bb}")
                probs_free[d["pslot"]].free = P.now("t")
                if g == 1:
                    AVP = P.now("t")
                    aw = [AVP]
                    if d["j"] == 0 and attn32_free.free and p == 0:
                        aw.append(attn32_free.free)
                    if d["j"] == 0:
                        P.emit("vector",
                               (lambda E, dst=attn32[:, p * 512:(p + 1) * 512],
                                sp=d["av"][:, :]: E.tensor_copy(dst, sp)),
                               waits=tuple(aw), inc="v", tag=f"ac{i}_{j}_{p}")
                    else:
                        P.emit("vector",
                               (lambda E, dst=attn32[:, p * 512:(p + 1) * 512],
                                sp=d["av"][:, :]:
                                E.tensor_tensor(dst, sp, dst, op=ALU.add)),
                               waits=tuple(aw), inc="v", tag=f"ac{i}_{j}_{p}")
                    d["avf"].free = P.now("v")
                    if d["vlast"]:
                        v_free[d["sl"]].free = P.now("t")

            zw = [ZERO, C1B]
            if pswo_free.free:
                zw.append(pswo_free.free)
            P.emit("tensor",
                   (lambda E: E.matmul(psSums[:, :], zero128[:, :],
                                       mask_sb[:, 0:512], start=True,
                                       stop=False, skip_group_check=True)),
                   waits=tuple(zw), inc="t", tag=f"zs{i}")

            for j in range(i + 1):
                sl = j % 2
                P.dma("sync", k_sb[sl][:, :], self.kf_s[j][:, :],
                      waits=tuple((SPILLS if i == 0 else [])
                                  + ([k_free[sl].free] if k_free[sl].free else [])),
                      inc=f"d_k{sl}", tag=f"k_ld{i}_{j}")
                KL = P.now(f"d_k{sl}")
                P.dma("gpsimd", v_sb[sl][:, :], self.vt_s[j][:, :],
                      waits=(v_free[sl].free,) if v_free[sl].free else (),
                      inc=f"d_v{sl}", tag=f"v_ld{i}_{j}")
                VL = P.now(f"d_v{sl}")
                if j == min(1, i):
                    flush_spill()
                    P.dma("sync", xr_sb[qb][:, :].rearrange("p (u d) -> p u d", u=4),
                          self.xin[i * TT:(i + 1) * TT, :].rearrange(
                              "(u p) d -> p u d", p=128),
                          waits=(xr_free[qb].free,) if xr_free[qb].free else (),
                          inc=f"d_xr{qb}", tag=f"xr_ld{i}")
                    XR = P.now(f"d_xr{qb}")
                    if i + 1 < NT:
                        emit_q_load(i + 1)
                diag = (j == i)
                first_of_j = True
                for p in range(8):
                    av = psAV[p % 2]
                    avf = psav_free[p % 2]
                    for g in range(2):
                        for bb in range(2):
                            b = 2 * g + bb
                            for hh in range(2):
                                w = []
                                if first_of_j:
                                    w = [KL, QL]
                                    first_of_j = False
                                if bb == 0 and hh == 0 and ps_free.free:
                                    w.append(ps_free.free)
                                P.emit("tensor",
                                       (lambda E,
                                        o=psA[:, hh * 1024 + bb * 512:
                                              hh * 1024 + bb * 512 + 512],
                                        l=k_sb[sl][64 * hh:64 * hh + 64,
                                                   p * 512 + b * 128:
                                                   p * 512 + b * 128 + 128],
                                        r=q_sb[qb][64 * hh:64 * hh + 64,
                                                   p * 512:(p + 1) * 512]:
                                        E.matmul(o, l, r, start=True, stop=True)),
                                       waits=tuple(w), inc="t",
                                       tag=f"sc{i}_{j}_{p}{g}{hh}{bb}")
                        SCM = P.now("t")
                        pcol = g * 2048
                        ew = [SCM]
                        if probs_free[g].free:
                            ew.append(probs_free[g].free)
                        if i == 0 and j == 0 and p == 0:
                            ew.append(P.now("v"))
                        P.emit("scalar",
                               (lambda E, o=probs[:, pcol:pcol + 2048],
                                bv=self.bias_pr[:, p:p + 1]:
                                E.activation(o, psA[:, :], AF.Exp, bias=bv,
                                             scale=0.125)),
                               waits=tuple(ew), inc="s",
                               tag=f"ex{i}_{j}_{p}{g}")
                        EX = P.now("s")
                        ps_free.free = EX
                        if diag:
                            for hh in range(2):
                                for bb in range(2):
                                    b = 2 * g + bb
                                    P.emit("gpsimd",
                                           (lambda E,
                                            o=probs[:, pcol + hh * 1024 + bb * 512:
                                                    pcol + hh * 1024 + bb * 512 + 512],
                                            mk=mask_sb[:, b * 512:(b + 1) * 512]:
                                            E.tensor_mul(o, o, mk)),
                                           waits=(EX, C1B) if (hh == 0 and bb == 0) else (),
                                           inc="g", tag=f"mk{i}_{p}{g}{hh}{bb}")
                            PRD = P.now("g")
                        else:
                            PRD = EX
                        pend.append(dict(p=p, g=g, pcol=pcol, PRD=PRD, VL=VL,
                                         sl=sl, j=j, av=av, avf=avf, pslot=g,
                                         slast=(j == i), vlast=(p == 7)))
                        if len(pend) > 1:
                            emit_avsums(pend.pop(0))
                k_free[sl].free = P.now("t")
            while pend:
                emit_avsums(pend.pop(0))
            AVD = P.now("t")
            AVV = P.now("v")
            P.emit("vector", lambda E: E.tensor_scalar_max(
                Ssb[:, :], psSums[:, :], 1e-30),
                waits=(AVD,), inc="v", tag=f"S{i}")
            P.emit("vector", lambda E: E.reciprocal(recS[:, :], Ssb[:, :]),
                   waits=(P.now("v"),), inc="v", tag=f"Sr{i}")
            SREC = P.now("v")
            for p in range(8):
                w = [SREC]
                if pswo_free.free:
                    w.append(pswo_free.free)
                P.emit("tensor",
                       (lambda E, l=selp_sb[:, p * 128:(p + 1) * 128]:
                        E.matmul(psWo[:, :], l, recS[:, :],
                                 start=True, stop=True)),
                       waits=tuple(w), inc="t", tag=f"bc{i}_{p}")
                BC = P.now("t")
                aw = [BC, AVV]
                if attn_free.free and p == 0:
                    aw.append(attn_free.free)
                P.emit("vector",
                       (lambda E, o=attn[:, p * 512:(p + 1) * 512],
                        a32=attn32[:, p * 512:(p + 1) * 512]:
                        E.tensor_mul(o, a32, psWo[:, :])),
                       waits=tuple(aw), inc="v", tag=f"an{i}_{p}")
                pswo_free.free = P.now("v")
            attn32_free.free = P.now("v")
            ATTN = P.now("v")
            q_free[qb].free = P.now("t")

            if "dbg_attn" in self.taps:
                P.dma("sync", self.dbg_attn_s[i][:, :], attn[:, :],
                      waits=(ATTN,), inc="d_dbg", tag=f"dbga{i}")
                P.dma("sync", self.dbg_s_s[i][:, :], Ssb[:, :],
                      waits=(ATTN,), inc="d_dbg", tag=f"dbgs{i}")

            ww = [ATTN, WOT]
            if out1_free.free:
                ww.append(out1_free.free)
            for u in range(NSUB):
                for oh in range(2):
                    k2 = (u * 2 + oh) % 2
                    pwo = (psWo, psWo2)[k2]
                    pwf = (pswo_free, pswo2_free)[k2]
                    for dc in range(8):
                        w = list(ww) if (u == 0 and oh == 0 and dc == 0) else []
                        if dc == 0 and pwf.free:
                            w.append(pwf.free)
                        P.emit("tensor",
                               (lambda E, o=pwo[:, :],
                                l=attn[:, dc * 512 + u * 128:
                                        dc * 512 + u * 128 + 128],
                                r=wot[:, dc * 1024 + oh * 512:
                                      dc * 1024 + oh * 512 + 512],
                                s0=(dc == 0), s1=(dc == 7):
                                E.matmul(o, l, r, start=s0, stop=s1)),
                               waits=tuple(w), inc="t" if dc == 7 else None,
                               tag=f"wo{i}_{u}{oh}{dc}")
                    ww = []
                    P.emit("vector",
                           (lambda E, o=out1[:, u * 1024 + oh * 512:
                                             u * 1024 + oh * 512 + 512],
                            sp=pwo[:, :],
                            bo=self.bo_rep[:, oh * 512:(oh + 1) * 512]:
                            E.tensor_add(o, sp, bo)),
                           waits=(P.now("t"), BOREP), inc="v",
                           tag=f"woe{i}_{u}{oh}")
                    pwf.free = P.now("v")
            O1 = P.now("v")
            attn_free.free = P.now("t")

            LN2 = self.ln_stats(lambda u: out1[:, u * 1024:(u + 1) * 1024],
                                O1, f"ln2_{i}")
            xw = [LN2, XR]
            if x1t_free.free:
                xw.append(x1t_free.free)
            for u in range(NSUB):
                P.emit("vector",
                       (lambda E, o=x1t[:, u * 1024:(u + 1) * 1024],
                        i_=out1[:, u * 1024:(u + 1) * 1024],
                        m=self.stats[:, 8 + u:9 + u],
                        r=self.stats[:, 20 + u:21 + u]:
                        E.tensor_scalar(o, i_, m, r, op0=ALU.subtract,
                                        op1=ALU.mult)),
                       waits=tuple(xw) if u == 0 else (), inc="v",
                       tag=f"ln2n_{i}_{u}")
                P.emit("vector",
                       (lambda E, o=x1t[:, u * 1024:(u + 1) * 1024],
                        xs=xr_sb[qb][:, u * 1024:(u + 1) * 1024]:
                        E.tensor_add(o, o, xs)),
                       waits=(P.now("v"),), inc="v", tag=f"ln2a_{i}_{u}")
            X1 = P.now("v")
            self.stats_free.free = X1
            out1_free.free = X1
            xr_free[qb].free = X1
            pending_spill.append((i, X1))
        flush_spill()

    def phase_2(self, sc):
        nc, P = self.nc, self.P
        x1_sb = [self.sb(f"x1_sb{j}", [128, 4096], F32, sc) for j in range(2)]
        xn2_tok = self.sb("xn2_tok", [128, 4096], BF16, sc)
        xn2_f = [self.sb(f"xn2_f{j}", [128, 4096], BF16, sc) for j in range(2)]
        rwt = self.sb("rwt_sb", [128, 128], BF16, sc)
        selr_sb = self.sb("selr_sb", [16, 2048], BF16, sc)
        wu = [self.sb(f"wu{j}", [128, 4096], BF16, sc) for j in range(4)]
        wd = [self.sb(f"wd{j}", [128, 4096], BF16, sc) for j in range(4)]
        actg = [self.sb(f"actg{j}", [128, 2048], BF16, sc) for j in range(4)]
        acc = self.sb("acc", [128, 4096], F32, sc)
        outt = self.sb("outt", [128, 4096], F32, sc)
        G_sb = [self.sb(f"G_sb{j}", [128, 512], BF16, sc) for j in range(2)]
        pr_sb = self.sb("pr_sb", [128, 16], F32, sc)
        wk_sb = self.sb("wk_sb", [128, 16], F32, sc)
        mk_sb = self.sb("mk_sb", [128, 16], F32, sc)
        g16 = self.sb("g16", [128, 16], BF16, sc)
        gateT = [self.sb(f"gateT{j}", [16, 512], BF16, sc) for j in range(2)]
        rsum = self.sb("rsum", [128, 4], F32, sc)
        psUp = [self.ps(f"psUp{j}", [128, 512], F32, sc) for j in range(2)]
        psDn = [self.ps(f"psDn{j}", [128, 512], F32, sc) for j in range(2)]
        psR = self.ps("psR", [128, 16], F32, sc)
        psM = self.ps("psM", [128, 512], F32, sc)
        psL4 = self.ps("psL4", [128, 1024], F32, sc)

        P.dma("sync", rwt[:, :], self.rwt_s[:, :], waits=(P.now("d_rwS"),),
              inc="d_c", tag="rwt_ld")
        P.dma("sync", selr_sb[:, :], self.cd["selr"][:, :], inc="d_c",
              tag="c_selr")
        C2 = P.now("d_c")
        FW = [P.now("d_fwst0"), P.now("d_fwst1")]

        x1_free = [Buf(), Buf()]
        xn2t_free = Buf()
        xn2f_free = [Buf(), Buf()]
        wslot_free = [Buf() for _ in range(4)]
        actg_free = [Buf() for _ in range(4)]
        psup_free = [Buf(), Buf()]
        psdn_free = [Buf(), Buf()]
        psr_free = Buf()
        psm_free = Buf()
        psl4_free = Buf()
        g_free = [Buf(), Buf()]
        gateT_free = [Buf(), Buf()]
        acc_free = Buf()
        outt_free = Buf()
        xf_ms = [None, None]
        gatet_ms = [None, None]
        xn2_ms = [None, None]

        passes = []
        k0 = 0
        for n in PASSES:
            passes.append(list(range(NEXP))[k0:k0 + n])
            k0 += n

        def emit_prep_stats(ip):
            bi = ip % 2
            P.dma("sync", x1_sb[bi][:, :], self.x1_s[ip][:, :],
                  waits=tuple(x for x in (("d_x1w", 16 * (ip + 1)),
                                          x1_free[bi].free) if x),
                  inc=f"d_x1r{bi}", tag=f"x1_ld{ip}")
            XL = P.now(f"d_x1r{bi}")
            LN3 = self.ln_stats(lambda u: x1_sb[bi][:, u * 1024:(u + 1) * 1024],
                                XL, f"ln3_{ip}")
            xw = [LN3]
            if xn2t_free.free:
                xw.append(xn2t_free.free)
            for u in range(NSUB):
                P.emit("scalar",
                       (lambda E, o=xn2_tok[:, u * 1024:(u + 1) * 1024],
                        i_=x1_sb[bi][:, u * 1024:(u + 1) * 1024],
                        scv=self.stats[:, 20 + u:21 + u],
                        bv=self.stats[:, 24 + u:25 + u]:
                        E.activation(o, i_, AF.Identity, bias=bv, scale=scv)),
                       waits=tuple(xw) if u == 0 else (), inc="s",
                       tag=f"ln3n_{ip}_{u}")
            XN2 = P.now("s")
            self.stats_free.free = XN2
            xn2_ms[bi] = XN2

        def emit_prep_rest(ip):
            bi = ip % 2
            XN2 = xn2_ms[bi]
            tw = [XN2]
            if xn2f_free[bi].free:
                tw.append(xn2f_free[bi].free)
            for c in range(8):
                k = c % 2
                pw = psUp[k]
                for u in range(NSUB):
                    w = list(tw) if u == 0 else []
                    if u == 0 and psup_free[k].free:
                        w.append(psup_free[k].free)
                    P.emit("tensor",
                           (lambda E, o=pw.bitcast(BF16)[:, u * 128:u * 128 + 128],
                            i_=xn2_tok[:, u * 1024 + c * 128:u * 1024 + c * 128 + 128]:
                            E.transpose(o, i_, self.identb[:, :])),
                           waits=tuple(w), inc="t" if u == 3 else None,
                           tag=f"x2t{ip}_{c}{u}")
                tw = []
                P.emit("vector",
                       (lambda E, d=xn2_f[bi][:, c * 512:(c + 1) * 512],
                        sp=pw.bitcast(BF16)[:, 0:512]: E.tensor_copy(d, sp)),
                       waits=(P.now("t"),), inc="v", tag=f"x2e{ip}_{c}")
                psup_free[k].free = P.now("v")
            XF = P.now("v")
            xn2t_free.free = P.now("t")
            xf_ms[bi] = XF

            for u in range(NSUB):
                for dc in range(8):
                    w = [XF, C2] if dc == 0 else []
                    if dc == 0 and psr_free.free:
                        w.append(psr_free.free)
                    P.emit("tensor",
                           (lambda E, l=xn2_f[bi][:, dc * 512 + u * 128:
                                                  dc * 512 + u * 128 + 128],
                            r=rwt[:, dc * 16:(dc + 1) * 16],
                            s0=(dc == 0), s1=(dc == 7):
                            E.matmul(psR[:, :], l, r, start=s0, stop=s1)),
                           waits=tuple(w), inc="t" if dc == 7 else None,
                           tag=f"rt{ip}_{u}{dc}")
                RT = P.now("t")
                P.emit("vector", lambda E: E.reduce_max(self.nw1[:, 0:1],
                                                        psR[:, :], axis=AX),
                       waits=(RT,), inc="v", tag=f"rm{ip}_{u}")
                P.emit("vector", lambda E: E.tensor_scalar_mul(
                    self.nw1[:, 1:2], self.nw1[:, 0:1], -1.0),
                    waits=(P.now("v"),), inc="v", tag=f"rmn{ip}_{u}")
                NM = P.now("v")
                P.emit("scalar",
                       (lambda E, a=rsum[:, u:u + 1]:
                        E.activation(pr_sb[:, :], psR[:, :], AF.Exp,
                                     bias=self.nw1[:, 1:2], accum_out=a)),
                       waits=(NM,), inc="s", tag=f"re{ip}_{u}")
                psr_free.free = P.now("s")
                P.emit("vector", (lambda E, a=rsum[:, u:u + 1]:
                                  E.reciprocal(self.nw1[:, 2:3], a)),
                       waits=(P.now("s"),), inc="v", tag=f"rrc{ip}_{u}")
                P.emit("vector", lambda E: E.tensor_scalar_mul(
                    pr_sb[:, :], pr_sb[:, :], self.nw1[:, 2:3]),
                    waits=(P.now("v"),), inc="v", tag=f"rn{ip}_{u}")
                P.emit("vector", lambda E: E.tensor_copy(wk_sb[:, :], pr_sb[:, :]),
                       waits=(P.now("v"),), inc="v", tag=f"tk0{ip}_{u}")
                for it in range(3):
                    P.emit("vector", lambda E: E.reduce_max(self.nw1[:, 3:4],
                                                            wk_sb[:, :], axis=AX),
                           waits=(P.now("v"),), inc="v", tag=f"tkm{ip}_{u}{it}")
                    P.emit("vector", lambda E: E.tensor_scalar(
                        mk_sb[:, :], wk_sb[:, :], self.nw1[:, 3:4], -1e9,
                        op0=ALU.is_ge, op1=ALU.mult),
                        waits=(P.now("v"),), inc="v", tag=f"tka{ip}_{u}{it}")
                    P.emit("vector", lambda E: E.tensor_add(
                        wk_sb[:, :], wk_sb[:, :], mk_sb[:, :]),
                        waits=(P.now("v"),), inc="v", tag=f"tkb{ip}_{u}{it}")
                P.emit("vector", lambda E: E.reduce_max(self.nw1[:, 3:4],
                                                        wk_sb[:, :], axis=AX),
                       waits=(P.now("v"),), inc="v", tag=f"tkt{ip}_{u}")
                P.emit("vector", lambda E: E.tensor_scalar(
                    mk_sb[:, :], pr_sb[:, :], self.nw1[:, 3:4], None,
                    op0=ALU.is_ge), waits=(P.now("v"),), inc="v",
                    tag=f"tkc{ip}_{u}")
                P.emit("vector", lambda E: E.tensor_mul(
                    g16[:, :], pr_sb[:, :], mk_sb[:, :]),
                    waits=(P.now("v"),), inc="v", tag=f"tkg{ip}_{u}")
                GT = P.now("v")
                w = [GT]
                if u == 0 and psm_free.free:
                    w.append(psm_free.free)
                if u == 0 and gateT_free[bi].free:
                    w.append(gateT_free[bi].free)
                P.emit("tensor",
                       (lambda E, o=psM.bitcast(BF16)[0:16, u * 256:u * 256 + 128]:
                        E.transpose(o, g16[:, :], self.identb[:, :])),
                       waits=tuple(w), inc="t", tag=f"gt{ip}_{u}")
            for u in range(NSUB):
                P.emit("vector",
                       (lambda E, d=gateT[bi][:, u * 128:(u + 1) * 128],
                        sp=psM.bitcast(BF16)[0:16, u * 256:u * 256 + 128]:
                        E.tensor_copy(d, sp)),
                       waits=(P.now("t"),) if u == 0 else (), inc="v",
                       tag=f"gte{ip}_{u}")
            gatet_ms[bi] = P.now("v")
            psm_free.free = P.now("v")

        emit_prep_stats(0)
        emit_prep_rest(0)
        for i in range(self.nt2):
            bi = i % 2
            XF = xf_ms[bi]
            GATET = gatet_ms[bi]
            xnf = xn2_f[bi]
            gT = gateT[bi]

            pass_first = True
            for pi_, pl in enumerate(passes):
                for ei_, e in enumerate(pl):
                    slot = (i * NEXP + sum(PASSES[:pi_]) + ei_) % 4
                    P.dma("sync", wu[slot][:, :], self.utp_s[e][:, :],
                          waits=tuple((FW if (i == 0 and pi_ == 0 and ei_ == 0) else [])
                                      + ([wslot_free[slot].free] if wslot_free[slot].free else [])),
                          inc=f"d_fw{slot}", tag=f"fu_ld{i}_{e}")
                    P.dma("sync", wd[slot][:, :], self.dtp_s[e][:, :],
                          inc=f"d_fw{slot}", tag=f"fd_ld{i}_{e}")
                    WL = P.now(f"d_fw{slot}")
                    aslot = slot
                    if e < 16:
                        w = [GATET, C2]
                        if psm_free.free:
                            w.append(psm_free.free)
                        P.emit("tensor",
                               (lambda E, l=selr_sb[:, e * 128:(e + 1) * 128],
                                r=gT[:, :]:
                                E.matmul(psM[:, :], l, r, start=True, stop=True)),
                               waits=tuple(w), inc="t", tag=f"gb{i}_{e}")
                        gslot = e % 2
                        gw = [P.now("t")]
                        if g_free[gslot].free:
                            gw.append(g_free[gslot].free)
                        P.emit("vector", (lambda E, d=G_sb[e % 2][:, :]:
                                          E.tensor_copy(d, psM[:, :])),
                               waits=tuple(gw), inc="v", tag=f"gbe{i}_{e}")
                        psm_free.free = P.now("v")
                        GB = P.now("v")
                    else:
                        GB = None
                    for ec in range(4):
                        k = ec % 2
                        pw = psUp[k]
                        for dc in range(8):
                            w = [WL, XF] if (ec == 0 and dc == 0) else []
                            if dc == 0 and psup_free[k].free:
                                w.append(psup_free[k].free)
                            P.emit("tensor",
                                   (lambda E, o=pw[:, :],
                                    l=wu[slot][:, dc * 512 + ec * 128:
                                               dc * 512 + ec * 128 + 128],
                                    r=xnf[:, dc * 512:(dc + 1) * 512],
                                    s0=(dc == 0), s1=(dc == 7):
                                    E.matmul(o, l, r, start=s0, stop=s1)),
                                   waits=tuple(w), inc="t" if dc == 7 else None,
                                   tag=f"up{i}_{e}_{ec}{dc}")
                        UPM = P.now("t")
                        gw = [UPM]
                        if ec == 0 and actg_free[aslot].free:
                            gw.append(actg_free[aslot].free)
                        P.emit("scalar",
                               (lambda E, o=actg[aslot][:, ec * 512:(ec + 1) * 512],
                                sp=pw[:, :]: E.activation(o, sp, AF.Gelu)),
                               waits=tuple(gw), inc="s", tag=f"gl{i}_{e}_{ec}")
                        psup_free[k].free = P.now("s")
                        if e < 16:
                            P.emit("gpsimd",
                                   (lambda E, o=actg[aslot][:, ec * 512:(ec + 1) * 512],
                                    gs=G_sb[e % 2][:, :]:
                                    E.tensor_mul(o, o, gs)),
                                   waits=(P.now("s"), GB), inc="g",
                                   tag=f"gm{i}_{e}_{ec}")
                    ACT = P.now("g") if e < 16 else P.now("s")
                    if e < 16:
                        g_free[e % 2].free = ACT
                    actg_free[aslot].ready = ACT
                for c in range(8):
                    k = c % 2
                    pd = psDn[k]
                    nel = len(pl)
                    for ei_, e in enumerate(pl):
                        slot = (i * NEXP + sum(PASSES[:pi_]) + ei_) % 4
                        for ec in range(4):
                            first = (ei_ == 0 and ec == 0)
                            last = (ei_ == nel - 1 and ec == 3)
                            w = []
                            if ec == 0 and c == 0:
                                w.append(actg_free[slot].ready)
                            if first and psdn_free[k].free:
                                w.append(psdn_free[k].free)
                            P.emit("tensor",
                                   (lambda E, o=pd[:, :],
                                    l=wd[slot][:, ec * 1024 + c * 128:
                                               ec * 1024 + c * 128 + 128],
                                    r=actg[slot][:, ec * 512:(ec + 1) * 512],
                                    s0=first, s1=last:
                                    E.matmul(o, l, r, start=s0, stop=s1)),
                                   waits=tuple(w), inc="t" if last else None,
                                   tag=f"dn{i}_{pi_}_{c}{e}{ec}")
                    DN = P.now("t")
                    aw = [DN]
                    if pass_first and acc_free.free and c == 0:
                        aw.append(acc_free.free)
                    if pass_first:
                        P.emit("vector",
                               (lambda E, d=acc[:, c * 512:(c + 1) * 512],
                                sp=pd[:, :]: E.tensor_copy(d, sp)),
                               waits=tuple(aw), inc="v", tag=f"dne{i}_{pi_}_{c}")
                    else:
                        P.emit("vector",
                               (lambda E, d=acc[:, c * 512:(c + 1) * 512],
                                sp=pd[:, :]: E.tensor_tensor(d, sp, d, op=ALU.add)),
                               waits=tuple(aw), inc="v", tag=f"dna{i}_{pi_}_{c}")
                    psdn_free[k].free = P.now("v")
                if pass_first and i + 1 < self.nt2:
                    emit_prep_stats(i + 1)
                pass_first = False
                for ei_, e in enumerate(pl):
                    slot = (i * NEXP + sum(PASSES[:pi_]) + ei_) % 4
                    wslot_free[slot].free = P.now("t")
                    actg_free[slot].free = P.now("t")
            ACC = P.now("v")
            xn2f_free[bi].free = P.now("t")
            gateT_free[bi].free = P.now("t")

            if i + 1 < self.nt2:
                emit_prep_rest(i + 1)

            for u in range(NSUB):
                tw = [ACC] if u == 0 else []
                if psl4_free.free:
                    tw.append(psl4_free.free)
                for c in range(8):
                    w = list(tw) if c == 0 else []
                    P.emit("tensor",
                           (lambda E, o=psL4[:, c * 128:(c + 1) * 128],
                            i_=acc[:, c * 512 + u * 128:c * 512 + u * 128 + 128]:
                            E.transpose(o, i_, self.identf[:, :])),
                           waits=tuple(w), inc="t" if c == 7 else None,
                           tag=f"l4t{i}_{u}{c}")
                    tw = []
                L4T = P.now("t")
                ew = [L4T]
                if u == 0 and outt_free.free:
                    ew.append(outt_free.free)
                P.emit("vector",
                       (lambda E, o=outt[:, u * 1024:(u + 1) * 1024]:
                        E.tensor_copy(o, psL4[:, :])),
                       waits=tuple(ew), inc="v", tag=f"l4e{i}_{u}")
                psl4_free.free = P.now("v")
            LN4 = self.ln_stats(lambda u: outt[:, u * 1024:(u + 1) * 1024],
                                P.now("v"), f"ln4_{i}")
            for u in range(NSUB):
                P.emit("vector",
                       (lambda E, o=outt[:, u * 1024:(u + 1) * 1024],
                        m=self.stats[:, 8 + u:9 + u],
                        r=self.stats[:, 20 + u:21 + u]:
                        E.tensor_scalar(o, o, m, r,
                                        op0=ALU.subtract, op1=ALU.mult)),
                       waits=(LN4,) if u == 0 else (), inc="v",
                       tag=f"l4n{i}_{u}")
                P.emit("vector",
                       (lambda E, o=outt[:, u * 1024:(u + 1) * 1024],
                        xs=x1_sb[bi][:, u * 1024:(u + 1) * 1024]:
                        E.tensor_add(o, o, xs)),
                       waits=(P.now("v"),), inc="v", tag=f"l4a{i}_{u}")
            self.stats_free.free = P.now("v")
            OUT = P.now("v")
            x1_free[bi].free = OUT
            acc_free.free = P.now("t")
            P.dma("gpsimd", self.out_d[i * TT:(i + 1) * TT, :].rearrange(
                "(u p) d -> p u d", p=128),
                outt[:, :].rearrange("p (u d) -> p u d", u=4),
                waits=(OUT,), inc="d_o", tag=f"out{i}")
            outt_free.free = P.now("d_o")



_BUILD_CACHE = {}


def finalize_program(b):
    nc, P = b.nc, b.P
    with ExitStack() as st:
        for name in sorted(P.cnt):
            P.sem_handles[name] = st.enter_context(nc.semaphore(name))
        with nc.Block() as block:
            for eng in ENGS:
                ops = P.ops[eng]

                def runner(E, ops=ops):
                    for op in ops:
                        op(E)

                getattr(block, eng)(runner)
    b.finalized = True
    return b


def get_build(taps=(), phases=("W", "1a", "1b", "2")):
    key = (tuple(sorted(taps)), tuple(phases))
    if key not in _BUILD_CACHE:
        b = Build(taps=taps, phases=phases)
        b.build()
        finalize_program(b)
        _BUILD_CACHE[key] = b
    return _BUILD_CACHE[key]


def make_in_maps(inputs, consts):
    perm = _qk_perm()
    wq = np.ascontiguousarray(np.asarray(inputs["L0_attn_W_q"])[perm])
    wk = np.ascontiguousarray(np.asarray(inputs["L0_attn_W_k"])[perm])
    bq = np.asarray(inputs["L0_attn_b_q"])[perm]
    bk = np.asarray(inputs["L0_attn_b_k"])[perm]
    bv = np.asarray(inputs["L0_attn_b_v"])
    bqkv = np.zeros((128, 24), np.float32)
    for pi, b in enumerate((bq, bk, bv)):
        for oc in range(8):
            bqkv[:, pi * 8 + oc] = b[oc * 128:(oc + 1) * 128]
    shared = {
        "wq": wq, "wk": wk,
        "wv": np.asarray(inputs["L0_attn_W_v"]),
        "wo": np.asarray(inputs["L0_attn_W_o"]),
        "bqkv": bqkv,
        "bo_in": np.asarray(inputs["L0_attn_b_o"]).reshape(1, D),
        "rw_in": np.asarray(inputs["L0_ffn_router_w"]),
        "sup": np.asarray(inputs["L0_ffn_shared_up"]),
        "sdn": np.asarray(inputs["L0_ffn_shared_down"]),
        "rup": np.asarray(inputs["L0_ffn_routed_up"]),
        "rdn": np.asarray(inputs["L0_ffn_routed_down"]),
    }
    shared.update(consts)
    x = np.asarray(inputs["L0_attn_x"])
    maps = []
    for c in range(8):
        m = dict(shared)
        m["xin"] = np.ascontiguousarray(x[c])
        maps.append(m)
    return maps


def run_device(b, in_maps, core_ids):
    if not getattr(b, "finalized", False):
        finalize_program(b)
    return run_bass_kernel_spmd(b.nc, in_maps, core_ids=core_ids)


def kernel(**inputs):
    consts = host_constants()
    b = get_build()
    in_maps = make_in_maps(inputs, consts)
    res = run_device(b, in_maps, core_ids=list(range(8)))
    return np.stack([np.asarray(r["out"]) for r in res.results], axis=0)


# version 20
# speedup vs baseline: 1.1899x; 1.1899x over previous
import numpy as np
import ml_dtypes
from contextlib import ExitStack

import concourse.bass as bass
import concourse.mybir as mybir
from concourse.bass_utils import run_bass_kernel_spmd

F32 = mybir.dt.float32
BF16 = mybir.dt.bfloat16
I32 = mybir.dt.int32
AF = mybir.ActivationFunctionType
ALU = mybir.AluOpType
AX = mybir.AxisListType.X
BF = ml_dtypes.bfloat16

D = 1024
T = 4096
H = 16
NT = 8
TT = 512
NSUB = 4
NEXP = 18
PASSES = (4, 4, 4, 4, 2)
ENGS = ("sync", "tensor", "vector", "scalar", "gpsimd")



def _qk_perm():
    p64 = np.concatenate([np.arange(0, 64, 2), np.arange(1, 64, 2)])
    return np.concatenate([h * 64 + p64 for h in range(H)])


def host_constants():
    c = {}
    theta = 1.0 / 10000.0 ** (np.arange(32, dtype=np.float64) / 32.0)
    ang = np.arange(T, dtype=np.float64)[None, :] * theta[:, None]
    cos, sin = np.cos(ang), np.sin(ang)
    C = np.zeros((128, T), np.float64)
    S = np.zeros((128, T), np.float64)
    for blk in (0, 64):
        C[blk:blk + 32] = cos
        C[blk + 32:blk + 64] = cos
        S[blk:blk + 32] = -sin
        S[blk + 32:blk + 64] = sin
    c["crep"] = C.astype(BF)
    c["srep"] = S.astype(BF)
    P = np.zeros((128, 128), np.float32)
    for blk in (0, 64):
        for j in range(32):
            P[blk + j, blk + 32 + j] = 1.0
            P[blk + 32 + j, blk + j] = 1.0
    c["pswap"] = P.astype(BF)
    c["identb"] = np.eye(128, dtype=np.float32).astype(BF)
    c["identf"] = np.eye(128, dtype=np.float32)
    hs = np.zeros((128, 128), np.float32)
    for oc in range(8):
        hs[0:64, oc * 16 + 2 * oc] = 1.0
        hs[64:128, oc * 16 + 2 * oc + 1] = 1.0
    c["hsall"] = hs.astype(BF)
    e4 = np.zeros((128, 64), np.float32)
    for r in range(4):
        e4[:, r * 16 + r] = 1.0
    c["e4"] = e4.astype(BF)
    selp = np.zeros((128, 8 * 128), np.float32)
    for p in range(8):
        for i in range(128):
            h = 2 * p + (1 if i >= 64 else 0)
            selp[32 * (h % 4) + h // 4, p * 128 + i] = 1.0
    c["selp"] = selp
    m = np.zeros((128, 4 * 512), np.float32)
    for b in range(4):
        s = np.arange(128)[:, None] + 128 * b
        t = np.arange(512)[None, :]
        m[:, b * 512:(b + 1) * 512] = (s <= t)
    c["mask4"] = m.astype(BF)
    selr = np.zeros((16, 16 * 128), np.float32)
    for r in range(16):
        selr[r, r * 128:(r + 1) * 128] = 1.0
    c["selr"] = selr.astype(BF)
    c["ones1"] = np.ones((1, 128), np.float32)
    return c



class Buf:
    __slots__ = ("free", "ready")

    def __init__(self):
        self.free = None
        self.ready = None


class Planner:
    def __init__(self):
        self.ops = {e: [] for e in ENGS}
        self.cnt = {}
        self.sem_handles = {}
        self._gate = {e: None for e in ENGS}

    def sem(self, name):
        self.cnt.setdefault(name, 0)
        return name

    def now(self, sem):
        return (sem, self.cnt.get(sem, 0))

    def gate_all(self):
        snap = tuple((s, v) for s, v in self.cnt.items() if v > 0)
        for e in ENGS:
            self._gate[e] = snap

    def emit(self, eng, build, waits=(), inc=None, inc_amt=1, tag=""):
        waits = [w for w in waits if w is not None]
        for s, _ in waits:
            self.sem(s)
        if self._gate[eng]:
            waits = list(self._gate[eng]) + list(waits)
            self._gate[eng] = None
        val = None
        if inc is not None:
            self.sem(inc)
            self.cnt[inc] += inc_amt
            val = self.cnt[inc]
        waits = tuple((s, v) for (s, v) in waits if v > 0)

        def closure(E, waits=waits, build=build, inc=inc, inc_amt=inc_amt):
            for s, v in waits:
                E.wait_ge(self.sem_handles[s], v)
            ins = build(E)
            if inc is not None:
                if ins is None:
                    ins = E.nop()
                ins.then_inc(self.sem_handles[inc], inc_amt)
            return ins

        closure.tag = tag
        closure.waits = waits
        closure.inc = (inc, val, inc_amt) if inc else None
        self.ops[eng].append(closure)
        return self.now(inc) if inc else None

    def dma(self, eng, out_ap, in_ap, waits=(), inc=None, tag="", **kw):
        return self.emit(eng, lambda E: E.dma_start(out_ap, in_ap, **kw),
                         waits=waits, inc=inc, inc_amt=16, tag=tag)

    def simulate(self):
        idx = {e: 0 for e in ENGS}
        sem = {s: 0 for s in self.cnt}
        progress = True
        while progress:
            progress = False
            for e in ENGS:
                while idx[e] < len(self.ops[e]):
                    op = self.ops[e][idx[e]]
                    if all(sem[s] >= v for s, v in op.waits):
                        if op.inc:
                            sem[op.inc[0]] += op.inc[2]
                        idx[e] += 1
                        progress = True
                    else:
                        break
        stuck = {e: idx[e] for e in ENGS if idx[e] < len(self.ops[e])}
        if stuck:
            lines = []
            for e, i in stuck.items():
                op = self.ops[e][i]
                miss = [(s, v, sem.get(s, 0)) for s, v in op.waits
                        if sem.get(s, 0) < v]
                lines.append(f"  {e}[{i}/{len(self.ops[e])}] tag={op.tag} missing={miss}")
            raise RuntimeError("planned-program deadlock:\n" + "\n".join(lines))

    def n_ops(self):
        return {e: len(self.ops[e]) for e in ENGS}



class Build:
    def __init__(self, taps=(), phases=("W", "1a", "1b", "2"), nt2=NT,
                 p2parts=("router", "experts", "ln4")):
        self.taps = set(taps)
        self.phases = phases
        self.nt2 = nt2
        self.p2parts = set(p2parts)
        self.nc = bass.Bass("TRN2", target_bir_lowering=False)
        self.P = Planner()
        self.st = ExitStack()

    def sb(self, name, shape, dtype, scope=None):
        return (scope or self.st).enter_context(
            self.nc.sbuf_tensor(name, list(shape), dtype))

    def ps(self, name, shape, dtype=F32, scope=None):
        return (scope or self.st).enter_context(
            self.nc.psum_tensor(name, list(shape), dtype))

    def dram(self, name, shape, dtype, kind="ExternalInput"):
        return self.nc.dram_tensor(name, list(shape), dtype, kind=kind)

    def scratch(self, name, shape, dtype):
        kind = "ExternalOutput" if name in self.taps else "Internal"
        return self.nc.dram_tensor(name, list(shape), dtype, kind=kind)

    def newton_rsqrt(self, dst, src, pre=(), tag="nwt"):
        P = self.P
        p = src.shape[0]
        n = src.shape[1]
        t1 = self.nw1[0:p, 0:n]
        t2 = self.nw2[0:p, 0:n]
        th = self.nw3[0:p, 0:n]
        P.emit("vector", lambda E: E.memset(t1.bitcast(I32), 0x5F3759DF),
               waits=pre, inc="v", tag=tag + "_m")
        P.emit("vector", lambda E: E.tensor_scalar(
            t2.bitcast(I32), src.bitcast(I32), 1, None,
            op0=ALU.logical_shift_right), waits=(P.now("v"),), inc="v",
            tag=tag + "_sh")
        P.emit("vector", lambda E: E.tensor_tensor(
            dst.bitcast(I32), t1.bitcast(I32), t2.bitcast(I32),
            op=ALU.subtract), waits=(P.now("v"),), inc="v", tag=tag + "_y0")
        P.emit("vector", lambda E: E.tensor_scalar_mul(th, src, 0.5),
               waits=(P.now("v"),), inc="v", tag=tag + "_h")
        for it in range(3):
            P.emit("vector", lambda E: E.tensor_mul(t2, dst, dst),
                   waits=(P.now("v"),), inc="v", tag=tag + f"_a{it}")
            P.emit("vector", lambda E: E.tensor_mul(t2, th, t2),
                   waits=(P.now("v"),), inc="v", tag=tag + f"_b{it}")
            P.emit("vector", lambda E: E.tensor_scalar(
                t2, t2, -1.0, 1.5, op0=ALU.mult, op1=ALU.add),
                waits=(P.now("v"),), inc="v", tag=tag + f"_c{it}")
            P.emit("vector", lambda E: E.tensor_mul(dst, dst, t2),
                   waits=(P.now("v"),), inc="v", tag=tag + f"_d{it}")
        return P.now("v")

    def ln_stats(self, src_fn, xwait, tag, n=NSUB, extra_s2_wait=(),
                 serialize_ps=False):
        P = self.P
        stats = self.stats
        sw = [xwait] + list(extra_s2_wait)
        if self.stats_free.free:
            sw.append(self.stats_free.free)
        for u in range(n):
            P.emit("vector", (lambda E, o=stats[:, u:u + 1], i_=src_fn(u):
                              E.reduce_sum(o, i_, axis=AX)),
                   waits=tuple(sw) if u == 0 else (), inc="v",
                   tag=f"{tag}_s1_{u}")
            s2w = [P.now("s")] if u > 0 else list(sw)
            if serialize_ps:
                s2w.append(P.now("v"))
            P.emit("scalar", (lambda E, i_=src_fn(u), a=stats[:, 4 + u:5 + u]:
                              E.activation(self.sqdump[:, 0:i_.shape[1]], i_,
                                           AF.Square, accum_out=a)),
                   waits=tuple(s2w), inc="s",
                   tag=f"{tag}_s2_{u}")
        S1, S2 = P.now("v"), P.now("s")
        P.emit("vector", lambda E: E.tensor_scalar_mul(
            stats[:, 8:8 + n], stats[:, 0:n], 1.0 / 1024),
            waits=(S1, S2), inc="v", tag=f"{tag}_mu")
        P.emit("vector", lambda E: E.tensor_mul(
            stats[:, 16:16 + n], stats[:, 8:8 + n], stats[:, 8:8 + n]),
            waits=(P.now("v"),), inc="v", tag=f"{tag}_mu2")
        P.emit("vector", lambda E: E.tensor_scalar_mul(
            stats[:, 12:12 + n], stats[:, 4:4 + n], 1.0 / 1024),
            waits=(P.now("v"),), inc="v", tag=f"{tag}_va")
        P.emit("vector", lambda E: E.tensor_tensor(
            stats[:, 12:12 + n], stats[:, 12:12 + n], stats[:, 16:16 + n],
            op=ALU.subtract), waits=(P.now("v"),), inc="v", tag=f"{tag}_vb")
        P.emit("vector", lambda E: E.tensor_scalar_add(
            stats[:, 12:12 + n], stats[:, 12:12 + n], 1e-5),
            waits=(P.now("v"),), inc="v", tag=f"{tag}_vc")
        self.newton_rsqrt(stats[:, 20:20 + n], stats[:, 12:12 + n],
                          pre=(P.now("v"),), tag=f"{tag}_nw")
        P.emit("vector", lambda E: E.tensor_mul(
            stats[:, 24:24 + n], stats[:, 8:8 + n], stats[:, 20:20 + n]),
            waits=(P.now("v"),), inc="v", tag=f"{tag}_nm")
        P.emit("vector", lambda E: E.tensor_scalar_mul(
            stats[:, 24:24 + n], stats[:, 24:24 + n], -1.0),
            waits=(P.now("v"),), inc="v", tag=f"{tag}_nm2")
        return P.now("v")

    def build(self):
        nc, P = self.nc, self.P
        self.xin = self.dram("xin", [T, D], F32)
        self.wsrc = {n: self.dram("w" + n, [D, D], F32) for n in "qkvo"}
        self.bqkv_d = self.dram("bqkv", [128, 24], F32)
        self.bo_d = self.dram("bo_in", [1, D], F32)
        self.rw_d = self.dram("rw_in", [16, D], F32)
        self.sup_d = self.dram("sup", [2, 512, D], F32)
        self.sdn_d = self.dram("sdn", [2, D, 512], F32)
        self.rup_d = self.dram("rup", [16, 512, D], F32)
        self.rdn_d = self.dram("rdn", [16, D, 512], F32)
        cd = {}
        for n, sh, dt in (("crep", [128, T], BF16), ("srep", [128, T], BF16),
                          ("pswap", [128, 128], BF16), ("identb", [128, 128], BF16),
                          ("identf", [128, 128], F32), ("hsall", [128, 128], BF16),
                          ("e4", [128, 64], BF16), ("selp", [128, 1024], F32),
                          ("mask4", [128, 2048], BF16), ("selr", [16, 2048], BF16),
                          ("ones1", [1, 128], F32)):
            cd[n] = self.dram(n, sh, dt)
        self.cd = cd
        self.out_d = self.dram("out", [T, D], F32, kind="ExternalOutput")
        self.qf_s = self.scratch("qf_s", [NT, 128, 4096], BF16)
        self.kf_s = self.scratch("kf_s", [NT, 128, 4096], BF16)
        self.vt_s = self.scratch("vt_s", [NT, 128, 4096], BF16)
        self.x1_s = self.scratch("x1_s", [NT, 128, 4096], F32)
        self.wot_s = self.scratch("wot_s", [128, 8192], BF16)
        self.rwt_s = self.scratch("rwt_s", [128, 128], BF16)
        self.utp_s = self.scratch("utp_s", [NEXP, 128, 4096], BF16)
        self.dtp_s = self.scratch("dtp_s", [NEXP, 128, 4096], BF16)

        self.crep = [self.sb(f"crep_sb{j}", [128, 512], BF16) for j in range(2)]
        self.srep = [self.sb(f"srep_sb{j}", [128, 512], BF16) for j in range(2)]
        self.pswap = self.sb("pswap_sb", [128, 128], BF16)
        self.identb = self.sb("identb_sb", [128, 128], BF16)
        self.identf = self.sb("identf_sb", [128, 128], F32)
        self.hsall = self.sb("hsall_sb", [128, 128], BF16)
        self.ones1 = self.sb("ones1_sb", [1, 128], F32)
        self.bqkv = self.sb("bqkv_sb", [128, 24], F32)
        self.bo_sb = self.sb("bo_sb", [1, 1024], F32)
        self.runmax = self.sb("runmax_sb", [16, 16], F32)
        self.bias_ch = self.sb("bias_ch", [128, 16], F32)
        self.bias_pr = self.sb("bias_pr", [128, 8], F32)
        self.bo_rep = self.sb("bo_rep", [128, 1024], F32)
        self.stats = self.sb("stats_sb", [128, 32], F32)
        self.nw1 = self.sb("nw1", [128, 8], F32)
        self.nw2 = self.sb("nw2", [128, 8], F32)
        self.nw3 = self.sb("nw3", [128, 8], F32)
        self.sqdump = self.sb("sqdump", [128, 1024], BF16)
        self.stats_free = Buf()

        for n, t in (("pswap", self.pswap), ("identb", self.identb),
                     ("identf", self.identf), ("hsall", self.hsall),
                     ("ones1", self.ones1)):
            P.dma("sync", t[:, :], cd[n][:, :], inc="d_c", tag="c_" + n)
        P.dma("sync", self.bqkv[:, :], self.bqkv_d[:, :], inc="d_c", tag="c_bqkv")
        P.dma("sync", self.bo_sb[:, :], self.bo_d[:, :], inc="d_c", tag="c_bo")
        self.CONSTS = P.now("d_c")

        with ExitStack() as s1a:
            self.phase_w_attn(s1a)
            if "1a" in self.phases:
                self.phase_1a(s1a)
                self.phase_15()
        if "1b" in self.phases:
            P.gate_all()
            with ExitStack() as s1b:
                self.phase_1b(s1b)
        if "2" in self.phases:
            P.gate_all()
            with ExitStack() as s2:
                self.phase_2(s2)
        fin = [P.now("d_o")] if P.cnt.get("d_o") else []
        for s in ("d_spq", "d_spk", "d_spv", "d_x1w", "d_fwst0", "d_fwst1",
                  "d_woS", "d_rwS", "d_dbg"):
            if P.cnt.get(s):
                fin.append(P.now(s))
        P.emit("sync", lambda E: E.nop(), waits=tuple(fin), tag="final")
        P.simulate()
        return self

    def phase_w_attn(self, sc):
        nc, P = self.nc, self.P
        self.wstage = self.sb("wstage", [128, 8192], BF16, sc)
        self.wtrg = {n: self.sb(f"w{n}t_sb", [128, 8192], BF16, sc)
                     for n in "qkv"}
        self.fin = [self.sb(f"fw_in{j}", [128, 4096], BF16, sc) for j in range(2)]
        self.fout = [self.sb("fw_out0", [128, 4096], BF16, sc)] * 2
        self.fin_free = [Buf(), Buf()]
        _fb = Buf()
        self.fout_free = [_fb, _fb]
        self.psW = self.ps("psW", [128, 512], BF16, sc)
        self.psW2 = self.ps("psW2", [128, 512], BF16, sc)
        self.psw_free = [Buf(), Buf()]
        wstage_free = Buf()
        self.attnw_ready = {}
        for wn in "qkvo":
            dst = self.wtrg[wn] if wn != "o" else self.fout[0]
            P.dma("gpsimd", self.wstage[:, :].rearrange("p (c d) -> p c d", c=8),
                  self.wsrc[wn][:, :].rearrange("(c p) d -> p c d", p=128),
                  waits=(wstage_free.free,) if wstage_free.free else (),
                  inc="d_wA", tag=f"w{wn}_ld")
            LD = P.now("d_wA")
            for dc in range(8):
                for half in range(2):
                    k = (dc * 2 + half) % 2
                    pw = (self.psW, self.psW2)[k]
                    pb = self.psw_free[k]
                    for oq in range(4):
                        oc = half * 4 + oq
                        w = [LD, self.CONSTS] if oq == 0 else []
                        if oq == 0 and pb.free:
                            w.append(pb.free)
                        if (wn == "o" and oq == 0 and half == 0 and dc in (0, 4)
                                and self.fout_free[0].free):
                            w.append(self.fout_free[0].free)
                        P.emit("tensor",
                               (lambda E, o=pw[:, oq * 128:(oq + 1) * 128],
                                i_=self.wstage[:, oc * 1024 + dc * 128:
                                               oc * 1024 + dc * 128 + 128]:
                                E.transpose(o, i_, self.identb[:, :])),
                               waits=tuple(w), inc="t" if oq == 3 else None,
                               tag=f"w{wn}_t{dc}{half}")
                    dcol = (dc * 1024 + half * 512) if wn != "o" else \
                        ((dc % 4) * 1024 + half * 512)
                    P.emit("vector",
                           (lambda E, d=dst[:, dcol:dcol + 512],
                            sp=pw[:, :]: E.tensor_copy(d, sp)),
                           waits=(P.now("t"),), inc="v", tag=f"w{wn}_e{dc}{half}")
                    pb.free = P.now("v")
                if wn == "o" and dc in (3, 7):
                    g = dc // 4
                    P.dma("sync", self.wot_s[:, g * 4096:(g + 1) * 4096],
                          self.fout[0][:, :], waits=(P.now("v"),),
                          inc="d_woS", tag=f"wo_st{g}")
                    self.fout_free[0].free = P.now("d_woS")
            self.attnw_ready[wn] = P.now("v")
            wstage_free.free = P.now("v")

        rw_stage = self.sb("rw_stage", [16, 1024], BF16, sc)
        rwt_stage = self.sb("rwt_stage", [128, 128], BF16, sc)
        P.dma("gpsimd", rw_stage[:, :], self.rw_d[:, :],
              waits=(wstage_free.free,), inc="d_wA", tag="wrw_ld")
        RWL = P.now("d_wA")
        for g in range(2):
            pw = (self.psW, self.psW2)[g]
            pb = self.psw_free[g]
            for q in range(4):
                dc = g * 4 + q
                w = [RWL, self.CONSTS] if q == 0 else []
                if q == 0 and pb.free:
                    w.append(pb.free)
                P.emit("tensor",
                       (lambda E, o=pw[:, q * 16:(q + 1) * 16],
                        i_=rw_stage[:, dc * 128:(dc + 1) * 128]:
                        E.transpose(o, i_, self.identb[0:16, 0:16])),
                       waits=tuple(w), inc="t" if q == 3 else None,
                       tag=f"wrw_t{dc}")
            P.emit("vector", (lambda E, d=rwt_stage[:, g * 64:(g + 1) * 64],
                              sp=pw[:, 0:64]: E.tensor_copy(d, sp)),
                   waits=(P.now("t"),), inc="v", tag=f"wrw_e{g}")
            pb.free = P.now("v")
        P.dma("sync", self.rwt_s[:, :], rwt_stage[:, :], waits=(P.now("v"),),
              inc="d_rwS", tag="wrw_st")

        mats = []
        for ei in range(16):
            mats.append((self.rup_d[ei].rearrange("(c p) d -> p c d", p=128),
                         4, self.utp_s[ei], f"fu{ei}"))
            mats.append((self.rdn_d[ei].rearrange("(c p) e -> p c e", p=128),
                         8, self.dtp_s[ei], f"fd{ei}"))
        for si in range(2):
            mats.append((self.sup_d[si].rearrange("(c p) d -> p c d", p=128),
                         4, self.utp_s[16 + si], f"fu{16 + si}"))
            mats.append((self.sdn_d[si].rearrange("(c p) e -> p c e", p=128),
                         8, self.dtp_s[16 + si], f"fd{16 + si}"))
        self.ffn_mats = mats
        self.ffn_load_i = 0
        self.ffn_comp_i = 0
        self.ffn_ld_ms = {}

    def _ffn_load(self, mi):
        P = self.P
        src, nch, dst, tag = self.ffn_mats[mi]
        bi = mi % 2
        P.dma("gpsimd",
              self.fin[bi][:, :].rearrange("p (c d) -> p c d", c=nch), src,
              waits=(self.fin_free[bi].free,) if self.fin_free[bi].free else (),
              inc=f"d_f{bi}", tag=tag + "_ld")
        self.ffn_ld_ms[mi] = P.now(f"d_f{bi}")

    def _ffn_comp(self, mi):
        P = self.P
        src, nch, dst, tag = self.ffn_mats[mi]
        bi = mi % 2
        LD = self.ffn_ld_ms[mi]
        fr = 4096 // nch
        nblk = fr // 128
        ow = self.fout[bi]
        first = True
        for b in range(nblk):
            ngr = max(1, nch // 4)
            for g in range(ngr):
                k = (b * ngr + g) % 2
                pw = (self.psW, self.psW2)[k]
                pb = self.psw_free[k]
                span = min(4, nch)
                for q in range(span):
                    c_in = g * 4 + q
                    w = []
                    if q == 0:
                        w = [LD, self.CONSTS]
                        if pb.free:
                            w.append(pb.free)
                        if first and self.fout_free[bi].free:
                            w.append(self.fout_free[bi].free)
                        first = False
                    P.emit("tensor",
                           (lambda E, o=pw[:, q * 128:(q + 1) * 128],
                            i_=self.fin[bi][:, c_in * fr + b * 128:
                                            c_in * fr + b * 128 + 128]:
                            E.transpose(o, i_, self.identb[:, :])),
                           waits=tuple(w), inc="t" if q == span - 1 else None,
                           tag=tag + f"_t{b}{g}")
                P.emit("vector",
                       (lambda E, d=ow[:, b * (nch * 128) + g * 512:
                                       b * (nch * 128) + g * 512 + span * 128],
                        sp=pw[:, 0:span * 128]: E.tensor_copy(d, sp)),
                       waits=(P.now("t"),), inc="v", tag=tag + f"_e{b}{g}")
                pb.free = P.now("v")
        self.fin_free[bi].free = P.now("v")
        P.dma("sync", dst[:, :], ow[:, :], waits=(P.now("v"),),
              inc=f"d_fwst{bi}", tag=tag + "_st")
        self.fout_free[bi].free = P.now(f"d_fwst{bi}")

    def emit_ffn_prep(self, nmat, flush=False):
        n = len(self.ffn_mats)
        target = min(self.ffn_load_i + nmat, n)
        while (self.ffn_load_i < target
               or (flush and self.ffn_comp_i < n)):
            if (self.ffn_load_i < target
                    and self.ffn_load_i - self.ffn_comp_i < 2):
                self._ffn_load(self.ffn_load_i)
                self.ffn_load_i += 1
            elif self.ffn_comp_i < self.ffn_load_i:
                self._ffn_comp(self.ffn_comp_i)
                self.ffn_comp_i += 1
            else:
                break

    def phase_1a(self, sc):
        nc, P = self.nc, self.P
        x_sb = [self.sb(f"x_sb{j}", [128, 4096], F32, sc) for j in range(2)]
        xn_tok = self.sb("xn_tok", [128, 4096], BF16, sc)
        xn_f = self.sb("xn_f", [128, 4096], BF16, sc)
        proj = [self.sb(f"proj{j}", [128, 4096], BF16, sc) for j in range(3)]
        qrot = self.sb("qrot", [128, 4096], BF16, sc)
        krot = self.sb("krot", [128, 4096], BF16, sc)
        vtok = self.sb("vtok", [128, 4096], BF16, sc)
        sqn = self.sb("sqn", [128, 1024], BF16, sc)
        ttmp = self.sb("ttmp", [128, 512], F32, sc)
        psQ = self.ps("psQ", [128, 512], F32, sc)
        psQ2 = self.ps("psQ2", [128, 512], F32, sc)
        psSw = self.ps("psSw", [128, 512], F32, sc)
        psSw2 = self.ps("psSw2", [128, 512], F32, sc)
        psNorm = self.ps("psNorm", [16, 512], F32, sc)
        self.psNorm = psNorm
        self.psQ_ref = psQ

        x_free = [Buf(), Buf()]
        xnf_free = Buf()
        xntok_free = Buf()
        proj_free = [Buf(), Buf(), Buf()]
        rot_free = {"q": Buf(), "k": Buf(), "v": Buf()}
        psq_free = [Buf(), Buf()]
        self.psq_free_ref = psq_free
        pssw_free = [Buf(), Buf()]
        self.psnorm_free = Buf()
        cs_free = [Buf(), Buf()]
        sqn_free = [Buf(), Buf()]
        xn_ms = [None, None]
        csl_ms = [None, None]

        xl_ms = [None, None]

        def emit_x_load(ii):
            bi = ii % 2
            xb = x_sb[bi]
            P.dma("sync", xb[:, :].rearrange("p (u d) -> p u d", u=4),
                  self.xin[ii * TT:(ii + 1) * TT, :].rearrange(
                      "(u p) d -> p u d", p=128),
                  waits=(x_free[bi].free,) if x_free[bi].free else (),
                  inc=f"d_x{bi}", tag=f"x_ld{ii}")
            xl_ms[bi] = P.now(f"d_x{bi}")
            P.dma("sync", self.crep[bi][:, :],
                  self.cd["crep"][:, ii * TT:(ii + 1) * TT],
                  waits=(cs_free[bi].free,) if cs_free[bi].free else (),
                  inc="d_cs", tag=f"cs_c{ii}")
            P.dma("sync", self.srep[bi][:, :],
                  self.cd["srep"][:, ii * TT:(ii + 1) * TT],
                  inc="d_cs", tag=f"cs_s{ii}")
            csl_ms[bi] = P.now("d_cs")

        def emit_ln1(ii):
            bi = ii % 2
            xb = x_sb[bi]
            XL = xl_ms[bi]
            LN1 = self.ln_stats(lambda u: xb[:, u * 1024:(u + 1) * 1024],
                                XL, f"ln1_{ii}", extra_s2_wait=(self.CONSTS,))
            xw = [LN1]
            if xntok_free.free:
                xw.append(xntok_free.free)
            for u in range(NSUB):
                P.emit("scalar",
                       (lambda E, o=xn_tok[:, u * 1024:(u + 1) * 1024],
                        i_=xb[:, u * 1024:(u + 1) * 1024],
                        scv=self.stats[:, 20 + u:21 + u],
                        bv=self.stats[:, 24 + u:25 + u]:
                        E.activation(o, i_, AF.Identity, bias=bv, scale=scv)),
                       waits=tuple(xw) if u == 0 else (), inc="s",
                       tag=f"ln1n_{ii}_{u}")
            XN = P.now("s")
            self.stats_free.free = XN
            x_free[bi].free = XN
            xn_ms[bi] = XN

        emit_x_load(0)
        emit_ln1(0)
        for i in range(NT):
            bi = i % 2
            XN = xn_ms[bi]
            CSL = csl_ms[i % 2]

            tw = [XN]
            if xnf_free.free:
                tw.append(xnf_free.free)
            for c in range(8):
                k = c % 2
                pw = (psQ, psQ2)[k]
                pb = psq_free[k]
                for u in range(NSUB):
                    w = list(tw) if (c == 0 and u == 0) else []
                    if u == 0 and pb.free:
                        w.append(pb.free)
                    P.emit("tensor",
                           (lambda E, o=pw.bitcast(BF16)[:, u * 128:(u + 1) * 128],
                            i_=xn_tok[:, u * 1024 + c * 128:u * 1024 + c * 128 + 128]:
                            E.transpose(o, i_, self.identb[:, :])),
                           waits=tuple(w), inc="t" if u == 3 else None,
                           tag=f"xnt{i}_{c}{u}")
                P.emit("vector",
                       (lambda E, d=xn_f[:, c * 512:(c + 1) * 512],
                        sp=pw.bitcast(BF16)[:, 0:512]: E.tensor_copy(d, sp)),
                       waits=(P.now("t"),), inc="v", tag=f"xne{i}_{c}")
                pb.free = P.now("v")
            XNF = P.now("v")
            xntok_free.free = P.now("t")
            if i + 1 < NT:
                emit_x_load(i + 1)

            for pi, pn in enumerate("qkv"):
                wt = self.wtrg[pn]
                bcol = pi * 8
                projb = proj[pi]
                pj_w = [XNF, self.attnw_ready[pn]]
                if proj_free[pi].free:
                    pj_w.append(proj_free[pi].free)
                if pn in "qk":
                    rotb = qrot if pn == "q" else krot
                    rfree = rot_free[pn]
                    EV = {}
                    RP = {}
                    for step in range(10):
                        if step < 8:
                            oc = step
                            k = oc % 2
                            pw = (psQ, psQ2)[k]
                            pb = psq_free[k]
                            for dc in range(8):
                                w = list(pj_w) if (oc == 0 and dc == 0) else []
                                if dc == 0 and pb.free:
                                    w.append(pb.free)
                                P.emit("tensor",
                                       (lambda E, o=pw[:, :],
                                        l=wt[:, dc * 1024 + oc * 128:
                                             dc * 1024 + oc * 128 + 128],
                                        r=xn_f[:, dc * 512:(dc + 1) * 512],
                                        s0=(dc == 0), s1=(dc == 7):
                                        E.matmul(o, l, r, start=s0, stop=s1)),
                                       waits=tuple(w),
                                       inc="t" if dc == 7 else None,
                                       tag=f"{pn}mm{i}_{oc}{dc}")
                            pj_w = []
                            P.emit("scalar",
                                   (lambda E, o=projb[:, oc * 512:(oc + 1) * 512],
                                    sp=pw[:, :],
                                    bv=self.bqkv[:, bcol + oc:bcol + oc + 1]:
                                    E.activation(o, sp, AF.Identity, bias=bv)),
                                   waits=(P.now("t"),), inc="s",
                                   tag=f"{pn}ev{i}_{oc}")
                            pb.free = P.now("s")
                            EV[oc] = P.now("s")
                        if 1 <= step <= 8:
                            oc = step - 1
                            k = oc % 2
                            pw = (psSw, psSw2)[k]
                            pb = pssw_free[k]
                            w = [EV[oc], self.CONSTS]
                            if pb.free:
                                w.append(pb.free)
                            P.emit("tensor",
                                   (lambda E, o=pw[:, :],
                                    r=projb[:, oc * 512:(oc + 1) * 512]:
                                    E.matmul(o, self.pswap[:, :], r,
                                             start=True, stop=True)),
                                   waits=tuple(w), inc="t", tag=f"{pn}sw{i}_{oc}")
                            SW = P.now("t")
                            w2 = [SW, CSL]
                            if oc == 0 and rfree.free:
                                w2.append(rfree.free)
                            P.emit("vector",
                                   (lambda E, o=rotb[:, oc * 512:(oc + 1) * 512],
                                    a=projb[:, oc * 512:(oc + 1) * 512],
                                    cr=self.crep[i % 2][:, :]:
                                    E.tensor_mul(o, a, cr)),
                                   waits=tuple(w2), inc="v", tag=f"{pn}r1{i}_{oc}")
                            P.emit("vector",
                                   (lambda E, sp=pw[:, :],
                                    sr=self.srep[i % 2][:, :]:
                                    E.tensor_mul(ttmp[:, :], sp, sr)),
                                   waits=(P.now("v"),), inc="v",
                                   tag=f"{pn}r2{i}_{oc}")
                            pb.free = P.now("v")
                            P.emit("vector",
                                   (lambda E, o=rotb[:, oc * 512:(oc + 1) * 512]:
                                    E.tensor_add(o, o, ttmp[:, :])),
                                   waits=(P.now("v"),), inc="v",
                                   tag=f"{pn}r3{i}_{oc}")
                            RP[oc] = P.now("v")
                        if step >= 2:
                            oc = step - 2
                            k = oc % 2
                            w = [RP[oc]]
                            if oc == 0 and self.psnorm_free.free:
                                w.append(self.psnorm_free.free)
                            if sqn_free[k].free:
                                w.append(sqn_free[k].free)
                            P.emit("scalar",
                                   (lambda E, o=sqn[:, k * 512:(k + 1) * 512],
                                    i_=rotb[:, oc * 512:(oc + 1) * 512]:
                                    E.activation(o, i_, AF.Square)),
                                   waits=tuple(w), inc="s",
                                   tag=f"{pn}nsq{i}_{oc}")
                            P.emit("tensor",
                                   (lambda E, l=self.hsall[:, oc * 16:(oc + 1) * 16],
                                    r=sqn[:, k * 512:(k + 1) * 512],
                                    s0=(oc == 0), s1=(oc == 7):
                                    E.matmul(psNorm[:, :], l, r, start=s0, stop=s1,
                                             skip_group_check=True)),
                                   waits=(P.now("s"),), inc="t",
                                   tag=f"{pn}nmm{i}_{oc}")
                            sqn_free[k].free = P.now("t")
                    ROT = RP[7]
                    col = i if pn == "q" else 8 + i
                    P.emit("vector",
                           (lambda E, o=self.runmax[:, col:col + 1]:
                            E.reduce_max(o, psNorm[:, :], axis=AX)),
                           waits=(P.now("t"),), inc="v", tag=f"{pn}nrm{i}")
                    self.psnorm_free.free = P.now("v")
                    spsem = "d_spq" if pn == "q" else "d_spk"
                    P.dma("sync", (self.qf_s if pn == "q" else self.kf_s)[i][:, :],
                          rotb[:, :], waits=(ROT,), inc=spsem, tag=f"{pn}sp{i}")
                    rfree.free = P.now(spsem)
                    proj_free[pi].free = ROT
                    if pn == "k" and i + 1 < NT:
                        emit_ln1(i + 1)
                else:
                    for oc in range(8):
                        k = oc % 2
                        pw = (psQ, psQ2)[k]
                        pb = psq_free[k]
                        for dc in range(8):
                            w = list(pj_w) if (oc == 0 and dc == 0) else []
                            if dc == 0 and pb.free:
                                w.append(pb.free)
                            P.emit("tensor",
                                   (lambda E, o=pw[:, :],
                                    l=wt[:, dc * 1024 + oc * 128:
                                         dc * 1024 + oc * 128 + 128],
                                    r=xn_f[:, dc * 512:(dc + 1) * 512],
                                    s0=(dc == 0), s1=(dc == 7):
                                    E.matmul(o, l, r, start=s0, stop=s1)),
                                   waits=tuple(w), inc="t" if dc == 7 else None,
                                   tag=f"{pn}mm{i}_{oc}{dc}")
                        pj_w = []
                        P.emit("scalar",
                               (lambda E, o=projb[:, oc * 512:(oc + 1) * 512],
                                sp=pw[:, :],
                                bv=self.bqkv[:, bcol + oc:bcol + oc + 1]:
                                E.activation(o, sp, AF.Identity, bias=bv)),
                               waits=(P.now("t"),), inc="s", tag=f"{pn}ev{i}_{oc}")
                        pb.free = P.now("s")
                    PRJ = P.now("s")
                    vw = [PRJ]
                    if rot_free["v"].free:
                        vw.append(rot_free["v"].free)
                    for u in range(NSUB):
                        for half in range(2):
                            k = half
                            pw = (psSw, psSw2)[k]
                            pb = pssw_free[k]
                            for oq in range(4):
                                oc = half * 4 + oq
                                w = list(vw) if (u == 0 and half == 0 and oq == 0) else []
                                if oq == 0 and pb.free:
                                    w.append(pb.free)
                                P.emit("tensor",
                                       (lambda E,
                                        o=pw.bitcast(BF16)[:, oq * 128:(oq + 1) * 128],
                                        i_=projb[:, oc * 512 + u * 128:
                                                 oc * 512 + u * 128 + 128]:
                                        E.transpose(o, i_, self.identb[:, :])),
                                       waits=tuple(w),
                                       inc="t" if oq == 3 else None,
                                       tag=f"vt{i}_{u}{half}{oq}")
                            P.emit("vector",
                                   (lambda E, d=vtok[:, u * 1024 + half * 512:
                                                     u * 1024 + half * 512 + 512],
                                    sp=pw.bitcast(BF16)[:, 0:512]:
                                    E.tensor_copy(d, sp)),
                                   waits=(P.now("t"),), inc="v",
                                   tag=f"vte{i}_{u}{half}")
                            pb.free = P.now("v")
                        vw = []
                    VT = P.now("v")
                    P.dma("sync", self.vt_s[i][:, :], vtok[:, :], waits=(VT,),
                          inc="d_spv", tag=f"vsp{i}")
                    rot_free["v"].free = P.now("d_spv")
                    proj_free[pi].free = VT
            xnf_free.free = P.now("t")
            cs_free[bi].free = P.now("v")
            self.emit_ffn_prep(9 if i >= 4 else 4)
        self.emit_ffn_prep(len(self.ffn_mats), flush=True)
        self.x_free_1a = x_free

    def phase_15(self):
        P = self.P
        nw1, nw2, nw3 = self.nw1, self.nw2, self.nw3
        psNorm = self.psNorm
        P.emit("vector", lambda E: E.reduce_max(nw1[0:16, 0:1],
                                                self.runmax[:, 0:8], axis=AX),
               waits=(P.now("v"), P.now("t"), P.now("s")), inc="v", tag="ch_mq")
        P.emit("vector", lambda E: E.reduce_max(nw1[0:16, 1:2],
                                                self.runmax[:, 8:16], axis=AX),
               waits=(P.now("v"),), inc="v", tag="ch_mk")
        P.emit("vector", lambda E: E.tensor_mul(nw1[0:16, 2:3], nw1[0:16, 0:1],
                                                nw1[0:16, 1:2]),
               waits=(P.now("v"),), inc="v", tag="ch_p")
        P.emit("vector", lambda E: E.tensor_scalar_max(nw1[0:16, 2:3],
                                                       nw1[0:16, 2:3], 1e-12),
               waits=(P.now("v"),), inc="v", tag="ch_cl")
        self.newton_rsqrt(self.stats[0:16, 28:29], nw1[0:16, 2:3],
                          pre=(P.now("v"), self.stats_free.free), tag="ch_nw")
        P.emit("vector", lambda E: E.tensor_mul(nw1[0:16, 3:4], nw1[0:16, 2:3],
                                                self.stats[0:16, 28:29]),
               waits=(P.now("v"),), inc="v", tag="ch_s")
        P.emit("vector", lambda E: E.tensor_scalar_mul(nw1[0:16, 3:4],
                                                       nw1[0:16, 3:4], -0.125),
               waits=(P.now("v"),), inc="v", tag="ch_n")
        CH = P.now("v")
        P.emit("tensor", lambda E: E.transpose(psNorm[0:1, 0:16],
                                               nw1[0:16, 3:4],
                                               self.identf[0:16, 0:16]),
               waits=(CH, self.psnorm_free.free), inc="t", tag="ch_t")
        P.emit("vector", lambda E: E.tensor_copy(self.stats[0:1, 0:16],
                                                 psNorm[0:1, 0:16]),
               waits=(P.now("t"), self.stats_free.free), inc="v", tag="ch_e")
        psQ = self.psQ_ref
        pqf = self.psq_free_ref[0]
        P.emit("tensor", lambda E: E.matmul(psQ[:, 0:16], self.ones1[:, :],
                                            self.stats[0:1, 0:16],
                                            start=True, stop=True),
               waits=(P.now("v"), pqf.free), inc="t", tag="ch_b")
        P.emit("vector", lambda E: E.tensor_copy(self.bias_ch[:, :],
                                                 psQ[:, 0:16]),
               waits=(P.now("t"),), inc="v", tag="ch_be")
        P.emit("vector", lambda E: E.tensor_scalar_max(
            self.bias_ch[:, :], self.bias_ch[:, :], -30.0),
            waits=(P.now("v"),), inc="v", tag="ch_cap")
        for p_ in range(8):
            P.emit("vector",
                   (lambda E, o=self.bias_pr[:, p_:p_ + 1],
                    a=self.bias_ch[:, 2 * p_:2 * p_ + 1],
                    b2=self.bias_ch[:, 2 * p_ + 1:2 * p_ + 2]:
                    E.tensor_tensor(o, a, b2, op=ALU.min)),
                   waits=(P.now("v"),), inc="v", tag=f"ch_pr{p_}")
        self.psnorm_free.free = P.now("v")
        self.stats_free.free = P.now("v")
        if "dbg_bias" in self.taps:
            dbg_b = self.scratch("dbg_bias", [128, 16], F32)
            dbg_r = self.scratch("dbg_rmax", [16, 16], F32)
            P.dma("sync", dbg_b[:, :], self.bias_ch[:, :],
                  waits=(P.now("v"),), inc="d_dbg", tag="dbg_bias")
            P.dma("sync", dbg_r[:, :], self.runmax[:, :],
                  waits=(P.now("v"),), inc="d_dbg", tag="dbg_rmax")

    def phase_1b(self, sc):
        nc, P = self.nc, self.P
        wot = self.sb("wot_sb", [128, 8192], BF16, sc)
        q_sb = [self.sb(f"q_sb{j}", [128, 4096], BF16, sc) for j in range(2)]
        k_sb = [self.sb(f"k_sb{j}", [128, 4096], BF16, sc) for j in range(2)]
        v_sb = [self.sb(f"v_sb{j}", [128, 4096], BF16, sc) for j in range(2)]
        probs = self.sb("probs", [128, 4096], BF16, sc)
        attn = self.sb("attn", [128, 4096], BF16, sc)
        av32 = self.sb("av32", [128, 512], F32, sc)
        xr_sb = [self.sb(f"xr_sb{j}", [128, 4096], F32, sc) for j in range(2)]
        out1 = self.sb("out1", [128, 4096], F32, sc)
        x1t = self.sb("x1t", [128, 4096], F32, sc)
        selp_sb = self.sb("selp_sb", [128, 1024], F32, sc)
        mask_sb = self.sb("mask_sb", [128, 2048], BF16, sc)
        e4_sb = self.sb("e4_sb", [128, 64], BF16, sc)
        Ssb = self.sb("Ssb", [128, 512], F32, sc)
        recS = self.sb("recS", [128, 512], F32, sc)
        psA = self.ps("psA", [128, 2048], F32, sc)
        psAV = [self.ps(f"psAV{j}", [128, 512], F32, sc) for j in range(2)]
        psWo = self.ps("psWo", [128, 512], F32, sc)
        psWo2 = self.ps("psWo2", [128, 512], F32, sc)
        psSums = psWo
        zero128 = self.sb("zero128", [128, 128], BF16, sc)
        P.emit("vector", lambda E: E.memset(zero128[:, :], 0.0), inc="v",
               tag="zero_init")
        ZERO = P.now("v")

        if "dbg_attn" in self.taps:
            self.dbg_attn_s = self.scratch("dbg_attn", [NT, 128, 4096], BF16)
            self.dbg_s_s = self.scratch("dbg_s", [NT, 128, 512], F32)
        P.dma("sync", wot[:, :], self.wot_s[:, :],
              waits=(P.now("d_woS"),), inc="d_wo2", tag="wot_ld")
        WOT = P.now("d_wo2")
        P.dma("sync", selp_sb[:, :], self.cd["selp"][:, :], inc="d_c", tag="c_selp")
        P.dma("sync", mask_sb[:, :], self.cd["mask4"][:, :], inc="d_c", tag="c_mask")
        P.dma("sync", e4_sb[:, :], self.cd["e4"][:, :], inc="d_c", tag="c_e4")
        C1B = P.now("d_c")

        for hf in range(2):
            P.emit("tensor", (lambda E, r=self.bo_sb[:, hf * 512:(hf + 1) * 512]:
                              E.matmul(psWo[:, :], self.ones1[:, :], r,
                                       start=True, stop=True)),
                   waits=(self.CONSTS,) if hf == 0 else (P.now("v"),),
                   inc="t", tag=f"bo_b{hf}")
            P.emit("vector", (lambda E, d=self.bo_rep[:, hf * 512:(hf + 1) * 512]:
                              E.tensor_copy(d, psWo[:, :])),
                   waits=(P.now("t"),), inc="v", tag=f"bo_e{hf}")
        BOREP = P.now("v")
        pswo_free_init = BOREP

        SPILLS = [("d_spq", 16 * 8), ("d_spk", 16 * 8), ("d_spv", 16 * 8)]
        q_free = [Buf(), Buf()]
        k_free = [Buf(), Buf()]
        v_free = [Buf(), Buf()]
        xr_free = [Buf(), Buf()]
        probs_free = {g: Buf() for g in range(2)}
        ps_free = Buf()
        psav_free = [Buf(), Buf()]
        pswo_free = Buf()
        pswo_free.free = pswo_free_init
        pswo2_free = Buf()
        attn_free = Buf()
        out1_free = Buf()
        x1t_free = Buf()

        attn32 = self.sb("attn32", [128, 4096], F32, sc)
        attn32_free = Buf()
        pending_spill = []

        def flush_spill():
            while pending_spill:
                si, sX1 = pending_spill.pop(0)
                P.dma("sync", self.x1_s[si][:, :], x1t[:, :], waits=(sX1,),
                      inc="d_x1w", tag=f"x1sp{si}")
                x1t_free.free = P.now("d_x1w")
        ql_ms = [None, None]

        def emit_q_load(ii):
            qb2 = ii % 2
            P.dma("sync", q_sb[qb2][:, :], self.qf_s[ii][:, :],
                  waits=tuple((SPILLS if ii == 0 else [])
                              + ([q_free[qb2].free] if q_free[qb2].free else [])),
                  inc=f"d_q{qb2}", tag=f"q_ld{ii}")
            ql_ms[qb2] = P.now(f"d_q{qb2}")

        emit_q_load(0)
        for i in range(NT):
            qb = i % 2
            QL = ql_ms[qb]
            XR = None

            pend = []

            def emit_avsums(d):
                g, j, p = d["g"], d["j"], d["p"]
                for bb in range(2):
                    b = 2 * g + bb
                    first = (b == 0)
                    last = (b == 3)
                    for hh in range(2):
                        h = 2 * p + hh
                        w = [d["PRD"], d["VL"]] if (bb == 0 and hh == 0) else []
                        if first and bb == 0 and hh == 0 and d["avf"].free:
                            w.append(d["avf"].free)
                        P.emit("tensor",
                               (lambda E,
                                o=d["av"][64 * hh:64 * hh + 64, :],
                                l=v_sb[d["sl"]][:, b * 1024 + h * 64:
                                                b * 1024 + h * 64 + 64],
                                r=probs[:, d["pcol"] + hh * 1024 + bb * 512:
                                        d["pcol"] + hh * 1024 + bb * 512 + 512],
                                s0=first, s1=last, tp=(0, 64 * hh):
                                E.matmul(o, l, r, start=s0, stop=s1,
                                         tile_position=tp,
                                         skip_group_check=True)),
                               waits=tuple(w), inc="t",
                               tag=f"av{i}_{j}_{p}{g}{hh}{bb}")
                    for hh in range(2):
                        h = 2 * p + hh
                        P.emit("tensor",
                               (lambda E,
                                o=psSums[32 * (h % 4):32 * (h % 4) + 16, :],
                                l=e4_sb[:, (h // 4) * 16:(h // 4) * 16 + 16],
                                r=probs[:, d["pcol"] + hh * 1024 + bb * 512:
                                        d["pcol"] + hh * 1024 + bb * 512 + 512],
                                s1=d["slast"] and bb == 1 and g == 1,
                                tp=(0, 32 * (h % 4)):
                                E.matmul(o, l, r, start=False, stop=s1,
                                         tile_position=tp,
                                         skip_group_check=True)),
                               waits=(), inc="t",
                               tag=f"su{i}_{j}_{p}{g}{hh}{bb}")
                probs_free[d["pslot"]].free = P.now("t")
                if g == 1:
                    AVP = P.now("t")
                    aw = [AVP]
                    if d["j"] == 0 and attn32_free.free and p == 0:
                        aw.append(attn32_free.free)
                    if d["j"] == 0:
                        P.emit("vector",
                               (lambda E, dst=attn32[:, p * 512:(p + 1) * 512],
                                sp=d["av"][:, :]: E.tensor_copy(dst, sp)),
                               waits=tuple(aw), inc="v", tag=f"ac{i}_{j}_{p}")
                    else:
                        P.emit("vector",
                               (lambda E, dst=attn32[:, p * 512:(p + 1) * 512],
                                sp=d["av"][:, :]:
                                E.tensor_tensor(dst, sp, dst, op=ALU.add)),
                               waits=tuple(aw), inc="v", tag=f"ac{i}_{j}_{p}")
                    d["avf"].free = P.now("v")
                    if d["vlast"]:
                        v_free[d["sl"]].free = P.now("t")

            zw = [ZERO, C1B]
            if pswo_free.free:
                zw.append(pswo_free.free)
            P.emit("tensor",
                   (lambda E: E.matmul(psSums[:, :], zero128[:, :],
                                       mask_sb[:, 0:512], start=True,
                                       stop=False, skip_group_check=True)),
                   waits=tuple(zw), inc="t", tag=f"zs{i}")

            for j in range(i + 1):
                sl = j % 2
                P.dma("sync", k_sb[sl][:, :], self.kf_s[j][:, :],
                      waits=tuple((SPILLS if i == 0 else [])
                                  + ([k_free[sl].free] if k_free[sl].free else [])),
                      inc=f"d_k{sl}", tag=f"k_ld{i}_{j}")
                KL = P.now(f"d_k{sl}")
                P.dma("gpsimd", v_sb[sl][:, :], self.vt_s[j][:, :],
                      waits=(v_free[sl].free,) if v_free[sl].free else (),
                      inc=f"d_v{sl}", tag=f"v_ld{i}_{j}")
                VL = P.now(f"d_v{sl}")
                if j == min(1, i):
                    flush_spill()
                    P.dma("sync", xr_sb[qb][:, :].rearrange("p (u d) -> p u d", u=4),
                          self.xin[i * TT:(i + 1) * TT, :].rearrange(
                              "(u p) d -> p u d", p=128),
                          waits=(xr_free[qb].free,) if xr_free[qb].free else (),
                          inc=f"d_xr{qb}", tag=f"xr_ld{i}")
                    XR = P.now(f"d_xr{qb}")
                    if i + 1 < NT:
                        emit_q_load(i + 1)
                diag = (j == i)
                first_of_j = True
                for p in range(8):
                    av = psAV[p % 2]
                    avf = psav_free[p % 2]
                    for g in range(2):
                        for bb in range(2):
                            b = 2 * g + bb
                            for hh in range(2):
                                w = []
                                if first_of_j:
                                    w = [KL, QL]
                                    first_of_j = False
                                if bb == 0 and hh == 0 and ps_free.free:
                                    w.append(ps_free.free)
                                P.emit("tensor",
                                       (lambda E,
                                        o=psA[:, hh * 1024 + bb * 512:
                                              hh * 1024 + bb * 512 + 512],
                                        l=k_sb[sl][64 * hh:64 * hh + 64,
                                                   p * 512 + b * 128:
                                                   p * 512 + b * 128 + 128],
                                        r=q_sb[qb][64 * hh:64 * hh + 64,
                                                   p * 512:(p + 1) * 512]:
                                        E.matmul(o, l, r, start=True, stop=True)),
                                       waits=tuple(w), inc="t",
                                       tag=f"sc{i}_{j}_{p}{g}{hh}{bb}")
                        SCM = P.now("t")
                        pcol = g * 2048
                        ew = [SCM]
                        if probs_free[g].free:
                            ew.append(probs_free[g].free)
                        if i == 0 and j == 0 and p == 0:
                            ew.append(P.now("v"))
                        P.emit("scalar",
                               (lambda E, o=probs[:, pcol:pcol + 2048],
                                bv=self.bias_pr[:, p:p + 1]:
                                E.activation(o, psA[:, :], AF.Exp, bias=bv,
                                             scale=0.125)),
                               waits=tuple(ew), inc="s",
                               tag=f"ex{i}_{j}_{p}{g}")
                        EX = P.now("s")
                        ps_free.free = EX
                        if diag:
                            for hh in range(2):
                                for bb in range(2):
                                    b = 2 * g + bb
                                    P.emit("gpsimd",
                                           (lambda E,
                                            o=probs[:, pcol + hh * 1024 + bb * 512:
                                                    pcol + hh * 1024 + bb * 512 + 512],
                                            mk=mask_sb[:, b * 512:(b + 1) * 512]:
                                            E.tensor_mul(o, o, mk)),
                                           waits=(EX, C1B) if (hh == 0 and bb == 0) else (),
                                           inc="g", tag=f"mk{i}_{p}{g}{hh}{bb}")
                            PRD = P.now("g")
                        else:
                            PRD = EX
                        pend.append(dict(p=p, g=g, pcol=pcol, PRD=PRD, VL=VL,
                                         sl=sl, j=j, av=av, avf=avf, pslot=g,
                                         slast=(j == i), vlast=(p == 7)))
                        if len(pend) > 1:
                            emit_avsums(pend.pop(0))
                k_free[sl].free = P.now("t")
            while pend:
                emit_avsums(pend.pop(0))
            AVD = P.now("t")
            AVV = P.now("v")
            P.emit("vector", lambda E: E.tensor_scalar_max(
                Ssb[:, :], psSums[:, :], 1e-30),
                waits=(AVD,), inc="v", tag=f"S{i}")
            P.emit("vector", lambda E: E.reciprocal(recS[:, :], Ssb[:, :]),
                   waits=(P.now("v"),), inc="v", tag=f"Sr{i}")
            SREC = P.now("v")
            for p in range(8):
                w = [SREC]
                if pswo_free.free:
                    w.append(pswo_free.free)
                P.emit("tensor",
                       (lambda E, l=selp_sb[:, p * 128:(p + 1) * 128]:
                        E.matmul(psWo[:, :], l, recS[:, :],
                                 start=True, stop=True)),
                       waits=tuple(w), inc="t", tag=f"bc{i}_{p}")
                BC = P.now("t")
                aw = [BC, AVV]
                if attn_free.free and p == 0:
                    aw.append(attn_free.free)
                P.emit("vector",
                       (lambda E, o=attn[:, p * 512:(p + 1) * 512],
                        a32=attn32[:, p * 512:(p + 1) * 512]:
                        E.tensor_mul(o, a32, psWo[:, :])),
                       waits=tuple(aw), inc="v", tag=f"an{i}_{p}")
                pswo_free.free = P.now("v")
            attn32_free.free = P.now("v")
            ATTN = P.now("v")
            q_free[qb].free = P.now("t")

            if "dbg_attn" in self.taps:
                P.dma("sync", self.dbg_attn_s[i][:, :], attn[:, :],
                      waits=(ATTN,), inc="d_dbg", tag=f"dbga{i}")
                P.dma("sync", self.dbg_s_s[i][:, :], Ssb[:, :],
                      waits=(ATTN,), inc="d_dbg", tag=f"dbgs{i}")

            ww = [ATTN, WOT]
            if out1_free.free:
                ww.append(out1_free.free)
            for u in range(NSUB):
                for oh in range(2):
                    k2 = (u * 2 + oh) % 2
                    pwo = (psWo, psWo2)[k2]
                    pwf = (pswo_free, pswo2_free)[k2]
                    for dc in range(8):
                        w = list(ww) if (u == 0 and oh == 0 and dc == 0) else []
                        if dc == 0 and pwf.free:
                            w.append(pwf.free)
                        P.emit("tensor",
                               (lambda E, o=pwo[:, :],
                                l=attn[:, dc * 512 + u * 128:
                                        dc * 512 + u * 128 + 128],
                                r=wot[:, dc * 1024 + oh * 512:
                                      dc * 1024 + oh * 512 + 512],
                                s0=(dc == 0), s1=(dc == 7):
                                E.matmul(o, l, r, start=s0, stop=s1)),
                               waits=tuple(w), inc="t" if dc == 7 else None,
                               tag=f"wo{i}_{u}{oh}{dc}")
                    ww = []
                    P.emit("vector",
                           (lambda E, o=out1[:, u * 1024 + oh * 512:
                                             u * 1024 + oh * 512 + 512],
                            sp=pwo[:, :],
                            bo=self.bo_rep[:, oh * 512:(oh + 1) * 512]:
                            E.tensor_add(o, sp, bo)),
                           waits=(P.now("t"), BOREP), inc="v",
                           tag=f"woe{i}_{u}{oh}")
                    pwf.free = P.now("v")
            O1 = P.now("v")
            attn_free.free = P.now("t")

            LN2 = self.ln_stats(lambda u: out1[:, u * 1024:(u + 1) * 1024],
                                O1, f"ln2_{i}")
            xw = [LN2, XR]
            if x1t_free.free:
                xw.append(x1t_free.free)
            for u in range(NSUB):
                P.emit("vector",
                       (lambda E, o=x1t[:, u * 1024:(u + 1) * 1024],
                        i_=out1[:, u * 1024:(u + 1) * 1024],
                        m=self.stats[:, 8 + u:9 + u],
                        r=self.stats[:, 20 + u:21 + u]:
                        E.tensor_scalar(o, i_, m, r, op0=ALU.subtract,
                                        op1=ALU.mult)),
                       waits=tuple(xw) if u == 0 else (), inc="v",
                       tag=f"ln2n_{i}_{u}")
                P.emit("vector",
                       (lambda E, o=x1t[:, u * 1024:(u + 1) * 1024],
                        xs=xr_sb[qb][:, u * 1024:(u + 1) * 1024]:
                        E.tensor_add(o, o, xs)),
                       waits=(P.now("v"),), inc="v", tag=f"ln2a_{i}_{u}")
            X1 = P.now("v")
            self.stats_free.free = X1
            out1_free.free = X1
            xr_free[qb].free = X1
            pending_spill.append((i, X1))
        flush_spill()

    def phase_2(self, sc):
        nc, P = self.nc, self.P
        x1_sb = [self.sb(f"x1_sb{j}", [128, 4096], F32, sc) for j in range(2)]
        xn2_tok = self.sb("xn2_tok", [128, 4096], BF16, sc)
        xn2_f = [self.sb(f"xn2_f{j}", [128, 4096], BF16, sc) for j in range(2)]
        rwt = self.sb("rwt_sb", [128, 128], BF16, sc)
        selr_sb = self.sb("selr_sb", [16, 2048], BF16, sc)
        wu = [self.sb(f"wu{j}", [128, 4096], BF16, sc) for j in range(4)]
        wd = [self.sb(f"wd{j}", [128, 4096], BF16, sc) for j in range(4)]
        actg = [self.sb(f"actg{j}", [128, 2048], BF16, sc) for j in range(4)]
        acc = self.sb("acc", [128, 4096], F32, sc)
        outt = self.sb("outt", [128, 4096], F32, sc)
        G_sb = [self.sb(f"G_sb{j}", [128, 512], BF16, sc) for j in range(2)]
        pr_sb = self.sb("pr_sb", [128, 16], F32, sc)
        wk_sb = self.sb("wk_sb", [128, 16], F32, sc)
        mk_sb = self.sb("mk_sb", [128, 16], F32, sc)
        g16 = self.sb("g16", [128, 16], BF16, sc)
        gateT = [self.sb(f"gateT{j}", [16, 512], BF16, sc) for j in range(2)]
        rsum = self.sb("rsum", [128, 4], F32, sc)
        psUp = [self.ps(f"psUp{j}", [128, 512], F32, sc) for j in range(2)]
        psDn = [self.ps(f"psDn{j}", [128, 512], F32, sc) for j in range(2)]
        psR = self.ps("psR", [128, 16], F32, sc)
        psM = self.ps("psM", [128, 512], F32, sc)
        psL4 = self.ps("psL4", [128, 1024], F32, sc)

        P.dma("sync", rwt[:, :], self.rwt_s[:, :], waits=(P.now("d_rwS"),),
              inc="d_c", tag="rwt_ld")
        P.dma("sync", selr_sb[:, :], self.cd["selr"][:, :], inc="d_c",
              tag="c_selr")
        C2 = P.now("d_c")
        FW = [P.now("d_fwst0"), P.now("d_fwst1")]

        x1_free = [Buf(), Buf()]
        xn2t_free = Buf()
        xn2f_free = [Buf(), Buf()]
        wslot_free = [Buf() for _ in range(4)]
        actg_free = [Buf() for _ in range(4)]
        psup_free = [Buf(), Buf()]
        psdn_free = [Buf(), Buf()]
        psr_free = Buf()
        psm_free = Buf()
        psl4_free = Buf()
        g_free = [Buf(), Buf()]
        gateT_free = [Buf(), Buf()]
        acc_free = Buf()
        outt_free = Buf()
        xf_ms = [None, None]
        gatet_ms = [None, None]
        xn2_ms = [None, None]

        passes = []
        k0 = 0
        for n in PASSES:
            passes.append(list(range(NEXP))[k0:k0 + n])
            k0 += n

        def emit_prep_stats(ip):
            bi = ip % 2
            P.dma("sync", x1_sb[bi][:, :], self.x1_s[ip][:, :],
                  waits=tuple(x for x in (("d_x1w", 16 * (ip + 1)),
                                          x1_free[bi].free) if x),
                  inc=f"d_x1r{bi}", tag=f"x1_ld{ip}")
            XL = P.now(f"d_x1r{bi}")
            LN3 = self.ln_stats(lambda u: x1_sb[bi][:, u * 1024:(u + 1) * 1024],
                                XL, f"ln3_{ip}")
            xw = [LN3]
            if xn2t_free.free:
                xw.append(xn2t_free.free)
            for u in range(NSUB):
                P.emit("scalar",
                       (lambda E, o=xn2_tok[:, u * 1024:(u + 1) * 1024],
                        i_=x1_sb[bi][:, u * 1024:(u + 1) * 1024],
                        scv=self.stats[:, 20 + u:21 + u],
                        bv=self.stats[:, 24 + u:25 + u]:
                        E.activation(o, i_, AF.Identity, bias=bv, scale=scv)),
                       waits=tuple(xw) if u == 0 else (), inc="s",
                       tag=f"ln3n_{ip}_{u}")
            XN2 = P.now("s")
            self.stats_free.free = XN2
            xn2_ms[bi] = XN2

        def emit_prep_rest(ip):
            bi = ip % 2
            XN2 = xn2_ms[bi]
            tw = [XN2]
            if xn2f_free[bi].free:
                tw.append(xn2f_free[bi].free)
            for c in range(8):
                k = c % 2
                pw = psUp[k]
                for u in range(NSUB):
                    w = list(tw) if u == 0 else []
                    if u == 0 and psup_free[k].free:
                        w.append(psup_free[k].free)
                    P.emit("tensor",
                           (lambda E, o=pw.bitcast(BF16)[:, u * 128:u * 128 + 128],
                            i_=xn2_tok[:, u * 1024 + c * 128:u * 1024 + c * 128 + 128]:
                            E.transpose(o, i_, self.identb[:, :])),
                           waits=tuple(w), inc="t" if u == 3 else None,
                           tag=f"x2t{ip}_{c}{u}")
                tw = []
                P.emit("vector",
                       (lambda E, d=xn2_f[bi][:, c * 512:(c + 1) * 512],
                        sp=pw.bitcast(BF16)[:, 0:512]: E.tensor_copy(d, sp)),
                       waits=(P.now("t"),), inc="v", tag=f"x2e{ip}_{c}")
                psup_free[k].free = P.now("v")
            XF = P.now("v")
            xn2t_free.free = P.now("t")
            xf_ms[bi] = XF

            for u in range(NSUB):
                for dc in range(8):
                    w = [XF, C2] if dc == 0 else []
                    if dc == 0 and psr_free.free:
                        w.append(psr_free.free)
                    P.emit("tensor",
                           (lambda E, l=xn2_f[bi][:, dc * 512 + u * 128:
                                                  dc * 512 + u * 128 + 128],
                            r=rwt[:, dc * 16:(dc + 1) * 16],
                            s0=(dc == 0), s1=(dc == 7):
                            E.matmul(psR[:, :], l, r, start=s0, stop=s1)),
                           waits=tuple(w), inc="t" if dc == 7 else None,
                           tag=f"rt{ip}_{u}{dc}")
                RT = P.now("t")
                P.emit("vector", lambda E: E.reduce_max(self.nw1[:, 0:1],
                                                        psR[:, :], axis=AX),
                       waits=(RT,), inc="v", tag=f"rm{ip}_{u}")
                P.emit("vector", lambda E: E.tensor_scalar_mul(
                    self.nw1[:, 1:2], self.nw1[:, 0:1], -1.0),
                    waits=(P.now("v"),), inc="v", tag=f"rmn{ip}_{u}")
                NM = P.now("v")
                P.emit("scalar",
                       (lambda E, a=rsum[:, u:u + 1]:
                        E.activation(pr_sb[:, :], psR[:, :], AF.Exp,
                                     bias=self.nw1[:, 1:2], accum_out=a)),
                       waits=(NM,), inc="s", tag=f"re{ip}_{u}")
                psr_free.free = P.now("s")
                P.emit("vector", (lambda E, a=rsum[:, u:u + 1]:
                                  E.reciprocal(self.nw1[:, 2:3], a)),
                       waits=(P.now("s"),), inc="v", tag=f"rrc{ip}_{u}")
                P.emit("vector", lambda E: E.tensor_scalar_mul(
                    pr_sb[:, :], pr_sb[:, :], self.nw1[:, 2:3]),
                    waits=(P.now("v"),), inc="v", tag=f"rn{ip}_{u}")
                P.emit("vector", lambda E: E.tensor_copy(wk_sb[:, :], pr_sb[:, :]),
                       waits=(P.now("v"),), inc="v", tag=f"tk0{ip}_{u}")
                for it in range(3):
                    P.emit("vector", lambda E: E.reduce_max(self.nw1[:, 3:4],
                                                            wk_sb[:, :], axis=AX),
                           waits=(P.now("v"),), inc="v", tag=f"tkm{ip}_{u}{it}")
                    P.emit("vector", lambda E: E.tensor_scalar(
                        mk_sb[:, :], wk_sb[:, :], self.nw1[:, 3:4], -1e9,
                        op0=ALU.is_ge, op1=ALU.mult),
                        waits=(P.now("v"),), inc="v", tag=f"tka{ip}_{u}{it}")
                    P.emit("vector", lambda E: E.tensor_add(
                        wk_sb[:, :], wk_sb[:, :], mk_sb[:, :]),
                        waits=(P.now("v"),), inc="v", tag=f"tkb{ip}_{u}{it}")
                P.emit("vector", lambda E: E.reduce_max(self.nw1[:, 3:4],
                                                        wk_sb[:, :], axis=AX),
                       waits=(P.now("v"),), inc="v", tag=f"tkt{ip}_{u}")
                P.emit("vector", lambda E: E.tensor_scalar(
                    mk_sb[:, :], pr_sb[:, :], self.nw1[:, 3:4], None,
                    op0=ALU.is_ge), waits=(P.now("v"),), inc="v",
                    tag=f"tkc{ip}_{u}")
                P.emit("vector", lambda E: E.tensor_mul(
                    g16[:, :], pr_sb[:, :], mk_sb[:, :]),
                    waits=(P.now("v"),), inc="v", tag=f"tkg{ip}_{u}")
                GT = P.now("v")
                w = [GT]
                if u == 0 and psm_free.free:
                    w.append(psm_free.free)
                if u == 0 and gateT_free[bi].free:
                    w.append(gateT_free[bi].free)
                P.emit("tensor",
                       (lambda E, o=psM.bitcast(BF16)[0:16, u * 256:u * 256 + 128]:
                        E.transpose(o, g16[:, :], self.identb[:, :])),
                       waits=tuple(w), inc="t", tag=f"gt{ip}_{u}")
            for u in range(NSUB):
                P.emit("vector",
                       (lambda E, d=gateT[bi][:, u * 128:(u + 1) * 128],
                        sp=psM.bitcast(BF16)[0:16, u * 256:u * 256 + 128]:
                        E.tensor_copy(d, sp)),
                       waits=(P.now("t"),) if u == 0 else (), inc="v",
                       tag=f"gte{ip}_{u}")
            gatet_ms[bi] = P.now("v")
            psm_free.free = P.now("v")

        emit_prep_stats(0)
        emit_prep_rest(0)
        for i in range(self.nt2):
            bi = i % 2
            XF = xf_ms[bi]
            GATET = gatet_ms[bi]
            xnf = xn2_f[bi]
            gT = gateT[bi]

            pass_first = True
            for pi_, pl in enumerate(passes):
                for ei_, e in enumerate(pl):
                    slot = (i * NEXP + sum(PASSES[:pi_]) + ei_) % 4
                    P.dma("sync", wu[slot][:, :], self.utp_s[e][:, :],
                          waits=tuple((FW if (i == 0 and pi_ == 0 and ei_ == 0) else [])
                                      + ([wslot_free[slot].free] if wslot_free[slot].free else [])),
                          inc=f"d_fw{slot}", tag=f"fu_ld{i}_{e}")
                    P.dma("sync", wd[slot][:, :], self.dtp_s[e][:, :],
                          inc=f"d_fw{slot}", tag=f"fd_ld{i}_{e}")
                    WL = P.now(f"d_fw{slot}")
                    aslot = slot
                    if e < 16:
                        w = [GATET, C2]
                        if psm_free.free:
                            w.append(psm_free.free)
                        P.emit("tensor",
                               (lambda E, l=selr_sb[:, e * 128:(e + 1) * 128],
                                r=gT[:, :]:
                                E.matmul(psM[:, :], l, r, start=True, stop=True)),
                               waits=tuple(w), inc="t", tag=f"gb{i}_{e}")
                        gslot = e % 2
                        gw = [P.now("t")]
                        if g_free[gslot].free:
                            gw.append(g_free[gslot].free)
                        P.emit("vector", (lambda E, d=G_sb[e % 2][:, :]:
                                          E.tensor_copy(d, psM[:, :])),
                               waits=tuple(gw), inc="v", tag=f"gbe{i}_{e}")
                        psm_free.free = P.now("v")
                        GB = P.now("v")
                    else:
                        GB = None
                    for ec in range(4):
                        k = ec % 2
                        pw = psUp[k]
                        for dc in range(8):
                            w = [WL, XF] if (ec == 0 and dc == 0) else []
                            if dc == 0 and psup_free[k].free:
                                w.append(psup_free[k].free)
                            P.emit("tensor",
                                   (lambda E, o=pw[:, :],
                                    l=wu[slot][:, dc * 512 + ec * 128:
                                               dc * 512 + ec * 128 + 128],
                                    r=xnf[:, dc * 512:(dc + 1) * 512],
                                    s0=(dc == 0), s1=(dc == 7):
                                    E.matmul(o, l, r, start=s0, stop=s1)),
                                   waits=tuple(w), inc="t" if dc == 7 else None,
                                   tag=f"up{i}_{e}_{ec}{dc}")
                        UPM = P.now("t")
                        gw = [UPM]
                        if ec == 0 and actg_free[aslot].free:
                            gw.append(actg_free[aslot].free)
                        P.emit("scalar",
                               (lambda E, o=actg[aslot][:, ec * 512:(ec + 1) * 512],
                                sp=pw[:, :]: E.activation(o, sp, AF.Gelu)),
                               waits=tuple(gw), inc="s", tag=f"gl{i}_{e}_{ec}")
                        psup_free[k].free = P.now("s")
                        if e < 16:
                            P.emit("gpsimd",
                                   (lambda E, o=actg[aslot][:, ec * 512:(ec + 1) * 512],
                                    gs=G_sb[e % 2][:, :]:
                                    E.tensor_mul(o, o, gs)),
                                   waits=(P.now("s"), GB), inc="g",
                                   tag=f"gm{i}_{e}_{ec}")
                    ACT = P.now("g") if e < 16 else P.now("s")
                    if e < 16:
                        g_free[e % 2].free = ACT
                    actg_free[aslot].ready = ACT
                for c in range(8):
                    k = c % 2
                    pd = psDn[k]
                    nel = len(pl)
                    for ei_, e in enumerate(pl):
                        slot = (i * NEXP + sum(PASSES[:pi_]) + ei_) % 4
                        for ec in range(4):
                            first = (ei_ == 0 and ec == 0)
                            last = (ei_ == nel - 1 and ec == 3)
                            w = []
                            if ec == 0 and c == 0:
                                w.append(actg_free[slot].ready)
                            if first and psdn_free[k].free:
                                w.append(psdn_free[k].free)
                            P.emit("tensor",
                                   (lambda E, o=pd[:, :],
                                    l=wd[slot][:, ec * 1024 + c * 128:
                                               ec * 1024 + c * 128 + 128],
                                    r=actg[slot][:, ec * 512:(ec + 1) * 512],
                                    s0=first, s1=last:
                                    E.matmul(o, l, r, start=s0, stop=s1)),
                                   waits=tuple(w), inc="t" if last else None,
                                   tag=f"dn{i}_{pi_}_{c}{e}{ec}")
                    DN = P.now("t")
                    aw = [DN]
                    if pass_first and acc_free.free and c == 0:
                        aw.append(acc_free.free)
                    if pass_first:
                        P.emit("vector",
                               (lambda E, d=acc[:, c * 512:(c + 1) * 512],
                                sp=pd[:, :]: E.tensor_copy(d, sp)),
                               waits=tuple(aw), inc="v", tag=f"dne{i}_{pi_}_{c}")
                    else:
                        P.emit("vector",
                               (lambda E, d=acc[:, c * 512:(c + 1) * 512],
                                sp=pd[:, :]: E.tensor_tensor(d, sp, d, op=ALU.add)),
                               waits=tuple(aw), inc="v", tag=f"dna{i}_{pi_}_{c}")
                    psdn_free[k].free = P.now("v")
                if pass_first and i + 1 < self.nt2:
                    emit_prep_stats(i + 1)
                pass_first = False
                for ei_, e in enumerate(pl):
                    slot = (i * NEXP + sum(PASSES[:pi_]) + ei_) % 4
                    wslot_free[slot].free = P.now("t")
                    actg_free[slot].free = P.now("t")
            ACC = P.now("v")
            xn2f_free[bi].free = P.now("t")
            gateT_free[bi].free = P.now("t")

            if i + 1 < self.nt2:
                emit_prep_rest(i + 1)

            for u in range(NSUB):
                tw = [ACC] if u == 0 else []
                if psl4_free.free:
                    tw.append(psl4_free.free)
                for c in range(8):
                    w = list(tw) if c == 0 else []
                    P.emit("tensor",
                           (lambda E, o=psL4[:, c * 128:(c + 1) * 128],
                            i_=acc[:, c * 512 + u * 128:c * 512 + u * 128 + 128]:
                            E.transpose(o, i_, self.identf[:, :])),
                           waits=tuple(w), inc="t" if c == 7 else None,
                           tag=f"l4t{i}_{u}{c}")
                    tw = []
                L4T = P.now("t")
                ew = [L4T]
                if u == 0 and outt_free.free:
                    ew.append(outt_free.free)
                P.emit("vector",
                       (lambda E, o=outt[:, u * 1024:(u + 1) * 1024]:
                        E.tensor_copy(o, psL4[:, :])),
                       waits=tuple(ew), inc="v", tag=f"l4e{i}_{u}")
                psl4_free.free = P.now("v")
            LN4 = self.ln_stats(lambda u: outt[:, u * 1024:(u + 1) * 1024],
                                P.now("v"), f"ln4_{i}")
            for u in range(NSUB):
                P.emit("vector",
                       (lambda E, o=outt[:, u * 1024:(u + 1) * 1024],
                        m=self.stats[:, 8 + u:9 + u],
                        r=self.stats[:, 20 + u:21 + u]:
                        E.tensor_scalar(o, o, m, r,
                                        op0=ALU.subtract, op1=ALU.mult)),
                       waits=(LN4,) if u == 0 else (), inc="v",
                       tag=f"l4n{i}_{u}")
                P.emit("vector",
                       (lambda E, o=outt[:, u * 1024:(u + 1) * 1024],
                        xs=x1_sb[bi][:, u * 1024:(u + 1) * 1024]:
                        E.tensor_add(o, o, xs)),
                       waits=(P.now("v"),), inc="v", tag=f"l4a{i}_{u}")
            self.stats_free.free = P.now("v")
            OUT = P.now("v")
            x1_free[bi].free = OUT
            acc_free.free = P.now("t")
            P.dma("gpsimd", self.out_d[i * TT:(i + 1) * TT, :].rearrange(
                "(u p) d -> p u d", p=128),
                outt[:, :].rearrange("p (u d) -> p u d", u=4),
                waits=(OUT,), inc="d_o", tag=f"out{i}")
            outt_free.free = P.now("d_o")



_BUILD_CACHE = {}


def finalize_program(b):
    nc, P = b.nc, b.P
    with ExitStack() as st:
        for name in sorted(P.cnt):
            P.sem_handles[name] = st.enter_context(nc.semaphore(name))
        with nc.Block() as block:
            for eng in ENGS:
                ops = P.ops[eng]

                def runner(E, ops=ops):
                    for op in ops:
                        op(E)

                getattr(block, eng)(runner)
    b.finalized = True
    return b


def get_build(taps=(), phases=("W", "1a", "1b", "2")):
    key = (tuple(sorted(taps)), tuple(phases))
    if key not in _BUILD_CACHE:
        b = Build(taps=taps, phases=phases)
        b.build()
        finalize_program(b)
        _BUILD_CACHE[key] = b
    return _BUILD_CACHE[key]


def make_in_maps(inputs, consts):
    perm = _qk_perm()
    wq = np.ascontiguousarray(np.asarray(inputs["L0_attn_W_q"])[perm])
    wk = np.ascontiguousarray(np.asarray(inputs["L0_attn_W_k"])[perm])
    bq = np.asarray(inputs["L0_attn_b_q"])[perm]
    bk = np.asarray(inputs["L0_attn_b_k"])[perm]
    bv = np.asarray(inputs["L0_attn_b_v"])
    bqkv = np.zeros((128, 24), np.float32)
    for pi, b in enumerate((bq, bk, bv)):
        for oc in range(8):
            bqkv[:, pi * 8 + oc] = b[oc * 128:(oc + 1) * 128]
    shared = {
        "wq": wq, "wk": wk,
        "wv": np.asarray(inputs["L0_attn_W_v"]),
        "wo": np.asarray(inputs["L0_attn_W_o"]),
        "bqkv": bqkv,
        "bo_in": np.asarray(inputs["L0_attn_b_o"]).reshape(1, D),
        "rw_in": np.asarray(inputs["L0_ffn_router_w"]),
        "sup": np.asarray(inputs["L0_ffn_shared_up"]),
        "sdn": np.asarray(inputs["L0_ffn_shared_down"]),
        "rup": np.asarray(inputs["L0_ffn_routed_up"]),
        "rdn": np.asarray(inputs["L0_ffn_routed_down"]),
    }
    shared.update(consts)
    x = np.asarray(inputs["L0_attn_x"])
    maps = []
    for c in range(8):
        m = dict(shared)
        m["xin"] = np.ascontiguousarray(x[c])
        maps.append(m)
    return maps


def run_device(b, in_maps, core_ids):
    if not getattr(b, "finalized", False):
        finalize_program(b)
    return run_bass_kernel_spmd(b.nc, in_maps, core_ids=core_ids)


def kernel(**inputs):
    consts = host_constants()
    b = get_build()
    in_maps = make_in_maps(inputs, consts)
    res = run_device(b, in_maps, core_ids=list(range(8)))
    return np.stack([np.asarray(r["out"]) for r in res.results], axis=0)


# version 21 (final)
# speedup vs baseline: 1.2332x; 1.0364x over previous
import numpy as np
import ml_dtypes
from contextlib import ExitStack

import concourse.bass as bass
import concourse.mybir as mybir
from concourse.bass_utils import run_bass_kernel_spmd

F32 = mybir.dt.float32
BF16 = mybir.dt.bfloat16
I32 = mybir.dt.int32
AF = mybir.ActivationFunctionType
ALU = mybir.AluOpType
AX = mybir.AxisListType.X
BF = ml_dtypes.bfloat16

D = 1024
T = 4096
H = 16
NT = 8
TT = 512
NSUB = 4
NEXP = 18
PASSES = (4, 4, 4, 4, 2)
ENGS = ("sync", "tensor", "vector", "scalar", "gpsimd")



def _qk_perm():
    p64 = np.concatenate([np.arange(0, 64, 2), np.arange(1, 64, 2)])
    return np.concatenate([h * 64 + p64 for h in range(H)])


def host_constants():
    c = {}
    theta = 1.0 / 10000.0 ** (np.arange(32, dtype=np.float64) / 32.0)
    ang = np.arange(T, dtype=np.float64)[None, :] * theta[:, None]
    cos, sin = np.cos(ang), np.sin(ang)
    C = np.zeros((128, T), np.float64)
    S = np.zeros((128, T), np.float64)
    for blk in (0, 64):
        C[blk:blk + 32] = cos
        C[blk + 32:blk + 64] = cos
        S[blk:blk + 32] = -sin
        S[blk + 32:blk + 64] = sin
    c["crep"] = C.astype(BF)
    c["srep"] = S.astype(BF)
    P = np.zeros((128, 128), np.float32)
    for blk in (0, 64):
        for j in range(32):
            P[blk + j, blk + 32 + j] = 1.0
            P[blk + 32 + j, blk + j] = 1.0
    c["pswap"] = P.astype(BF)
    c["identb"] = np.eye(128, dtype=np.float32).astype(BF)
    c["identf"] = np.eye(128, dtype=np.float32)
    hs = np.zeros((128, 128), np.float32)
    for oc in range(8):
        hs[0:64, oc * 16 + 2 * oc] = 1.0
        hs[64:128, oc * 16 + 2 * oc + 1] = 1.0
    c["hsall"] = hs.astype(BF)
    e4 = np.zeros((128, 64), np.float32)
    for r in range(4):
        e4[:, r * 16 + r] = 1.0
    c["e4"] = e4.astype(BF)
    selp = np.zeros((128, 8 * 128), np.float32)
    for p in range(8):
        for i in range(128):
            h = 2 * p + (1 if i >= 64 else 0)
            selp[32 * (h % 4) + h // 4, p * 128 + i] = 1.0
    c["selp"] = selp
    m = np.zeros((128, 4 * 512), np.float32)
    for b in range(4):
        s = np.arange(128)[:, None] + 128 * b
        t = np.arange(512)[None, :]
        m[:, b * 512:(b + 1) * 512] = (s <= t)
    c["mask4"] = m.astype(BF)
    selr = np.zeros((16, 16 * 128), np.float32)
    for r in range(16):
        selr[r, r * 128:(r + 1) * 128] = 1.0
    c["selr"] = selr.astype(BF)
    c["ones1"] = np.ones((1, 128), np.float32)
    return c



class Buf:
    __slots__ = ("free", "ready")

    def __init__(self):
        self.free = None
        self.ready = None


class Planner:
    def __init__(self):
        self.ops = {e: [] for e in ENGS}
        self.cnt = {}
        self.sem_handles = {}
        self._gate = {e: None for e in ENGS}

    def sem(self, name):
        self.cnt.setdefault(name, 0)
        return name

    def now(self, sem):
        return (sem, self.cnt.get(sem, 0))

    def gate_all(self):
        snap = tuple((s, v) for s, v in self.cnt.items() if v > 0)
        for e in ENGS:
            self._gate[e] = snap

    def emit(self, eng, build, waits=(), inc=None, inc_amt=1, tag=""):
        waits = [w for w in waits if w is not None]
        for s, _ in waits:
            self.sem(s)
        if self._gate[eng]:
            waits = list(self._gate[eng]) + list(waits)
            self._gate[eng] = None
        val = None
        if inc is not None:
            self.sem(inc)
            self.cnt[inc] += inc_amt
            val = self.cnt[inc]
        waits = tuple((s, v) for (s, v) in waits if v > 0)

        def closure(E, waits=waits, build=build, inc=inc, inc_amt=inc_amt):
            for s, v in waits:
                E.wait_ge(self.sem_handles[s], v)
            ins = build(E)
            if inc is not None:
                if ins is None:
                    ins = E.nop()
                ins.then_inc(self.sem_handles[inc], inc_amt)
            return ins

        closure.tag = tag
        closure.waits = waits
        closure.inc = (inc, val, inc_amt) if inc else None
        self.ops[eng].append(closure)
        return self.now(inc) if inc else None

    def dma(self, eng, out_ap, in_ap, waits=(), inc=None, tag="", **kw):
        return self.emit(eng, lambda E: E.dma_start(out_ap, in_ap, **kw),
                         waits=waits, inc=inc, inc_amt=16, tag=tag)

    def simulate(self):
        idx = {e: 0 for e in ENGS}
        sem = {s: 0 for s in self.cnt}
        progress = True
        while progress:
            progress = False
            for e in ENGS:
                while idx[e] < len(self.ops[e]):
                    op = self.ops[e][idx[e]]
                    if all(sem[s] >= v for s, v in op.waits):
                        if op.inc:
                            sem[op.inc[0]] += op.inc[2]
                        idx[e] += 1
                        progress = True
                    else:
                        break
        stuck = {e: idx[e] for e in ENGS if idx[e] < len(self.ops[e])}
        if stuck:
            lines = []
            for e, i in stuck.items():
                op = self.ops[e][i]
                miss = [(s, v, sem.get(s, 0)) for s, v in op.waits
                        if sem.get(s, 0) < v]
                lines.append(f"  {e}[{i}/{len(self.ops[e])}] tag={op.tag} missing={miss}")
            raise RuntimeError("planned-program deadlock:\n" + "\n".join(lines))

    def n_ops(self):
        return {e: len(self.ops[e]) for e in ENGS}



class Build:
    def __init__(self, taps=(), phases=("W", "1a", "1b", "2"), nt2=NT,
                 p2parts=("router", "experts", "ln4")):
        self.taps = set(taps)
        self.phases = phases
        self.nt2 = nt2
        self.p2parts = set(p2parts)
        self.nc = bass.Bass("TRN2", target_bir_lowering=False)
        self.P = Planner()
        self.st = ExitStack()

    def sb(self, name, shape, dtype, scope=None):
        return (scope or self.st).enter_context(
            self.nc.sbuf_tensor(name, list(shape), dtype))

    def ps(self, name, shape, dtype=F32, scope=None):
        return (scope or self.st).enter_context(
            self.nc.psum_tensor(name, list(shape), dtype))

    def dram(self, name, shape, dtype, kind="ExternalInput"):
        return self.nc.dram_tensor(name, list(shape), dtype, kind=kind)

    def scratch(self, name, shape, dtype):
        kind = "ExternalOutput" if name in self.taps else "Internal"
        return self.nc.dram_tensor(name, list(shape), dtype, kind=kind)

    def newton_rsqrt(self, dst, src, pre=(), tag="nwt"):
        P = self.P
        p = src.shape[0]
        n = src.shape[1]
        t1 = self.nw1[0:p, 0:n]
        t2 = self.nw2[0:p, 0:n]
        th = self.nw3[0:p, 0:n]
        P.emit("vector", lambda E: E.memset(t1.bitcast(I32), 0x5F3759DF),
               waits=pre, inc="v", tag=tag + "_m")
        P.emit("vector", lambda E: E.tensor_scalar(
            t2.bitcast(I32), src.bitcast(I32), 1, None,
            op0=ALU.logical_shift_right), waits=(P.now("v"),), inc="v",
            tag=tag + "_sh")
        P.emit("vector", lambda E: E.tensor_tensor(
            dst.bitcast(I32), t1.bitcast(I32), t2.bitcast(I32),
            op=ALU.subtract), waits=(P.now("v"),), inc="v", tag=tag + "_y0")
        P.emit("vector", lambda E: E.tensor_scalar_mul(th, src, 0.5),
               waits=(P.now("v"),), inc="v", tag=tag + "_h")
        for it in range(3):
            P.emit("vector", lambda E: E.tensor_mul(t2, dst, dst),
                   waits=(P.now("v"),), inc="v", tag=tag + f"_a{it}")
            P.emit("vector", lambda E: E.tensor_mul(t2, th, t2),
                   waits=(P.now("v"),), inc="v", tag=tag + f"_b{it}")
            P.emit("vector", lambda E: E.tensor_scalar(
                t2, t2, -1.0, 1.5, op0=ALU.mult, op1=ALU.add),
                waits=(P.now("v"),), inc="v", tag=tag + f"_c{it}")
            P.emit("vector", lambda E: E.tensor_mul(dst, dst, t2),
                   waits=(P.now("v"),), inc="v", tag=tag + f"_d{it}")
        return P.now("v")

    def ln_stats(self, src_fn, xwait, tag, n=NSUB, extra_s2_wait=(),
                 serialize_ps=False):
        P = self.P
        stats = self.stats
        sw = [xwait] + list(extra_s2_wait)
        if self.stats_free.free:
            sw.append(self.stats_free.free)
        for u in range(n):
            P.emit("vector", (lambda E, o=stats[:, u:u + 1], i_=src_fn(u):
                              E.reduce_sum(o, i_, axis=AX)),
                   waits=tuple(sw) if u == 0 else (), inc="v",
                   tag=f"{tag}_s1_{u}")
            s2w = [P.now("s")] if u > 0 else list(sw)
            if serialize_ps:
                s2w.append(P.now("v"))
            P.emit("scalar", (lambda E, i_=src_fn(u), a=stats[:, 4 + u:5 + u]:
                              E.activation(self.sqdump[:, 0:i_.shape[1]], i_,
                                           AF.Square, accum_out=a)),
                   waits=tuple(s2w), inc="s",
                   tag=f"{tag}_s2_{u}")
        S1, S2 = P.now("v"), P.now("s")
        P.emit("vector", lambda E: E.tensor_scalar_mul(
            stats[:, 8:8 + n], stats[:, 0:n], 1.0 / 1024),
            waits=(S1, S2), inc="v", tag=f"{tag}_mu")
        P.emit("vector", lambda E: E.tensor_mul(
            stats[:, 16:16 + n], stats[:, 8:8 + n], stats[:, 8:8 + n]),
            waits=(P.now("v"),), inc="v", tag=f"{tag}_mu2")
        P.emit("vector", lambda E: E.tensor_scalar_mul(
            stats[:, 12:12 + n], stats[:, 4:4 + n], 1.0 / 1024),
            waits=(P.now("v"),), inc="v", tag=f"{tag}_va")
        P.emit("vector", lambda E: E.tensor_tensor(
            stats[:, 12:12 + n], stats[:, 12:12 + n], stats[:, 16:16 + n],
            op=ALU.subtract), waits=(P.now("v"),), inc="v", tag=f"{tag}_vb")
        P.emit("vector", lambda E: E.tensor_scalar_add(
            stats[:, 12:12 + n], stats[:, 12:12 + n], 1e-5),
            waits=(P.now("v"),), inc="v", tag=f"{tag}_vc")
        self.newton_rsqrt(stats[:, 20:20 + n], stats[:, 12:12 + n],
                          pre=(P.now("v"),), tag=f"{tag}_nw")
        P.emit("vector", lambda E: E.tensor_mul(
            stats[:, 24:24 + n], stats[:, 8:8 + n], stats[:, 20:20 + n]),
            waits=(P.now("v"),), inc="v", tag=f"{tag}_nm")
        P.emit("vector", lambda E: E.tensor_scalar_mul(
            stats[:, 24:24 + n], stats[:, 24:24 + n], -1.0),
            waits=(P.now("v"),), inc="v", tag=f"{tag}_nm2")
        return P.now("v")

    def build(self):
        nc, P = self.nc, self.P
        self.xin = self.dram("xin", [T, D], F32)
        self.wsrc = {n: self.dram("w" + n, [D, D], F32) for n in "qkvo"}
        self.bqkv_d = self.dram("bqkv", [128, 24], F32)
        self.bo_d = self.dram("bo_in", [1, D], F32)
        self.rw_d = self.dram("rw_in", [16, D], F32)
        self.sup_d = self.dram("sup", [2, 512, D], F32)
        self.sdn_d = self.dram("sdn", [2, D, 512], F32)
        self.rup_d = self.dram("rup", [16, 512, D], F32)
        self.rdn_d = self.dram("rdn", [16, D, 512], F32)
        cd = {}
        for n, sh, dt in (("crep", [128, T], BF16), ("srep", [128, T], BF16),
                          ("pswap", [128, 128], BF16), ("identb", [128, 128], BF16),
                          ("identf", [128, 128], F32), ("hsall", [128, 128], BF16),
                          ("e4", [128, 64], BF16), ("selp", [128, 1024], F32),
                          ("mask4", [128, 2048], BF16), ("selr", [16, 2048], BF16),
                          ("ones1", [1, 128], F32)):
            cd[n] = self.dram(n, sh, dt)
        self.cd = cd
        self.out_d = self.dram("out", [T, D], F32, kind="ExternalOutput")
        self.qf_s = self.scratch("qf_s", [NT, 128, 4096], BF16)
        self.kf_s = self.scratch("kf_s", [NT, 128, 4096], BF16)
        self.vt_s = self.scratch("vt_s", [NT, 128, 4096], BF16)
        self.x1_s = self.scratch("x1_s", [NT, 128, 4096], F32)
        self.wot_s = self.scratch("wot_s", [128, 8192], BF16)
        self.rwt_s = self.scratch("rwt_s", [128, 128], BF16)
        self.utp_s = self.scratch("utp_s", [NEXP, 128, 4096], BF16)
        self.dtp_s = self.scratch("dtp_s", [NEXP, 128, 4096], BF16)

        self.crep = [self.sb(f"crep_sb{j}", [128, 512], BF16) for j in range(2)]
        self.srep = [self.sb(f"srep_sb{j}", [128, 512], BF16) for j in range(2)]
        self.pswap = self.sb("pswap_sb", [128, 128], BF16)
        self.identb = self.sb("identb_sb", [128, 128], BF16)
        self.identf = self.sb("identf_sb", [128, 128], F32)
        self.hsall = self.sb("hsall_sb", [128, 128], BF16)
        self.ones1 = self.sb("ones1_sb", [1, 128], F32)
        self.bqkv = self.sb("bqkv_sb", [128, 24], F32)
        self.bo_sb = self.sb("bo_sb", [1, 1024], F32)
        self.runmax = self.sb("runmax_sb", [16, 16], F32)
        self.bias_ch = self.sb("bias_ch", [128, 16], F32)
        self.bias_pr = self.sb("bias_pr", [128, 8], F32)
        self.bo_rep = self.sb("bo_rep", [128, 1024], F32)
        self.stats = self.sb("stats_sb", [128, 32], F32)
        self.nw1 = self.sb("nw1", [128, 8], F32)
        self.nw2 = self.sb("nw2", [128, 8], F32)
        self.nw3 = self.sb("nw3", [128, 8], F32)
        self.sqdump = self.sb("sqdump", [128, 1024], BF16)
        self.stats_free = Buf()

        for n, t in (("pswap", self.pswap), ("identb", self.identb),
                     ("identf", self.identf), ("hsall", self.hsall),
                     ("ones1", self.ones1)):
            P.dma("sync", t[:, :], cd[n][:, :], inc="d_c", tag="c_" + n)
        P.dma("sync", self.bqkv[:, :], self.bqkv_d[:, :], inc="d_c", tag="c_bqkv")
        P.dma("sync", self.bo_sb[:, :], self.bo_d[:, :], inc="d_c", tag="c_bo")
        self.CONSTS = P.now("d_c")

        with ExitStack() as s1a:
            self.phase_w_attn(s1a)
            if "1a" in self.phases:
                self.phase_1a(s1a)
                self.phase_15()
        if "1b" in self.phases:
            P.gate_all()
            with ExitStack() as s1b:
                self.phase_1b(s1b)
        if "2" in self.phases:
            P.gate_all()
            with ExitStack() as s2:
                self.phase_2(s2)
        fin = [P.now("d_o")] if P.cnt.get("d_o") else []
        for s in ("d_spq", "d_spk", "d_spv", "d_x1w", "d_fwst0", "d_fwst1",
                  "d_woS", "d_rwS", "d_dbg"):
            if P.cnt.get(s):
                fin.append(P.now(s))
        P.emit("sync", lambda E: E.nop(), waits=tuple(fin), tag="final")
        P.simulate()
        return self

    def phase_w_attn(self, sc):
        nc, P = self.nc, self.P
        self.wstage = self.sb("wstage", [128, 8192], BF16, sc)
        self.wtrg = {n: self.sb(f"w{n}t_sb", [128, 8192], BF16, sc)
                     for n in "qkv"}
        self.fin = [self.sb(f"fw_in{j}", [128, 4096], BF16, sc) for j in range(2)]
        self.fout = [self.sb("fw_out0", [128, 4096], BF16, sc)] * 2
        self.fin_free = [Buf(), Buf()]
        _fb = Buf()
        self.fout_free = [_fb, _fb]
        self.psW = self.ps("psW", [128, 512], BF16, sc)
        self.psW2 = self.ps("psW2", [128, 512], BF16, sc)
        self.psw_free = [Buf(), Buf()]
        wstage_free = Buf()
        self.attnw_ready = {}
        for wn in "qkvo":
            dst = self.wtrg[wn] if wn != "o" else self.fout[0]
            P.dma("gpsimd", self.wstage[:, :].rearrange("p (c d) -> p c d", c=8),
                  self.wsrc[wn][:, :].rearrange("(c p) d -> p c d", p=128),
                  waits=(wstage_free.free,) if wstage_free.free else (),
                  inc="d_wA", tag=f"w{wn}_ld")
            LD = P.now("d_wA")
            for dc in range(8):
                for half in range(2):
                    k = (dc * 2 + half) % 2
                    pw = (self.psW, self.psW2)[k]
                    pb = self.psw_free[k]
                    for oq in range(4):
                        oc = half * 4 + oq
                        w = [LD, self.CONSTS] if oq == 0 else []
                        if oq == 0 and pb.free:
                            w.append(pb.free)
                        if (wn == "o" and oq == 0 and half == 0 and dc in (0, 4)
                                and self.fout_free[0].free):
                            w.append(self.fout_free[0].free)
                        P.emit("tensor",
                               (lambda E, o=pw[:, oq * 128:(oq + 1) * 128],
                                i_=self.wstage[:, oc * 1024 + dc * 128:
                                               oc * 1024 + dc * 128 + 128]:
                                E.transpose(o, i_, self.identb[:, :])),
                               waits=tuple(w), inc="t" if oq == 3 else None,
                               tag=f"w{wn}_t{dc}{half}")
                    dcol = (dc * 1024 + half * 512) if wn != "o" else \
                        ((dc % 4) * 1024 + half * 512)
                    P.emit("vector",
                           (lambda E, d=dst[:, dcol:dcol + 512],
                            sp=pw[:, :]: E.tensor_copy(d, sp)),
                           waits=(P.now("t"),), inc="v", tag=f"w{wn}_e{dc}{half}")
                    pb.free = P.now("v")
                if wn == "o" and dc in (3, 7):
                    g = dc // 4
                    P.dma("sync", self.wot_s[:, g * 4096:(g + 1) * 4096],
                          self.fout[0][:, :], waits=(P.now("v"),),
                          inc="d_woS", tag=f"wo_st{g}")
                    self.fout_free[0].free = P.now("d_woS")
            self.attnw_ready[wn] = P.now("v")
            wstage_free.free = P.now("v")

        rw_stage = self.sb("rw_stage", [16, 1024], BF16, sc)
        rwt_stage = self.sb("rwt_stage", [128, 128], BF16, sc)
        P.dma("gpsimd", rw_stage[:, :], self.rw_d[:, :],
              waits=(wstage_free.free,), inc="d_wA", tag="wrw_ld")
        RWL = P.now("d_wA")
        for g in range(2):
            pw = (self.psW, self.psW2)[g]
            pb = self.psw_free[g]
            for q in range(4):
                dc = g * 4 + q
                w = [RWL, self.CONSTS] if q == 0 else []
                if q == 0 and pb.free:
                    w.append(pb.free)
                P.emit("tensor",
                       (lambda E, o=pw[:, q * 16:(q + 1) * 16],
                        i_=rw_stage[:, dc * 128:(dc + 1) * 128]:
                        E.transpose(o, i_, self.identb[0:16, 0:16])),
                       waits=tuple(w), inc="t" if q == 3 else None,
                       tag=f"wrw_t{dc}")
            P.emit("vector", (lambda E, d=rwt_stage[:, g * 64:(g + 1) * 64],
                              sp=pw[:, 0:64]: E.tensor_copy(d, sp)),
                   waits=(P.now("t"),), inc="v", tag=f"wrw_e{g}")
            pb.free = P.now("v")
        P.dma("sync", self.rwt_s[:, :], rwt_stage[:, :], waits=(P.now("v"),),
              inc="d_rwS", tag="wrw_st")

        mats = []
        for ei in range(16):
            mats.append((self.rup_d[ei].rearrange("(c p) d -> p c d", p=128),
                         4, self.utp_s[ei], f"fu{ei}"))
            mats.append((self.rdn_d[ei].rearrange("(c p) e -> p c e", p=128),
                         8, self.dtp_s[ei], f"fd{ei}"))
        for si in range(2):
            mats.append((self.sup_d[si].rearrange("(c p) d -> p c d", p=128),
                         4, self.utp_s[16 + si], f"fu{16 + si}"))
            mats.append((self.sdn_d[si].rearrange("(c p) e -> p c e", p=128),
                         8, self.dtp_s[16 + si], f"fd{16 + si}"))
        self.ffn_mats = mats
        self.ffn_load_i = 0
        self.ffn_comp_i = 0
        self.ffn_ld_ms = {}

    def _ffn_load(self, mi):
        P = self.P
        src, nch, dst, tag = self.ffn_mats[mi]
        bi = mi % 2
        P.dma("gpsimd",
              self.fin[bi][:, :].rearrange("p (c d) -> p c d", c=nch), src,
              waits=(self.fin_free[bi].free,) if self.fin_free[bi].free else (),
              inc=f"d_f{bi}", tag=tag + "_ld")
        self.ffn_ld_ms[mi] = P.now(f"d_f{bi}")

    def _ffn_comp(self, mi):
        P = self.P
        src, nch, dst, tag = self.ffn_mats[mi]
        bi = mi % 2
        LD = self.ffn_ld_ms[mi]
        fr = 4096 // nch
        nblk = fr // 128
        ow = self.fout[bi]
        first = True
        for b in range(nblk):
            ngr = max(1, nch // 4)
            for g in range(ngr):
                k = (b * ngr + g) % 2
                pw = (self.psW, self.psW2)[k]
                pb = self.psw_free[k]
                span = min(4, nch)
                for q in range(span):
                    c_in = g * 4 + q
                    w = []
                    if q == 0:
                        w = [LD, self.CONSTS]
                        if pb.free:
                            w.append(pb.free)
                        if first and self.fout_free[bi].free:
                            w.append(self.fout_free[bi].free)
                        first = False
                    P.emit("tensor",
                           (lambda E, o=pw[:, q * 128:(q + 1) * 128],
                            i_=self.fin[bi][:, c_in * fr + b * 128:
                                            c_in * fr + b * 128 + 128]:
                            E.transpose(o, i_, self.identb[:, :])),
                           waits=tuple(w), inc="t" if q == span - 1 else None,
                           tag=tag + f"_t{b}{g}")
                P.emit("vector",
                       (lambda E, d=ow[:, b * (nch * 128) + g * 512:
                                       b * (nch * 128) + g * 512 + span * 128],
                        sp=pw[:, 0:span * 128]: E.tensor_copy(d, sp)),
                       waits=(P.now("t"),), inc="v", tag=tag + f"_e{b}{g}")
                pb.free = P.now("v")
        self.fin_free[bi].free = P.now("v")
        P.dma("sync", dst[:, :], ow[:, :], waits=(P.now("v"),),
              inc=f"d_fwst{bi}", tag=tag + "_st")
        self.fout_free[bi].free = P.now(f"d_fwst{bi}")

    def emit_ffn_prep(self, nmat, flush=False):
        n = len(self.ffn_mats)
        target = min(self.ffn_load_i + nmat, n)
        while (self.ffn_load_i < target
               or (flush and self.ffn_comp_i < n)):
            if (self.ffn_load_i < target
                    and self.ffn_load_i - self.ffn_comp_i < 2):
                self._ffn_load(self.ffn_load_i)
                self.ffn_load_i += 1
            elif self.ffn_comp_i < self.ffn_load_i:
                self._ffn_comp(self.ffn_comp_i)
                self.ffn_comp_i += 1
            else:
                break

    def phase_1a(self, sc):
        nc, P = self.nc, self.P
        x_sb = [self.sb(f"x_sb{j}", [128, 4096], F32, sc) for j in range(2)]
        xn_tok = self.sb("xn_tok", [128, 4096], BF16, sc)
        xn_f = self.sb("xn_f", [128, 4096], BF16, sc)
        proj = [self.sb(f"proj{j}", [128, 4096], BF16, sc) for j in range(3)]
        qrot = self.sb("qrot", [128, 4096], BF16, sc)
        krot = self.sb("krot", [128, 4096], BF16, sc)
        vtok = self.sb("vtok", [128, 4096], BF16, sc)
        sqn = self.sb("sqn", [128, 1024], BF16, sc)
        ttmp = self.sb("ttmp", [128, 512], F32, sc)
        psQ = self.ps("psQ", [128, 512], F32, sc)
        psQ2 = self.ps("psQ2", [128, 512], F32, sc)
        psSw = self.ps("psSw", [128, 512], F32, sc)
        psSw2 = self.ps("psSw2", [128, 512], F32, sc)
        psNorm = self.ps("psNorm", [16, 512], F32, sc)
        self.psNorm = psNorm
        self.psQ_ref = psQ

        x_free = [Buf(), Buf()]
        xnf_free = Buf()
        xntok_free = Buf()
        proj_free = [Buf(), Buf(), Buf()]
        rot_free = {"q": Buf(), "k": Buf(), "v": Buf()}
        psq_free = [Buf(), Buf()]
        self.psq_free_ref = psq_free
        pssw_free = [Buf(), Buf()]
        self.psnorm_free = Buf()
        cs_free = [Buf(), Buf()]
        sqn_free = [Buf(), Buf()]
        xn_ms = [None, None]
        csl_ms = [None, None]

        xl_ms = [None, None]

        def emit_x_load(ii):
            bi = ii % 2
            xb = x_sb[bi]
            P.dma("sync", xb[:, :].rearrange("p (u d) -> p u d", u=4),
                  self.xin[ii * TT:(ii + 1) * TT, :].rearrange(
                      "(u p) d -> p u d", p=128),
                  waits=(x_free[bi].free,) if x_free[bi].free else (),
                  inc=f"d_x{bi}", tag=f"x_ld{ii}")
            xl_ms[bi] = P.now(f"d_x{bi}")
            P.dma("sync", self.crep[bi][:, :],
                  self.cd["crep"][:, ii * TT:(ii + 1) * TT],
                  waits=(cs_free[bi].free,) if cs_free[bi].free else (),
                  inc=f"d_cs{bi}", tag=f"cs_c{ii}")
            P.dma("sync", self.srep[bi][:, :],
                  self.cd["srep"][:, ii * TT:(ii + 1) * TT],
                  inc=f"d_cs{bi}", tag=f"cs_s{ii}")
            csl_ms[bi] = P.now(f"d_cs{bi}")

        def emit_ln1(ii):
            bi = ii % 2
            xb = x_sb[bi]
            XL = xl_ms[bi]
            LN1 = self.ln_stats(lambda u: xb[:, u * 1024:(u + 1) * 1024],
                                XL, f"ln1_{ii}", extra_s2_wait=(self.CONSTS,))
            xw = [LN1]
            if xntok_free.free:
                xw.append(xntok_free.free)
            for u in range(NSUB):
                P.emit("scalar",
                       (lambda E, o=xn_tok[:, u * 1024:(u + 1) * 1024],
                        i_=xb[:, u * 1024:(u + 1) * 1024],
                        scv=self.stats[:, 20 + u:21 + u],
                        bv=self.stats[:, 24 + u:25 + u]:
                        E.activation(o, i_, AF.Identity, bias=bv, scale=scv)),
                       waits=tuple(xw) if u == 0 else (), inc="s",
                       tag=f"ln1n_{ii}_{u}")
            XN = P.now("s")
            self.stats_free.free = XN
            x_free[bi].free = XN
            xn_ms[bi] = XN

        emit_x_load(0)
        emit_ln1(0)
        for i in range(NT):
            bi = i % 2
            XN = xn_ms[bi]
            CSL = csl_ms[i % 2]

            tw = [XN]
            if xnf_free.free:
                tw.append(xnf_free.free)
            for c in range(8):
                k = c % 2
                pw = (psQ, psQ2)[k]
                pb = psq_free[k]
                for u in range(NSUB):
                    w = list(tw) if (c == 0 and u == 0) else []
                    if u == 0 and pb.free:
                        w.append(pb.free)
                    P.emit("tensor",
                           (lambda E, o=pw.bitcast(BF16)[:, u * 128:(u + 1) * 128],
                            i_=xn_tok[:, u * 1024 + c * 128:u * 1024 + c * 128 + 128]:
                            E.transpose(o, i_, self.identb[:, :])),
                           waits=tuple(w), inc="t" if u == 3 else None,
                           tag=f"xnt{i}_{c}{u}")
                P.emit("vector",
                       (lambda E, d=xn_f[:, c * 512:(c + 1) * 512],
                        sp=pw.bitcast(BF16)[:, 0:512]: E.tensor_copy(d, sp)),
                       waits=(P.now("t"),), inc="v", tag=f"xne{i}_{c}")
                pb.free = P.now("v")
            XNF = P.now("v")
            xntok_free.free = P.now("t")
            if i + 1 < NT:
                emit_x_load(i + 1)

            for pi, pn in enumerate("qkv"):
                wt = self.wtrg[pn]
                bcol = pi * 8
                projb = proj[pi]
                pj_w = [XNF, self.attnw_ready[pn]]
                if proj_free[pi].free:
                    pj_w.append(proj_free[pi].free)
                if pn in "qk":
                    rotb = qrot if pn == "q" else krot
                    rfree = rot_free[pn]
                    EV = {}
                    RP = {}
                    for step in range(10):
                        if step < 8:
                            oc = step
                            k = oc % 2
                            pw = (psQ, psQ2)[k]
                            pb = psq_free[k]
                            for dc in range(8):
                                w = list(pj_w) if (oc == 0 and dc == 0) else []
                                if dc == 0 and pb.free:
                                    w.append(pb.free)
                                P.emit("tensor",
                                       (lambda E, o=pw[:, :],
                                        l=wt[:, dc * 1024 + oc * 128:
                                             dc * 1024 + oc * 128 + 128],
                                        r=xn_f[:, dc * 512:(dc + 1) * 512],
                                        s0=(dc == 0), s1=(dc == 7):
                                        E.matmul(o, l, r, start=s0, stop=s1)),
                                       waits=tuple(w),
                                       inc="t" if dc == 7 else None,
                                       tag=f"{pn}mm{i}_{oc}{dc}")
                            pj_w = []
                            P.emit("scalar",
                                   (lambda E, o=projb[:, oc * 512:(oc + 1) * 512],
                                    sp=pw[:, :],
                                    bv=self.bqkv[:, bcol + oc:bcol + oc + 1]:
                                    E.activation(o, sp, AF.Identity, bias=bv)),
                                   waits=(P.now("t"),), inc="s",
                                   tag=f"{pn}ev{i}_{oc}")
                            pb.free = P.now("s")
                            EV[oc] = P.now("s")
                        if 1 <= step <= 8:
                            oc = step - 1
                            k = oc % 2
                            pw = (psSw, psSw2)[k]
                            pb = pssw_free[k]
                            w = [EV[oc], self.CONSTS]
                            if pb.free:
                                w.append(pb.free)
                            P.emit("tensor",
                                   (lambda E, o=pw[:, :],
                                    r=projb[:, oc * 512:(oc + 1) * 512]:
                                    E.matmul(o, self.pswap[:, :], r,
                                             start=True, stop=True)),
                                   waits=tuple(w), inc="t", tag=f"{pn}sw{i}_{oc}")
                            SW = P.now("t")
                            w2 = [SW, CSL]
                            if oc == 0 and rfree.free:
                                w2.append(rfree.free)
                            P.emit("vector",
                                   (lambda E, o=rotb[:, oc * 512:(oc + 1) * 512],
                                    a=projb[:, oc * 512:(oc + 1) * 512],
                                    cr=self.crep[i % 2][:, :]:
                                    E.tensor_mul(o, a, cr)),
                                   waits=tuple(w2), inc="v", tag=f"{pn}r1{i}_{oc}")
                            P.emit("vector",
                                   (lambda E, sp=pw[:, :],
                                    sr=self.srep[i % 2][:, :]:
                                    E.tensor_mul(ttmp[:, :], sp, sr)),
                                   waits=(P.now("v"),), inc="v",
                                   tag=f"{pn}r2{i}_{oc}")
                            pb.free = P.now("v")
                            P.emit("vector",
                                   (lambda E, o=rotb[:, oc * 512:(oc + 1) * 512]:
                                    E.tensor_add(o, o, ttmp[:, :])),
                                   waits=(P.now("v"),), inc="v",
                                   tag=f"{pn}r3{i}_{oc}")
                            RP[oc] = P.now("v")
                        if step >= 2:
                            oc = step - 2
                            k = oc % 2
                            w = [RP[oc]]
                            if oc == 0 and self.psnorm_free.free:
                                w.append(self.psnorm_free.free)
                            if sqn_free[k].free:
                                w.append(sqn_free[k].free)
                            P.emit("scalar",
                                   (lambda E, o=sqn[:, k * 512:(k + 1) * 512],
                                    i_=rotb[:, oc * 512:(oc + 1) * 512]:
                                    E.activation(o, i_, AF.Square)),
                                   waits=tuple(w), inc="s",
                                   tag=f"{pn}nsq{i}_{oc}")
                            P.emit("tensor",
                                   (lambda E, l=self.hsall[:, oc * 16:(oc + 1) * 16],
                                    r=sqn[:, k * 512:(k + 1) * 512],
                                    s0=(oc == 0), s1=(oc == 7):
                                    E.matmul(psNorm[:, :], l, r, start=s0, stop=s1,
                                             skip_group_check=True)),
                                   waits=(P.now("s"),), inc="t",
                                   tag=f"{pn}nmm{i}_{oc}")
                            sqn_free[k].free = P.now("t")
                    ROT = RP[7]
                    col = i if pn == "q" else 8 + i
                    P.emit("vector",
                           (lambda E, o=self.runmax[:, col:col + 1]:
                            E.reduce_max(o, psNorm[:, :], axis=AX)),
                           waits=(P.now("t"),), inc="v", tag=f"{pn}nrm{i}")
                    self.psnorm_free.free = P.now("v")
                    spsem = "d_spq" if pn == "q" else "d_spk"
                    P.dma("sync", (self.qf_s if pn == "q" else self.kf_s)[i][:, :],
                          rotb[:, :], waits=(ROT,), inc=spsem, tag=f"{pn}sp{i}")
                    rfree.free = P.now(spsem)
                    proj_free[pi].free = ROT
                    if pn == "k" and i + 1 < NT:
                        emit_ln1(i + 1)
                else:
                    for oc in range(8):
                        k = oc % 2
                        pw = (psQ, psQ2)[k]
                        pb = psq_free[k]
                        for dc in range(8):
                            w = list(pj_w) if (oc == 0 and dc == 0) else []
                            if dc == 0 and pb.free:
                                w.append(pb.free)
                            P.emit("tensor",
                                   (lambda E, o=pw[:, :],
                                    l=wt[:, dc * 1024 + oc * 128:
                                         dc * 1024 + oc * 128 + 128],
                                    r=xn_f[:, dc * 512:(dc + 1) * 512],
                                    s0=(dc == 0), s1=(dc == 7):
                                    E.matmul(o, l, r, start=s0, stop=s1)),
                                   waits=tuple(w), inc="t" if dc == 7 else None,
                                   tag=f"{pn}mm{i}_{oc}{dc}")
                        pj_w = []
                        P.emit("scalar",
                               (lambda E, o=projb[:, oc * 512:(oc + 1) * 512],
                                sp=pw[:, :],
                                bv=self.bqkv[:, bcol + oc:bcol + oc + 1]:
                                E.activation(o, sp, AF.Identity, bias=bv)),
                               waits=(P.now("t"),), inc="s", tag=f"{pn}ev{i}_{oc}")
                        pb.free = P.now("s")
                    PRJ = P.now("s")
                    vw = [PRJ]
                    if rot_free["v"].free:
                        vw.append(rot_free["v"].free)
                    for u in range(NSUB):
                        for half in range(2):
                            k = half
                            pw = (psSw, psSw2)[k]
                            pb = pssw_free[k]
                            for oq in range(4):
                                oc = half * 4 + oq
                                w = list(vw) if (u == 0 and half == 0 and oq == 0) else []
                                if oq == 0 and pb.free:
                                    w.append(pb.free)
                                P.emit("tensor",
                                       (lambda E,
                                        o=pw.bitcast(BF16)[:, oq * 128:(oq + 1) * 128],
                                        i_=projb[:, oc * 512 + u * 128:
                                                 oc * 512 + u * 128 + 128]:
                                        E.transpose(o, i_, self.identb[:, :])),
                                       waits=tuple(w),
                                       inc="t" if oq == 3 else None,
                                       tag=f"vt{i}_{u}{half}{oq}")
                            P.emit("vector",
                                   (lambda E, d=vtok[:, u * 1024 + half * 512:
                                                     u * 1024 + half * 512 + 512],
                                    sp=pw.bitcast(BF16)[:, 0:512]:
                                    E.tensor_copy(d, sp)),
                                   waits=(P.now("t"),), inc="v",
                                   tag=f"vte{i}_{u}{half}")
                            pb.free = P.now("v")
                        vw = []
                    VT = P.now("v")
                    P.dma("sync", self.vt_s[i][:, :], vtok[:, :], waits=(VT,),
                          inc="d_spv", tag=f"vsp{i}")
                    rot_free["v"].free = P.now("d_spv")
                    proj_free[pi].free = VT
            xnf_free.free = P.now("t")
            cs_free[bi].free = P.now("v")
            self.emit_ffn_prep(9 if i >= 4 else 4)
        self.emit_ffn_prep(len(self.ffn_mats), flush=True)
        self.x_free_1a = x_free

    def phase_15(self):
        P = self.P
        nw1, nw2, nw3 = self.nw1, self.nw2, self.nw3
        psNorm = self.psNorm
        P.emit("vector", lambda E: E.reduce_max(nw1[0:16, 0:1],
                                                self.runmax[:, 0:8], axis=AX),
               waits=(P.now("v"), P.now("t"), P.now("s")), inc="v", tag="ch_mq")
        P.emit("vector", lambda E: E.reduce_max(nw1[0:16, 1:2],
                                                self.runmax[:, 8:16], axis=AX),
               waits=(P.now("v"),), inc="v", tag="ch_mk")
        P.emit("vector", lambda E: E.tensor_mul(nw1[0:16, 2:3], nw1[0:16, 0:1],
                                                nw1[0:16, 1:2]),
               waits=(P.now("v"),), inc="v", tag="ch_p")
        P.emit("vector", lambda E: E.tensor_scalar_max(nw1[0:16, 2:3],
                                                       nw1[0:16, 2:3], 1e-12),
               waits=(P.now("v"),), inc="v", tag="ch_cl")
        self.newton_rsqrt(self.stats[0:16, 28:29], nw1[0:16, 2:3],
                          pre=(P.now("v"), self.stats_free.free), tag="ch_nw")
        P.emit("vector", lambda E: E.tensor_mul(nw1[0:16, 3:4], nw1[0:16, 2:3],
                                                self.stats[0:16, 28:29]),
               waits=(P.now("v"),), inc="v", tag="ch_s")
        P.emit("vector", lambda E: E.tensor_scalar_mul(nw1[0:16, 3:4],
                                                       nw1[0:16, 3:4], -0.125),
               waits=(P.now("v"),), inc="v", tag="ch_n")
        CH = P.now("v")
        P.emit("tensor", lambda E: E.transpose(psNorm[0:1, 0:16],
                                               nw1[0:16, 3:4],
                                               self.identf[0:16, 0:16]),
               waits=(CH, self.psnorm_free.free), inc="t", tag="ch_t")
        P.emit("vector", lambda E: E.tensor_copy(self.stats[0:1, 0:16],
                                                 psNorm[0:1, 0:16]),
               waits=(P.now("t"), self.stats_free.free), inc="v", tag="ch_e")
        psQ = self.psQ_ref
        pqf = self.psq_free_ref[0]
        P.emit("tensor", lambda E: E.matmul(psQ[:, 0:16], self.ones1[:, :],
                                            self.stats[0:1, 0:16],
                                            start=True, stop=True),
               waits=(P.now("v"), pqf.free), inc="t", tag="ch_b")
        P.emit("vector", lambda E: E.tensor_copy(self.bias_ch[:, :],
                                                 psQ[:, 0:16]),
               waits=(P.now("t"),), inc="v", tag="ch_be")
        P.emit("vector", lambda E: E.tensor_scalar_max(
            self.bias_ch[:, :], self.bias_ch[:, :], -30.0),
            waits=(P.now("v"),), inc="v", tag="ch_cap")
        for p_ in range(8):
            P.emit("vector",
                   (lambda E, o=self.bias_pr[:, p_:p_ + 1],
                    a=self.bias_ch[:, 2 * p_:2 * p_ + 1],
                    b2=self.bias_ch[:, 2 * p_ + 1:2 * p_ + 2]:
                    E.tensor_tensor(o, a, b2, op=ALU.min)),
                   waits=(P.now("v"),), inc="v", tag=f"ch_pr{p_}")
        self.psnorm_free.free = P.now("v")
        self.stats_free.free = P.now("v")
        if "dbg_bias" in self.taps:
            dbg_b = self.scratch("dbg_bias", [128, 16], F32)
            dbg_r = self.scratch("dbg_rmax", [16, 16], F32)
            P.dma("sync", dbg_b[:, :], self.bias_ch[:, :],
                  waits=(P.now("v"),), inc="d_dbg", tag="dbg_bias")
            P.dma("sync", dbg_r[:, :], self.runmax[:, :],
                  waits=(P.now("v"),), inc="d_dbg", tag="dbg_rmax")

    def phase_1b(self, sc):
        nc, P = self.nc, self.P
        wot = self.sb("wot_sb", [128, 8192], BF16, sc)
        q_sb = [self.sb(f"q_sb{j}", [128, 4096], BF16, sc) for j in range(2)]
        k_sb = [self.sb(f"k_sb{j}", [128, 4096], BF16, sc) for j in range(2)]
        v_sb = [self.sb(f"v_sb{j}", [128, 4096], BF16, sc) for j in range(2)]
        probs = self.sb("probs", [128, 4096], BF16, sc)
        attn = self.sb("attn", [128, 4096], BF16, sc)
        av32 = self.sb("av32", [128, 512], F32, sc)
        xr_sb = [self.sb(f"xr_sb{j}", [128, 4096], F32, sc) for j in range(2)]
        out1 = self.sb("out1", [128, 4096], F32, sc)
        x1t = self.sb("x1t", [128, 4096], F32, sc)
        selp_sb = self.sb("selp_sb", [128, 1024], F32, sc)
        mask_sb = self.sb("mask_sb", [128, 2048], BF16, sc)
        e4_sb = self.sb("e4_sb", [128, 64], BF16, sc)
        Ssb = self.sb("Ssb", [128, 512], F32, sc)
        recS = self.sb("recS", [128, 512], F32, sc)
        psA = self.ps("psA", [128, 2048], F32, sc)
        psAV = [self.ps(f"psAV{j}", [128, 512], F32, sc) for j in range(2)]
        psWo = self.ps("psWo", [128, 512], F32, sc)
        psWo2 = self.ps("psWo2", [128, 512], F32, sc)
        psSums = psWo
        zero128 = self.sb("zero128", [128, 128], BF16, sc)
        P.emit("vector", lambda E: E.memset(zero128[:, :], 0.0), inc="v",
               tag="zero_init")
        ZERO = P.now("v")

        if "dbg_attn" in self.taps:
            self.dbg_attn_s = self.scratch("dbg_attn", [NT, 128, 4096], BF16)
            self.dbg_s_s = self.scratch("dbg_s", [NT, 128, 512], F32)
        P.dma("sync", wot[:, :], self.wot_s[:, :],
              waits=(P.now("d_woS"),), inc="d_wo2", tag="wot_ld")
        WOT = P.now("d_wo2")
        P.dma("sync", selp_sb[:, :], self.cd["selp"][:, :], inc="d_c", tag="c_selp")
        P.dma("sync", mask_sb[:, :], self.cd["mask4"][:, :], inc="d_c", tag="c_mask")
        P.dma("sync", e4_sb[:, :], self.cd["e4"][:, :], inc="d_c", tag="c_e4")
        C1B = P.now("d_c")

        for hf in range(2):
            P.emit("tensor", (lambda E, r=self.bo_sb[:, hf * 512:(hf + 1) * 512]:
                              E.matmul(psWo[:, :], self.ones1[:, :], r,
                                       start=True, stop=True)),
                   waits=(self.CONSTS,) if hf == 0 else (P.now("v"),),
                   inc="t", tag=f"bo_b{hf}")
            P.emit("vector", (lambda E, d=self.bo_rep[:, hf * 512:(hf + 1) * 512]:
                              E.tensor_copy(d, psWo[:, :])),
                   waits=(P.now("t"),), inc="v", tag=f"bo_e{hf}")
        BOREP = P.now("v")
        pswo_free_init = BOREP

        SPILLS = [("d_spq", 16 * 8), ("d_spk", 16 * 8), ("d_spv", 16 * 8)]
        q_free = [Buf(), Buf()]
        k_free = [Buf(), Buf()]
        v_free = [Buf(), Buf()]
        xr_free = [Buf(), Buf()]
        probs_free = {g: Buf() for g in range(2)}
        ps_free = Buf()
        psav_free = [Buf(), Buf()]
        pswo_free = Buf()
        pswo_free.free = pswo_free_init
        pswo2_free = Buf()
        attn_free = Buf()
        out1_free = Buf()
        x1t_free = Buf()

        attn32 = self.sb("attn32", [128, 4096], F32, sc)
        attn32_free = Buf()
        pending_spill = []

        def flush_spill():
            while pending_spill:
                si, sX1 = pending_spill.pop(0)
                P.dma("sync", self.x1_s[si][:, :], x1t[:, :], waits=(sX1,),
                      inc="d_x1w", tag=f"x1sp{si}")
                x1t_free.free = P.now("d_x1w")
        ql_ms = [None, None]

        def emit_q_load(ii):
            qb2 = ii % 2
            P.dma("sync", q_sb[qb2][:, :], self.qf_s[ii][:, :],
                  waits=tuple((SPILLS if ii == 0 else [])
                              + ([q_free[qb2].free] if q_free[qb2].free else [])),
                  inc=f"d_q{qb2}", tag=f"q_ld{ii}")
            ql_ms[qb2] = P.now(f"d_q{qb2}")

        emit_q_load(0)
        for i in range(NT):
            qb = i % 2
            QL = ql_ms[qb]
            XR = None

            pend = []

            def emit_avsums(d):
                g, j, p = d["g"], d["j"], d["p"]
                for bb in range(2):
                    b = 2 * g + bb
                    first = (b == 0)
                    last = (b == 3)
                    for hh in range(2):
                        h = 2 * p + hh
                        w = [d["PRD"], d["VL"]] if (bb == 0 and hh == 0) else []
                        if first and bb == 0 and hh == 0 and d["avf"].free:
                            w.append(d["avf"].free)
                        P.emit("tensor",
                               (lambda E,
                                o=d["av"][64 * hh:64 * hh + 64, :],
                                l=v_sb[d["sl"]][:, b * 1024 + h * 64:
                                                b * 1024 + h * 64 + 64],
                                r=probs[:, d["pcol"] + hh * 1024 + bb * 512:
                                        d["pcol"] + hh * 1024 + bb * 512 + 512],
                                s0=first, s1=last, tp=(0, 64 * hh):
                                E.matmul(o, l, r, start=s0, stop=s1,
                                         tile_position=tp,
                                         skip_group_check=True)),
                               waits=tuple(w), inc="t",
                               tag=f"av{i}_{j}_{p}{g}{hh}{bb}")
                    for hh in range(2):
                        h = 2 * p + hh
                        P.emit("tensor",
                               (lambda E,
                                o=psSums[32 * (h % 4):32 * (h % 4) + 16, :],
                                l=e4_sb[:, (h // 4) * 16:(h // 4) * 16 + 16],
                                r=probs[:, d["pcol"] + hh * 1024 + bb * 512:
                                        d["pcol"] + hh * 1024 + bb * 512 + 512],
                                s1=d["slast"] and bb == 1 and g == 1,
                                tp=(0, 32 * (h % 4)):
                                E.matmul(o, l, r, start=False, stop=s1,
                                         tile_position=tp,
                                         skip_group_check=True)),
                               waits=(), inc="t",
                               tag=f"su{i}_{j}_{p}{g}{hh}{bb}")
                probs_free[d["pslot"]].free = P.now("t")
                if g == 1:
                    AVP = P.now("t")
                    aw = [AVP]
                    if d["j"] == 0 and attn32_free.free and p == 0:
                        aw.append(attn32_free.free)
                    if d["j"] == 0:
                        P.emit("vector",
                               (lambda E, dst=attn32[:, p * 512:(p + 1) * 512],
                                sp=d["av"][:, :]: E.tensor_copy(dst, sp)),
                               waits=tuple(aw), inc="v", tag=f"ac{i}_{j}_{p}")
                    else:
                        P.emit("vector",
                               (lambda E, dst=attn32[:, p * 512:(p + 1) * 512],
                                sp=d["av"][:, :]:
                                E.tensor_tensor(dst, sp, dst, op=ALU.add)),
                               waits=tuple(aw), inc="v", tag=f"ac{i}_{j}_{p}")
                    d["avf"].free = P.now("v")
                    if d["vlast"]:
                        v_free[d["sl"]].free = P.now("t")

            zw = [ZERO, C1B]
            if pswo_free.free:
                zw.append(pswo_free.free)
            P.emit("tensor",
                   (lambda E: E.matmul(psSums[:, :], zero128[:, :],
                                       mask_sb[:, 0:512], start=True,
                                       stop=False, skip_group_check=True)),
                   waits=tuple(zw), inc="t", tag=f"zs{i}")

            for j in range(i + 1):
                sl = j % 2
                P.dma("sync", k_sb[sl][:, :], self.kf_s[j][:, :],
                      waits=tuple((SPILLS if i == 0 else [])
                                  + ([k_free[sl].free] if k_free[sl].free else [])),
                      inc=f"d_k{sl}", tag=f"k_ld{i}_{j}")
                KL = P.now(f"d_k{sl}")
                P.dma("gpsimd", v_sb[sl][:, :], self.vt_s[j][:, :],
                      waits=(v_free[sl].free,) if v_free[sl].free else (),
                      inc=f"d_v{sl}", tag=f"v_ld{i}_{j}")
                VL = P.now(f"d_v{sl}")
                if j == min(1, i):
                    flush_spill()
                    P.dma("sync", xr_sb[qb][:, :].rearrange("p (u d) -> p u d", u=4),
                          self.xin[i * TT:(i + 1) * TT, :].rearrange(
                              "(u p) d -> p u d", p=128),
                          waits=(xr_free[qb].free,) if xr_free[qb].free else (),
                          inc=f"d_xr{qb}", tag=f"xr_ld{i}")
                    XR = P.now(f"d_xr{qb}")
                    if i + 1 < NT:
                        emit_q_load(i + 1)
                diag = (j == i)
                first_of_j = True
                for p in range(8):
                    av = psAV[p % 2]
                    avf = psav_free[p % 2]
                    for g in range(2):
                        for bb in range(2):
                            b = 2 * g + bb
                            for hh in range(2):
                                w = []
                                if first_of_j:
                                    w = [KL, QL]
                                    first_of_j = False
                                if bb == 0 and hh == 0 and ps_free.free:
                                    w.append(ps_free.free)
                                P.emit("tensor",
                                       (lambda E,
                                        o=psA[:, hh * 1024 + bb * 512:
                                              hh * 1024 + bb * 512 + 512],
                                        l=k_sb[sl][64 * hh:64 * hh + 64,
                                                   p * 512 + b * 128:
                                                   p * 512 + b * 128 + 128],
                                        r=q_sb[qb][64 * hh:64 * hh + 64,
                                                   p * 512:(p + 1) * 512]:
                                        E.matmul(o, l, r, start=True, stop=True)),
                                       waits=tuple(w), inc="t",
                                       tag=f"sc{i}_{j}_{p}{g}{hh}{bb}")
                        SCM = P.now("t")
                        pcol = g * 2048
                        ew = [SCM]
                        if probs_free[g].free:
                            ew.append(probs_free[g].free)
                        if i == 0 and j == 0 and p == 0:
                            ew.append(P.now("v"))
                        P.emit("scalar",
                               (lambda E, o=probs[:, pcol:pcol + 2048],
                                bv=self.bias_pr[:, p:p + 1]:
                                E.activation(o, psA[:, :], AF.Exp, bias=bv,
                                             scale=0.125)),
                               waits=tuple(ew), inc="s",
                               tag=f"ex{i}_{j}_{p}{g}")
                        EX = P.now("s")
                        ps_free.free = EX
                        if diag:
                            for hh in range(2):
                                for bb in range(2):
                                    b = 2 * g + bb
                                    P.emit("gpsimd",
                                           (lambda E,
                                            o=probs[:, pcol + hh * 1024 + bb * 512:
                                                    pcol + hh * 1024 + bb * 512 + 512],
                                            mk=mask_sb[:, b * 512:(b + 1) * 512]:
                                            E.tensor_mul(o, o, mk)),
                                           waits=(EX, C1B) if (hh == 0 and bb == 0) else (),
                                           inc="g", tag=f"mk{i}_{p}{g}{hh}{bb}")
                            PRD = P.now("g")
                        else:
                            PRD = EX
                        pend.append(dict(p=p, g=g, pcol=pcol, PRD=PRD, VL=VL,
                                         sl=sl, j=j, av=av, avf=avf, pslot=g,
                                         slast=(j == i), vlast=(p == 7)))
                        if len(pend) > 1:
                            emit_avsums(pend.pop(0))
                k_free[sl].free = P.now("t")
            while pend:
                emit_avsums(pend.pop(0))
            AVD = P.now("t")
            AVV = P.now("v")
            P.emit("vector", lambda E: E.tensor_scalar_max(
                Ssb[:, :], psSums[:, :], 1e-30),
                waits=(AVD,), inc="v", tag=f"S{i}")
            P.emit("vector", lambda E: E.reciprocal(recS[:, :], Ssb[:, :]),
                   waits=(P.now("v"),), inc="v", tag=f"Sr{i}")
            SREC = P.now("v")
            for p in range(8):
                w = [SREC]
                if pswo_free.free:
                    w.append(pswo_free.free)
                P.emit("tensor",
                       (lambda E, l=selp_sb[:, p * 128:(p + 1) * 128]:
                        E.matmul(psWo[:, :], l, recS[:, :],
                                 start=True, stop=True)),
                       waits=tuple(w), inc="t", tag=f"bc{i}_{p}")
                BC = P.now("t")
                aw = [BC, AVV]
                if attn_free.free and p == 0:
                    aw.append(attn_free.free)
                P.emit("vector",
                       (lambda E, o=attn[:, p * 512:(p + 1) * 512],
                        a32=attn32[:, p * 512:(p + 1) * 512]:
                        E.tensor_mul(o, a32, psWo[:, :])),
                       waits=tuple(aw), inc="v", tag=f"an{i}_{p}")
                pswo_free.free = P.now("v")
            attn32_free.free = P.now("v")
            ATTN = P.now("v")
            q_free[qb].free = P.now("t")

            if "dbg_attn" in self.taps:
                P.dma("sync", self.dbg_attn_s[i][:, :], attn[:, :],
                      waits=(ATTN,), inc="d_dbg", tag=f"dbga{i}")
                P.dma("sync", self.dbg_s_s[i][:, :], Ssb[:, :],
                      waits=(ATTN,), inc="d_dbg", tag=f"dbgs{i}")

            ww = [ATTN, WOT]
            if out1_free.free:
                ww.append(out1_free.free)
            for u in range(NSUB):
                for oh in range(2):
                    k2 = (u * 2 + oh) % 2
                    pwo = (psWo, psWo2)[k2]
                    pwf = (pswo_free, pswo2_free)[k2]
                    for dc in range(8):
                        w = list(ww) if (u == 0 and oh == 0 and dc == 0) else []
                        if dc == 0 and pwf.free:
                            w.append(pwf.free)
                        P.emit("tensor",
                               (lambda E, o=pwo[:, :],
                                l=attn[:, dc * 512 + u * 128:
                                        dc * 512 + u * 128 + 128],
                                r=wot[:, dc * 1024 + oh * 512:
                                      dc * 1024 + oh * 512 + 512],
                                s0=(dc == 0), s1=(dc == 7):
                                E.matmul(o, l, r, start=s0, stop=s1)),
                               waits=tuple(w), inc="t" if dc == 7 else None,
                               tag=f"wo{i}_{u}{oh}{dc}")
                    ww = []
                    P.emit("vector",
                           (lambda E, o=out1[:, u * 1024 + oh * 512:
                                             u * 1024 + oh * 512 + 512],
                            sp=pwo[:, :],
                            bo=self.bo_rep[:, oh * 512:(oh + 1) * 512]:
                            E.tensor_add(o, sp, bo)),
                           waits=(P.now("t"), BOREP), inc="v",
                           tag=f"woe{i}_{u}{oh}")
                    pwf.free = P.now("v")
            O1 = P.now("v")
            attn_free.free = P.now("t")

            LN2 = self.ln_stats(lambda u: out1[:, u * 1024:(u + 1) * 1024],
                                O1, f"ln2_{i}")
            xw = [LN2, XR]
            if x1t_free.free:
                xw.append(x1t_free.free)
            for u in range(NSUB):
                P.emit("vector",
                       (lambda E, o=x1t[:, u * 1024:(u + 1) * 1024],
                        i_=out1[:, u * 1024:(u + 1) * 1024],
                        m=self.stats[:, 8 + u:9 + u],
                        r=self.stats[:, 20 + u:21 + u]:
                        E.tensor_scalar(o, i_, m, r, op0=ALU.subtract,
                                        op1=ALU.mult)),
                       waits=tuple(xw) if u == 0 else (), inc="v",
                       tag=f"ln2n_{i}_{u}")
                P.emit("vector",
                       (lambda E, o=x1t[:, u * 1024:(u + 1) * 1024],
                        xs=xr_sb[qb][:, u * 1024:(u + 1) * 1024]:
                        E.tensor_add(o, o, xs)),
                       waits=(P.now("v"),), inc="v", tag=f"ln2a_{i}_{u}")
            X1 = P.now("v")
            self.stats_free.free = X1
            out1_free.free = X1
            xr_free[qb].free = X1
            pending_spill.append((i, X1))
        flush_spill()

    def phase_2(self, sc):
        nc, P = self.nc, self.P
        x1_sb = [self.sb(f"x1_sb{j}", [128, 4096], F32, sc) for j in range(2)]
        xn2_tok = self.sb("xn2_tok", [128, 4096], BF16, sc)
        xn2_f = [self.sb(f"xn2_f{j}", [128, 4096], BF16, sc) for j in range(2)]
        rwt = self.sb("rwt_sb", [128, 128], BF16, sc)
        selr_sb = self.sb("selr_sb", [16, 2048], BF16, sc)
        wu = [self.sb(f"wu{j}", [128, 4096], BF16, sc) for j in range(4)]
        wd = [self.sb(f"wd{j}", [128, 4096], BF16, sc) for j in range(4)]
        actg = [self.sb(f"actg{j}", [128, 2048], BF16, sc) for j in range(4)]
        acc = self.sb("acc", [128, 4096], F32, sc)
        outt = self.sb("outt", [128, 4096], F32, sc)
        G_sb = [self.sb(f"G_sb{j}", [128, 512], BF16, sc) for j in range(2)]
        pr_sb = self.sb("pr_sb", [128, 16], F32, sc)
        wk_sb = self.sb("wk_sb", [128, 16], F32, sc)
        mk_sb = self.sb("mk_sb", [128, 16], F32, sc)
        g16 = self.sb("g16", [128, 16], BF16, sc)
        gateT = [self.sb(f"gateT{j}", [16, 512], BF16, sc) for j in range(2)]
        rsum = self.sb("rsum", [128, 4], F32, sc)
        psUp = [self.ps(f"psUp{j}", [128, 512], F32, sc) for j in range(2)]
        psDn = [self.ps(f"psDn{j}", [128, 512], F32, sc) for j in range(2)]
        psR = self.ps("psR", [128, 16], F32, sc)
        psM = self.ps("psM", [128, 512], F32, sc)
        psL4 = self.ps("psL4", [128, 1024], F32, sc)

        P.dma("sync", rwt[:, :], self.rwt_s[:, :], waits=(P.now("d_rwS"),),
              inc="d_c", tag="rwt_ld")
        P.dma("sync", selr_sb[:, :], self.cd["selr"][:, :], inc="d_c",
              tag="c_selr")
        C2 = P.now("d_c")
        FW = [P.now("d_fwst0"), P.now("d_fwst1")]

        x1_free = [Buf(), Buf()]
        xn2t_free = Buf()
        xn2f_free = [Buf(), Buf()]
        wuslot_free = [Buf() for _ in range(4)]
        wdslot_free = [Buf() for _ in range(4)]
        actg_free = [Buf() for _ in range(4)]
        psup_free = [Buf(), Buf()]
        psdn_free = [Buf(), Buf()]
        psr_free = Buf()
        psm_free = Buf()
        psl4_free = Buf()
        g_free = [Buf(), Buf()]
        gateT_free = [Buf(), Buf()]
        acc_free = Buf()
        outt_free = Buf()
        xf_ms = [None, None]
        gatet_ms = [None, None]
        xn2_ms = [None, None]

        passes = []
        k0 = 0
        for n in PASSES:
            passes.append(list(range(NEXP))[k0:k0 + n])
            k0 += n

        def emit_prep_stats(ip):
            bi = ip % 2
            P.dma("sync", x1_sb[bi][:, :], self.x1_s[ip][:, :],
                  waits=tuple(x for x in (("d_x1w", 16 * (ip + 1)),
                                          x1_free[bi].free) if x),
                  inc=f"d_x1r{bi}", tag=f"x1_ld{ip}")
            XL = P.now(f"d_x1r{bi}")
            LN3 = self.ln_stats(lambda u: x1_sb[bi][:, u * 1024:(u + 1) * 1024],
                                XL, f"ln3_{ip}")
            xw = [LN3]
            if xn2t_free.free:
                xw.append(xn2t_free.free)
            for u in range(NSUB):
                P.emit("scalar",
                       (lambda E, o=xn2_tok[:, u * 1024:(u + 1) * 1024],
                        i_=x1_sb[bi][:, u * 1024:(u + 1) * 1024],
                        scv=self.stats[:, 20 + u:21 + u],
                        bv=self.stats[:, 24 + u:25 + u]:
                        E.activation(o, i_, AF.Identity, bias=bv, scale=scv)),
                       waits=tuple(xw) if u == 0 else (), inc="s",
                       tag=f"ln3n_{ip}_{u}")
            XN2 = P.now("s")
            self.stats_free.free = XN2
            xn2_ms[bi] = XN2

        def emit_prep_rest(ip):
            bi = ip % 2
            XN2 = xn2_ms[bi]
            tw = [XN2]
            if xn2f_free[bi].free:
                tw.append(xn2f_free[bi].free)
            for c in range(8):
                k = c % 2
                pw = psUp[k]
                for u in range(NSUB):
                    w = list(tw) if u == 0 else []
                    if u == 0 and psup_free[k].free:
                        w.append(psup_free[k].free)
                    P.emit("tensor",
                           (lambda E, o=pw.bitcast(BF16)[:, u * 128:u * 128 + 128],
                            i_=xn2_tok[:, u * 1024 + c * 128:u * 1024 + c * 128 + 128]:
                            E.transpose(o, i_, self.identb[:, :])),
                           waits=tuple(w), inc="t" if u == 3 else None,
                           tag=f"x2t{ip}_{c}{u}")
                tw = []
                P.emit("vector",
                       (lambda E, d=xn2_f[bi][:, c * 512:(c + 1) * 512],
                        sp=pw.bitcast(BF16)[:, 0:512]: E.tensor_copy(d, sp)),
                       waits=(P.now("t"),), inc="v", tag=f"x2e{ip}_{c}")
                psup_free[k].free = P.now("v")
            XF = P.now("v")
            xn2t_free.free = P.now("t")
            xf_ms[bi] = XF

            for u in range(NSUB):
                for dc in range(8):
                    w = [XF, C2] if dc == 0 else []
                    if dc == 0 and psr_free.free:
                        w.append(psr_free.free)
                    P.emit("tensor",
                           (lambda E, l=xn2_f[bi][:, dc * 512 + u * 128:
                                                  dc * 512 + u * 128 + 128],
                            r=rwt[:, dc * 16:(dc + 1) * 16],
                            s0=(dc == 0), s1=(dc == 7):
                            E.matmul(psR[:, :], l, r, start=s0, stop=s1)),
                           waits=tuple(w), inc="t" if dc == 7 else None,
                           tag=f"rt{ip}_{u}{dc}")
                RT = P.now("t")
                P.emit("vector", lambda E: E.reduce_max(self.nw1[:, 0:1],
                                                        psR[:, :], axis=AX),
                       waits=(RT,), inc="v", tag=f"rm{ip}_{u}")
                P.emit("vector", lambda E: E.tensor_scalar_mul(
                    self.nw1[:, 1:2], self.nw1[:, 0:1], -1.0),
                    waits=(P.now("v"),), inc="v", tag=f"rmn{ip}_{u}")
                NM = P.now("v")
                P.emit("scalar",
                       (lambda E, a=rsum[:, u:u + 1]:
                        E.activation(pr_sb[:, :], psR[:, :], AF.Exp,
                                     bias=self.nw1[:, 1:2], accum_out=a)),
                       waits=(NM,), inc="s", tag=f"re{ip}_{u}")
                psr_free.free = P.now("s")
                P.emit("vector", (lambda E, a=rsum[:, u:u + 1]:
                                  E.reciprocal(self.nw1[:, 2:3], a)),
                       waits=(P.now("s"),), inc="v", tag=f"rrc{ip}_{u}")
                P.emit("vector", lambda E: E.tensor_scalar_mul(
                    pr_sb[:, :], pr_sb[:, :], self.nw1[:, 2:3]),
                    waits=(P.now("v"),), inc="v", tag=f"rn{ip}_{u}")
                P.emit("vector", lambda E: E.tensor_copy(wk_sb[:, :], pr_sb[:, :]),
                       waits=(P.now("v"),), inc="v", tag=f"tk0{ip}_{u}")
                for it in range(3):
                    P.emit("vector", lambda E: E.reduce_max(self.nw1[:, 3:4],
                                                            wk_sb[:, :], axis=AX),
                           waits=(P.now("v"),), inc="v", tag=f"tkm{ip}_{u}{it}")
                    P.emit("vector", lambda E: E.tensor_scalar(
                        mk_sb[:, :], wk_sb[:, :], self.nw1[:, 3:4], -1e9,
                        op0=ALU.is_ge, op1=ALU.mult),
                        waits=(P.now("v"),), inc="v", tag=f"tka{ip}_{u}{it}")
                    P.emit("vector", lambda E: E.tensor_add(
                        wk_sb[:, :], wk_sb[:, :], mk_sb[:, :]),
                        waits=(P.now("v"),), inc="v", tag=f"tkb{ip}_{u}{it}")
                P.emit("vector", lambda E: E.reduce_max(self.nw1[:, 3:4],
                                                        wk_sb[:, :], axis=AX),
                       waits=(P.now("v"),), inc="v", tag=f"tkt{ip}_{u}")
                P.emit("vector", lambda E: E.tensor_scalar(
                    mk_sb[:, :], pr_sb[:, :], self.nw1[:, 3:4], None,
                    op0=ALU.is_ge), waits=(P.now("v"),), inc="v",
                    tag=f"tkc{ip}_{u}")
                P.emit("vector", lambda E: E.tensor_mul(
                    g16[:, :], pr_sb[:, :], mk_sb[:, :]),
                    waits=(P.now("v"),), inc="v", tag=f"tkg{ip}_{u}")
                GT = P.now("v")
                w = [GT]
                if u == 0 and psm_free.free:
                    w.append(psm_free.free)
                if u == 0 and gateT_free[bi].free:
                    w.append(gateT_free[bi].free)
                P.emit("tensor",
                       (lambda E, o=psM.bitcast(BF16)[0:16, u * 256:u * 256 + 128]:
                        E.transpose(o, g16[:, :], self.identb[:, :])),
                       waits=tuple(w), inc="t", tag=f"gt{ip}_{u}")
            for u in range(NSUB):
                P.emit("vector",
                       (lambda E, d=gateT[bi][:, u * 128:(u + 1) * 128],
                        sp=psM.bitcast(BF16)[0:16, u * 256:u * 256 + 128]:
                        E.tensor_copy(d, sp)),
                       waits=(P.now("t"),) if u == 0 else (), inc="v",
                       tag=f"gte{ip}_{u}")
            gatet_ms[bi] = P.now("v")
            psm_free.free = P.now("v")

        emit_prep_stats(0)
        emit_prep_rest(0)
        for i in range(self.nt2):
            bi = i % 2
            XF = xf_ms[bi]
            GATET = gatet_ms[bi]
            xnf = xn2_f[bi]
            gT = gateT[bi]

            pass_first = True
            for pi_, pl in enumerate(passes):
                WU = {}
                WD = {}
                for ei_, e in enumerate(pl):
                    slot = (i * NEXP + sum(PASSES[:pi_]) + ei_) % 4
                    P.dma("sync", wu[slot][:, :], self.utp_s[e][:, :],
                          waits=tuple((FW if (i == 0 and pi_ == 0 and ei_ == 0) else [])
                                      + ([wuslot_free[slot].free] if wuslot_free[slot].free else [])),
                          inc=f"d_fwu{slot}", tag=f"fu_ld{i}_{e}")
                    WU[ei_] = P.now(f"d_fwu{slot}")
                for ei_, e in enumerate(pl):
                    slot = (i * NEXP + sum(PASSES[:pi_]) + ei_) % 4
                    P.dma("sync", wd[slot][:, :], self.dtp_s[e][:, :],
                          waits=tuple([wdslot_free[slot].free]
                                      if wdslot_free[slot].free else []),
                          inc=f"d_fwd{slot}", tag=f"fd_ld{i}_{e}")
                    WD[ei_] = P.now(f"d_fwd{slot}")
                for ei_, e in enumerate(pl):
                    slot = (i * NEXP + sum(PASSES[:pi_]) + ei_) % 4
                    WL = WU[ei_]
                    aslot = slot
                    if e < 16:
                        w = [GATET, C2]
                        if psm_free.free:
                            w.append(psm_free.free)
                        P.emit("tensor",
                               (lambda E, l=selr_sb[:, e * 128:(e + 1) * 128],
                                r=gT[:, :]:
                                E.matmul(psM[:, :], l, r, start=True, stop=True)),
                               waits=tuple(w), inc="t", tag=f"gb{i}_{e}")
                        gslot = e % 2
                        gw = [P.now("t")]
                        if g_free[gslot].free:
                            gw.append(g_free[gslot].free)
                        P.emit("vector", (lambda E, d=G_sb[e % 2][:, :]:
                                          E.tensor_copy(d, psM[:, :])),
                               waits=tuple(gw), inc="v", tag=f"gbe{i}_{e}")
                        psm_free.free = P.now("v")
                        GB = P.now("v")
                    else:
                        GB = None
                    for ec in range(4):
                        k = ec % 2
                        pw = psUp[k]
                        for dc in range(8):
                            w = [WL, XF] if (ec == 0 and dc == 0) else []
                            if dc == 0 and psup_free[k].free:
                                w.append(psup_free[k].free)
                            P.emit("tensor",
                                   (lambda E, o=pw[:, :],
                                    l=wu[slot][:, dc * 512 + ec * 128:
                                               dc * 512 + ec * 128 + 128],
                                    r=xnf[:, dc * 512:(dc + 1) * 512],
                                    s0=(dc == 0), s1=(dc == 7):
                                    E.matmul(o, l, r, start=s0, stop=s1)),
                                   waits=tuple(w), inc="t" if dc == 7 else None,
                                   tag=f"up{i}_{e}_{ec}{dc}")
                        UPM = P.now("t")
                        gw = [UPM]
                        if ec == 0 and actg_free[aslot].free:
                            gw.append(actg_free[aslot].free)
                        P.emit("scalar",
                               (lambda E, o=actg[aslot][:, ec * 512:(ec + 1) * 512],
                                sp=pw[:, :]: E.activation(o, sp, AF.Gelu)),
                               waits=tuple(gw), inc="s", tag=f"gl{i}_{e}_{ec}")
                        psup_free[k].free = P.now("s")
                        if e < 16:
                            P.emit("gpsimd",
                                   (lambda E, o=actg[aslot][:, ec * 512:(ec + 1) * 512],
                                    gs=G_sb[e % 2][:, :]:
                                    E.tensor_mul(o, o, gs)),
                                   waits=(P.now("s"), GB), inc="g",
                                   tag=f"gm{i}_{e}_{ec}")
                    ACT = P.now("g") if e < 16 else P.now("s")
                    if e < 16:
                        g_free[e % 2].free = ACT
                    actg_free[aslot].ready = ACT
                    wuslot_free[slot].free = P.now("t")
                for c in range(8):
                    k = c % 2
                    pd = psDn[k]
                    nel = len(pl)
                    for ei_, e in enumerate(pl):
                        slot = (i * NEXP + sum(PASSES[:pi_]) + ei_) % 4
                        for ec in range(4):
                            first = (ei_ == 0 and ec == 0)
                            last = (ei_ == nel - 1 and ec == 3)
                            w = []
                            if ec == 0 and c == 0:
                                w.append(actg_free[slot].ready)
                                w.append(WD[ei_])
                            if first and psdn_free[k].free:
                                w.append(psdn_free[k].free)
                            P.emit("tensor",
                                   (lambda E, o=pd[:, :],
                                    l=wd[slot][:, ec * 1024 + c * 128:
                                               ec * 1024 + c * 128 + 128],
                                    r=actg[slot][:, ec * 512:(ec + 1) * 512],
                                    s0=first, s1=last:
                                    E.matmul(o, l, r, start=s0, stop=s1)),
                                   waits=tuple(w), inc="t" if last else None,
                                   tag=f"dn{i}_{pi_}_{c}{e}{ec}")
                    DN = P.now("t")
                    aw = [DN]
                    if pass_first and acc_free.free and c == 0:
                        aw.append(acc_free.free)
                    if pass_first:
                        P.emit("vector",
                               (lambda E, d=acc[:, c * 512:(c + 1) * 512],
                                sp=pd[:, :]: E.tensor_copy(d, sp)),
                               waits=tuple(aw), inc="v", tag=f"dne{i}_{pi_}_{c}")
                    else:
                        P.emit("vector",
                               (lambda E, d=acc[:, c * 512:(c + 1) * 512],
                                sp=pd[:, :]: E.tensor_tensor(d, sp, d, op=ALU.add)),
                               waits=tuple(aw), inc="v", tag=f"dna{i}_{pi_}_{c}")
                    psdn_free[k].free = P.now("v")
                if pass_first and i + 1 < self.nt2:
                    emit_prep_stats(i + 1)
                pass_first = False
                for ei_, e in enumerate(pl):
                    slot = (i * NEXP + sum(PASSES[:pi_]) + ei_) % 4
                    wdslot_free[slot].free = P.now("t")
                    actg_free[slot].free = P.now("t")
            ACC = P.now("v")
            xn2f_free[bi].free = P.now("t")
            gateT_free[bi].free = P.now("t")

            if i + 1 < self.nt2:
                emit_prep_rest(i + 1)

            for u in range(NSUB):
                tw = [ACC] if u == 0 else []
                if psl4_free.free:
                    tw.append(psl4_free.free)
                for c in range(8):
                    w = list(tw) if c == 0 else []
                    P.emit("tensor",
                           (lambda E, o=psL4[:, c * 128:(c + 1) * 128],
                            i_=acc[:, c * 512 + u * 128:c * 512 + u * 128 + 128]:
                            E.transpose(o, i_, self.identf[:, :])),
                           waits=tuple(w), inc="t" if c == 7 else None,
                           tag=f"l4t{i}_{u}{c}")
                    tw = []
                L4T = P.now("t")
                ew = [L4T]
                if u == 0 and outt_free.free:
                    ew.append(outt_free.free)
                P.emit("vector",
                       (lambda E, o=outt[:, u * 1024:(u + 1) * 1024]:
                        E.tensor_copy(o, psL4[:, :])),
                       waits=tuple(ew), inc="v", tag=f"l4e{i}_{u}")
                psl4_free.free = P.now("v")
            LN4 = self.ln_stats(lambda u: outt[:, u * 1024:(u + 1) * 1024],
                                P.now("v"), f"ln4_{i}")
            for u in range(NSUB):
                P.emit("vector",
                       (lambda E, o=outt[:, u * 1024:(u + 1) * 1024],
                        m=self.stats[:, 8 + u:9 + u],
                        r=self.stats[:, 20 + u:21 + u]:
                        E.tensor_scalar(o, o, m, r,
                                        op0=ALU.subtract, op1=ALU.mult)),
                       waits=(LN4,) if u == 0 else (), inc="v",
                       tag=f"l4n{i}_{u}")
                P.emit("vector",
                       (lambda E, o=outt[:, u * 1024:(u + 1) * 1024],
                        xs=x1_sb[bi][:, u * 1024:(u + 1) * 1024]:
                        E.tensor_add(o, o, xs)),
                       waits=(P.now("v"),), inc="v", tag=f"l4a{i}_{u}")
            self.stats_free.free = P.now("v")
            OUT = P.now("v")
            x1_free[bi].free = OUT
            acc_free.free = P.now("t")
            P.dma("gpsimd", self.out_d[i * TT:(i + 1) * TT, :].rearrange(
                "(u p) d -> p u d", p=128),
                outt[:, :].rearrange("p (u d) -> p u d", u=4),
                waits=(OUT,), inc="d_o", tag=f"out{i}")
            outt_free.free = P.now("d_o")



_BUILD_CACHE = {}


def finalize_program(b):
    nc, P = b.nc, b.P
    with ExitStack() as st:
        for name in sorted(P.cnt):
            P.sem_handles[name] = st.enter_context(nc.semaphore(name))
        with nc.Block() as block:
            for eng in ENGS:
                ops = P.ops[eng]

                def runner(E, ops=ops):
                    for op in ops:
                        op(E)

                getattr(block, eng)(runner)
    b.finalized = True
    return b


def get_build(taps=(), phases=("W", "1a", "1b", "2")):
    key = (tuple(sorted(taps)), tuple(phases))
    if key not in _BUILD_CACHE:
        b = Build(taps=taps, phases=phases)
        b.build()
        finalize_program(b)
        _BUILD_CACHE[key] = b
    return _BUILD_CACHE[key]


def make_in_maps(inputs, consts):
    perm = _qk_perm()
    wq = np.ascontiguousarray(np.asarray(inputs["L0_attn_W_q"])[perm])
    wk = np.ascontiguousarray(np.asarray(inputs["L0_attn_W_k"])[perm])
    bq = np.asarray(inputs["L0_attn_b_q"])[perm]
    bk = np.asarray(inputs["L0_attn_b_k"])[perm]
    bv = np.asarray(inputs["L0_attn_b_v"])
    bqkv = np.zeros((128, 24), np.float32)
    for pi, b in enumerate((bq, bk, bv)):
        for oc in range(8):
            bqkv[:, pi * 8 + oc] = b[oc * 128:(oc + 1) * 128]
    shared = {
        "wq": wq, "wk": wk,
        "wv": np.asarray(inputs["L0_attn_W_v"]),
        "wo": np.asarray(inputs["L0_attn_W_o"]),
        "bqkv": bqkv,
        "bo_in": np.asarray(inputs["L0_attn_b_o"]).reshape(1, D),
        "rw_in": np.asarray(inputs["L0_ffn_router_w"]),
        "sup": np.asarray(inputs["L0_ffn_shared_up"]),
        "sdn": np.asarray(inputs["L0_ffn_shared_down"]),
        "rup": np.asarray(inputs["L0_ffn_routed_up"]),
        "rdn": np.asarray(inputs["L0_ffn_routed_down"]),
    }
    shared.update(consts)
    x = np.asarray(inputs["L0_attn_x"])
    maps = []
    for c in range(8):
        m = dict(shared)
        m["xin"] = np.ascontiguousarray(x[c])
        maps.append(m)
    return maps


def run_device(b, in_maps, core_ids):
    if not getattr(b, "finalized", False):
        finalize_program(b)
    return run_bass_kernel_spmd(b.nc, in_maps, core_ids=core_ids)


def kernel(**inputs):
    consts = host_constants()
    b = get_build()
    in_maps = make_in_maps(inputs, consts)
    res = run_device(b, in_maps, core_ids=list(range(8)))
    return np.stack([np.asarray(r["out"]) for r in res.results], axis=0)


# version 22
# speedup vs baseline: 1.2344x; 1.0010x over previous
import numpy as np
import ml_dtypes
from contextlib import ExitStack

import concourse.bass as bass
import concourse.mybir as mybir
from concourse.bass_utils import run_bass_kernel_spmd

F32 = mybir.dt.float32
BF16 = mybir.dt.bfloat16
I32 = mybir.dt.int32
AF = mybir.ActivationFunctionType
ALU = mybir.AluOpType
AX = mybir.AxisListType.X
BF = ml_dtypes.bfloat16

D = 1024
T = 4096
H = 16
NT = 8
TT = 512
NSUB = 4
NEXP = 18
PASSES = (4, 4, 4, 4, 2)
ENGS = ("sync", "tensor", "vector", "scalar", "gpsimd")



def _qk_perm():
    p64 = np.concatenate([np.arange(0, 64, 2), np.arange(1, 64, 2)])
    return np.concatenate([h * 64 + p64 for h in range(H)])


def host_constants():
    c = {}
    theta = 1.0 / 10000.0 ** (np.arange(32, dtype=np.float64) / 32.0)
    ang = np.arange(T, dtype=np.float64)[None, :] * theta[:, None]
    cos, sin = np.cos(ang), np.sin(ang)
    C = np.zeros((128, T), np.float64)
    S = np.zeros((128, T), np.float64)
    for blk in (0, 64):
        C[blk:blk + 32] = cos
        C[blk + 32:blk + 64] = cos
        S[blk:blk + 32] = -sin
        S[blk + 32:blk + 64] = sin
    c["crep"] = C.astype(BF)
    c["srep"] = S.astype(BF)
    P = np.zeros((128, 128), np.float32)
    for blk in (0, 64):
        for j in range(32):
            P[blk + j, blk + 32 + j] = 1.0
            P[blk + 32 + j, blk + j] = 1.0
    c["pswap"] = P.astype(BF)
    c["identb"] = np.eye(128, dtype=np.float32).astype(BF)
    c["identf"] = np.eye(128, dtype=np.float32)
    hs = np.zeros((128, 128), np.float32)
    for oc in range(8):
        hs[0:64, oc * 16 + 2 * oc] = 1.0
        hs[64:128, oc * 16 + 2 * oc + 1] = 1.0
    c["hsall"] = hs.astype(BF)
    e4 = np.zeros((128, 64), np.float32)
    for r in range(4):
        e4[:, r * 16 + r] = 1.0
    c["e4"] = e4.astype(BF)
    selp = np.zeros((128, 8 * 128), np.float32)
    for p in range(8):
        for i in range(128):
            h = 2 * p + (1 if i >= 64 else 0)
            selp[32 * (h % 4) + h // 4, p * 128 + i] = 1.0
    c["selp"] = selp
    m = np.zeros((128, 4 * 512), np.float32)
    for b in range(4):
        s = np.arange(128)[:, None] + 128 * b
        t = np.arange(512)[None, :]
        m[:, b * 512:(b + 1) * 512] = (s <= t)
    c["mask4"] = m.astype(BF)
    selr = np.zeros((16, 16 * 128), np.float32)
    for r in range(16):
        selr[r, r * 128:(r + 1) * 128] = 1.0
    c["selr"] = selr.astype(BF)
    c["ones1"] = np.ones((1, 128), np.float32)
    return c



class Buf:
    __slots__ = ("free", "ready")

    def __init__(self):
        self.free = None
        self.ready = None


class Planner:
    def __init__(self):
        self.ops = {e: [] for e in ENGS}
        self.cnt = {}
        self.sem_handles = {}
        self._gate = {e: None for e in ENGS}

    def sem(self, name):
        self.cnt.setdefault(name, 0)
        return name

    def now(self, sem):
        return (sem, self.cnt.get(sem, 0))

    def gate_all(self):
        snap = tuple((s, v) for s, v in self.cnt.items() if v > 0)
        for e in ENGS:
            self._gate[e] = snap

    def emit(self, eng, build, waits=(), inc=None, inc_amt=1, tag=""):
        waits = [w for w in waits if w is not None]
        for s, _ in waits:
            self.sem(s)
        if self._gate[eng]:
            waits = list(self._gate[eng]) + list(waits)
            self._gate[eng] = None
        val = None
        if inc is not None:
            self.sem(inc)
            self.cnt[inc] += inc_amt
            val = self.cnt[inc]
        waits = tuple((s, v) for (s, v) in waits if v > 0)

        def closure(E, waits=waits, build=build, inc=inc, inc_amt=inc_amt):
            for s, v in waits:
                E.wait_ge(self.sem_handles[s], v)
            ins = build(E)
            if inc is not None:
                if ins is None:
                    ins = E.nop()
                ins.then_inc(self.sem_handles[inc], inc_amt)
            return ins

        closure.tag = tag
        closure.waits = waits
        closure.inc = (inc, val, inc_amt) if inc else None
        self.ops[eng].append(closure)
        return self.now(inc) if inc else None

    def dma(self, eng, out_ap, in_ap, waits=(), inc=None, tag="", **kw):
        return self.emit(eng, lambda E: E.dma_start(out_ap, in_ap, **kw),
                         waits=waits, inc=inc, inc_amt=16, tag=tag)

    def simulate(self):
        idx = {e: 0 for e in ENGS}
        sem = {s: 0 for s in self.cnt}
        progress = True
        while progress:
            progress = False
            for e in ENGS:
                while idx[e] < len(self.ops[e]):
                    op = self.ops[e][idx[e]]
                    if all(sem[s] >= v for s, v in op.waits):
                        if op.inc:
                            sem[op.inc[0]] += op.inc[2]
                        idx[e] += 1
                        progress = True
                    else:
                        break
        stuck = {e: idx[e] for e in ENGS if idx[e] < len(self.ops[e])}
        if stuck:
            lines = []
            for e, i in stuck.items():
                op = self.ops[e][i]
                miss = [(s, v, sem.get(s, 0)) for s, v in op.waits
                        if sem.get(s, 0) < v]
                lines.append(f"  {e}[{i}/{len(self.ops[e])}] tag={op.tag} missing={miss}")
            raise RuntimeError("planned-program deadlock:\n" + "\n".join(lines))

    def n_ops(self):
        return {e: len(self.ops[e]) for e in ENGS}



class Build:
    def __init__(self, taps=(), phases=("W", "1a", "1b", "2"), nt2=NT,
                 p2parts=("router", "experts", "ln4")):
        self.taps = set(taps)
        self.phases = phases
        self.nt2 = nt2
        self.p2parts = set(p2parts)
        self.nc = bass.Bass("TRN2", target_bir_lowering=False)
        self.P = Planner()
        self.st = ExitStack()

    def sb(self, name, shape, dtype, scope=None):
        return (scope or self.st).enter_context(
            self.nc.sbuf_tensor(name, list(shape), dtype))

    def ps(self, name, shape, dtype=F32, scope=None):
        return (scope or self.st).enter_context(
            self.nc.psum_tensor(name, list(shape), dtype))

    def dram(self, name, shape, dtype, kind="ExternalInput"):
        return self.nc.dram_tensor(name, list(shape), dtype, kind=kind)

    def scratch(self, name, shape, dtype):
        kind = "ExternalOutput" if name in self.taps else "Internal"
        return self.nc.dram_tensor(name, list(shape), dtype, kind=kind)

    def newton_rsqrt(self, dst, src, pre=(), tag="nwt"):
        P = self.P
        p = src.shape[0]
        n = src.shape[1]
        t1 = self.nw1[0:p, 0:n]
        t2 = self.nw2[0:p, 0:n]
        th = self.nw3[0:p, 0:n]
        P.emit("vector", lambda E: E.memset(t1.bitcast(I32), 0x5F3759DF),
               waits=pre, inc="v", tag=tag + "_m")
        P.emit("vector", lambda E: E.tensor_scalar(
            t2.bitcast(I32), src.bitcast(I32), 1, None,
            op0=ALU.logical_shift_right), waits=(P.now("v"),), inc="v",
            tag=tag + "_sh")
        P.emit("vector", lambda E: E.tensor_tensor(
            dst.bitcast(I32), t1.bitcast(I32), t2.bitcast(I32),
            op=ALU.subtract), waits=(P.now("v"),), inc="v", tag=tag + "_y0")
        P.emit("vector", lambda E: E.tensor_scalar_mul(th, src, 0.5),
               waits=(P.now("v"),), inc="v", tag=tag + "_h")
        for it in range(3):
            P.emit("vector", lambda E: E.tensor_mul(t2, dst, dst),
                   waits=(P.now("v"),), inc="v", tag=tag + f"_a{it}")
            P.emit("vector", lambda E: E.tensor_mul(t2, th, t2),
                   waits=(P.now("v"),), inc="v", tag=tag + f"_b{it}")
            P.emit("vector", lambda E: E.tensor_scalar(
                t2, t2, -1.0, 1.5, op0=ALU.mult, op1=ALU.add),
                waits=(P.now("v"),), inc="v", tag=tag + f"_c{it}")
            P.emit("vector", lambda E: E.tensor_mul(dst, dst, t2),
                   waits=(P.now("v"),), inc="v", tag=tag + f"_d{it}")
        return P.now("v")

    def ln_stats(self, src_fn, xwait, tag, n=NSUB, extra_s2_wait=(),
                 serialize_ps=False):
        P = self.P
        stats = self.stats
        sw = [xwait] + list(extra_s2_wait)
        if self.stats_free.free:
            sw.append(self.stats_free.free)
        for u in range(n):
            P.emit("vector", (lambda E, o=stats[:, u:u + 1], i_=src_fn(u):
                              E.reduce_sum(o, i_, axis=AX)),
                   waits=tuple(sw) if u == 0 else (), inc="v",
                   tag=f"{tag}_s1_{u}")
            s2w = [P.now("s")] if u > 0 else list(sw)
            if serialize_ps:
                s2w.append(P.now("v"))
            P.emit("scalar", (lambda E, i_=src_fn(u), a=stats[:, 4 + u:5 + u]:
                              E.activation(self.sqdump[:, 0:i_.shape[1]], i_,
                                           AF.Square, accum_out=a)),
                   waits=tuple(s2w), inc="s",
                   tag=f"{tag}_s2_{u}")
        S1, S2 = P.now("v"), P.now("s")
        P.emit("vector", lambda E: E.tensor_scalar_mul(
            stats[:, 8:8 + n], stats[:, 0:n], 1.0 / 1024),
            waits=(S1, S2), inc="v", tag=f"{tag}_mu")
        P.emit("vector", lambda E: E.tensor_mul(
            stats[:, 16:16 + n], stats[:, 8:8 + n], stats[:, 8:8 + n]),
            waits=(P.now("v"),), inc="v", tag=f"{tag}_mu2")
        P.emit("vector", lambda E: E.tensor_scalar_mul(
            stats[:, 12:12 + n], stats[:, 4:4 + n], 1.0 / 1024),
            waits=(P.now("v"),), inc="v", tag=f"{tag}_va")
        P.emit("vector", lambda E: E.tensor_tensor(
            stats[:, 12:12 + n], stats[:, 12:12 + n], stats[:, 16:16 + n],
            op=ALU.subtract), waits=(P.now("v"),), inc="v", tag=f"{tag}_vb")
        P.emit("vector", lambda E: E.tensor_scalar_add(
            stats[:, 12:12 + n], stats[:, 12:12 + n], 1e-5),
            waits=(P.now("v"),), inc="v", tag=f"{tag}_vc")
        self.newton_rsqrt(stats[:, 20:20 + n], stats[:, 12:12 + n],
                          pre=(P.now("v"),), tag=f"{tag}_nw")
        P.emit("vector", lambda E: E.tensor_mul(
            stats[:, 24:24 + n], stats[:, 8:8 + n], stats[:, 20:20 + n]),
            waits=(P.now("v"),), inc="v", tag=f"{tag}_nm")
        P.emit("vector", lambda E: E.tensor_scalar_mul(
            stats[:, 24:24 + n], stats[:, 24:24 + n], -1.0),
            waits=(P.now("v"),), inc="v", tag=f"{tag}_nm2")
        return P.now("v")

    def build(self):
        nc, P = self.nc, self.P
        self.xin = self.dram("xin", [T, D], F32)
        self.wsrc = {n: self.dram("w" + n, [D, D], F32) for n in "qkvo"}
        self.bqkv_d = self.dram("bqkv", [128, 24], F32)
        self.bo_d = self.dram("bo_in", [1, D], F32)
        self.rw_d = self.dram("rw_in", [16, D], F32)
        self.sup_d = self.dram("sup", [2, 512, D], F32)
        self.sdn_d = self.dram("sdn", [2, D, 512], F32)
        self.rup_d = self.dram("rup", [16, 512, D], F32)
        self.rdn_d = self.dram("rdn", [16, D, 512], F32)
        cd = {}
        for n, sh, dt in (("crep", [128, T], BF16), ("srep", [128, T], BF16),
                          ("pswap", [128, 128], BF16), ("identb", [128, 128], BF16),
                          ("identf", [128, 128], F32), ("hsall", [128, 128], BF16),
                          ("e4", [128, 64], BF16), ("selp", [128, 1024], F32),
                          ("mask4", [128, 2048], BF16), ("selr", [16, 2048], BF16),
                          ("ones1", [1, 128], F32)):
            cd[n] = self.dram(n, sh, dt)
        self.cd = cd
        self.out_d = self.dram("out", [T, D], F32, kind="ExternalOutput")
        self.qf_s = self.scratch("qf_s", [NT, 128, 4096], BF16)
        self.kf_s = self.scratch("kf_s", [NT, 128, 4096], BF16)
        self.vt_s = self.scratch("vt_s", [NT, 128, 4096], BF16)
        self.x1_s = self.scratch("x1_s", [NT, 128, 4096], F32)
        self.wot_s = self.scratch("wot_s", [128, 8192], BF16)
        self.rwt_s = self.scratch("rwt_s", [128, 128], BF16)
        self.utp_s = self.scratch("utp_s", [NEXP, 128, 4096], BF16)
        self.dtp_s = self.scratch("dtp_s", [NEXP, 128, 4096], BF16)

        self.crep = [self.sb(f"crep_sb{j}", [128, 512], BF16) for j in range(2)]
        self.srep = [self.sb(f"srep_sb{j}", [128, 512], BF16) for j in range(2)]
        self.pswap = self.sb("pswap_sb", [128, 128], BF16)
        self.identb = self.sb("identb_sb", [128, 128], BF16)
        self.identf = self.sb("identf_sb", [128, 128], F32)
        self.hsall = self.sb("hsall_sb", [128, 128], BF16)
        self.ones1 = self.sb("ones1_sb", [1, 128], F32)
        self.bqkv = self.sb("bqkv_sb", [128, 24], F32)
        self.bo_sb = self.sb("bo_sb", [1, 1024], F32)
        self.runmax = self.sb("runmax_sb", [16, 16], F32)
        self.bias_ch = self.sb("bias_ch", [128, 16], F32)
        self.bias_pr = self.sb("bias_pr", [128, 8], F32)
        self.bo_rep = self.sb("bo_rep", [128, 1024], F32)
        self.stats = self.sb("stats_sb", [128, 32], F32)
        self.nw1 = self.sb("nw1", [128, 8], F32)
        self.nw2 = self.sb("nw2", [128, 8], F32)
        self.nw3 = self.sb("nw3", [128, 8], F32)
        self.sqdump = self.sb("sqdump", [128, 1024], BF16)
        self.stats_free = Buf()

        for n, t in (("pswap", self.pswap), ("identb", self.identb),
                     ("identf", self.identf), ("hsall", self.hsall),
                     ("ones1", self.ones1)):
            P.dma("sync", t[:, :], cd[n][:, :], inc="d_c", tag="c_" + n)
        P.dma("sync", self.bqkv[:, :], self.bqkv_d[:, :], inc="d_c", tag="c_bqkv")
        P.dma("sync", self.bo_sb[:, :], self.bo_d[:, :], inc="d_c", tag="c_bo")
        self.CONSTS = P.now("d_c")

        with ExitStack() as s1a:
            self.phase_w_attn(s1a)
            if "1a" in self.phases:
                self.phase_1a(s1a)
                self.phase_15()
        if "1b" in self.phases:
            P.gate_all()
            with ExitStack() as s1b:
                self.phase_1b(s1b)
        if "2" in self.phases:
            P.gate_all()
            with ExitStack() as s2:
                self.phase_2(s2)
        fin = [P.now("d_o")] if P.cnt.get("d_o") else []
        for s in ("d_spq", "d_spk", "d_spv", "d_x1w", "d_fwst0", "d_fwst1",
                  "d_woS", "d_rwS", "d_dbg"):
            if P.cnt.get(s):
                fin.append(P.now(s))
        P.emit("sync", lambda E: E.nop(), waits=tuple(fin), tag="final")
        P.simulate()
        return self

    def phase_w_attn(self, sc):
        nc, P = self.nc, self.P
        self.wstage = self.sb("wstage", [128, 8192], BF16, sc)
        self.wtrg = {n: self.sb(f"w{n}t_sb", [128, 8192], BF16, sc)
                     for n in "qkv"}
        self.fin = [self.sb(f"fw_in{j}", [128, 4096], BF16, sc) for j in range(2)]
        self.fout = [self.sb("fw_out0", [128, 4096], BF16, sc)] * 2
        self.fin_free = [Buf(), Buf()]
        _fb = Buf()
        self.fout_free = [_fb, _fb]
        self.psW = self.ps("psW", [128, 512], BF16, sc)
        self.psW2 = self.ps("psW2", [128, 512], BF16, sc)
        self.psw_free = [Buf(), Buf()]
        wstage_free = Buf()
        self.attnw_ready = {}
        for wn in "qkvo":
            dst = self.wtrg[wn] if wn != "o" else self.fout[0]
            P.dma("gpsimd", self.wstage[:, :].rearrange("p (c d) -> p c d", c=8),
                  self.wsrc[wn][:, :].rearrange("(c p) d -> p c d", p=128),
                  waits=(wstage_free.free,) if wstage_free.free else (),
                  inc="d_wA", tag=f"w{wn}_ld")
            LD = P.now("d_wA")
            for dc in range(8):
                for half in range(2):
                    k = (dc * 2 + half) % 2
                    pw = (self.psW, self.psW2)[k]
                    pb = self.psw_free[k]
                    for oq in range(4):
                        oc = half * 4 + oq
                        w = [LD, self.CONSTS] if oq == 0 else []
                        if oq == 0 and pb.free:
                            w.append(pb.free)
                        if (wn == "o" and oq == 0 and half == 0 and dc in (0, 4)
                                and self.fout_free[0].free):
                            w.append(self.fout_free[0].free)
                        P.emit("tensor",
                               (lambda E, o=pw[:, oq * 128:(oq + 1) * 128],
                                i_=self.wstage[:, oc * 1024 + dc * 128:
                                               oc * 1024 + dc * 128 + 128]:
                                E.transpose(o, i_, self.identb[:, :])),
                               waits=tuple(w), inc="t" if oq == 3 else None,
                               tag=f"w{wn}_t{dc}{half}")
                    dcol = (dc * 1024 + half * 512) if wn != "o" else \
                        ((dc % 4) * 1024 + half * 512)
                    P.emit("vector",
                           (lambda E, d=dst[:, dcol:dcol + 512],
                            sp=pw[:, :]: E.tensor_copy(d, sp)),
                           waits=(P.now("t"),), inc="v", tag=f"w{wn}_e{dc}{half}")
                    pb.free = P.now("v")
                if wn == "o" and dc in (3, 7):
                    g = dc // 4
                    P.dma("sync", self.wot_s[:, g * 4096:(g + 1) * 4096],
                          self.fout[0][:, :], waits=(P.now("v"),),
                          inc="d_woS", tag=f"wo_st{g}")
                    self.fout_free[0].free = P.now("d_woS")
            self.attnw_ready[wn] = P.now("v")
            wstage_free.free = P.now("v")

        rw_stage = self.sb("rw_stage", [16, 1024], BF16, sc)
        rwt_stage = self.sb("rwt_stage", [128, 128], BF16, sc)
        P.dma("gpsimd", rw_stage[:, :], self.rw_d[:, :],
              waits=(wstage_free.free,), inc="d_wA", tag="wrw_ld")
        RWL = P.now("d_wA")
        for g in range(2):
            pw = (self.psW, self.psW2)[g]
            pb = self.psw_free[g]
            for q in range(4):
                dc = g * 4 + q
                w = [RWL, self.CONSTS] if q == 0 else []
                if q == 0 and pb.free:
                    w.append(pb.free)
                P.emit("tensor",
                       (lambda E, o=pw[:, q * 16:(q + 1) * 16],
                        i_=rw_stage[:, dc * 128:(dc + 1) * 128]:
                        E.transpose(o, i_, self.identb[0:16, 0:16])),
                       waits=tuple(w), inc="t" if q == 3 else None,
                       tag=f"wrw_t{dc}")
            P.emit("vector", (lambda E, d=rwt_stage[:, g * 64:(g + 1) * 64],
                              sp=pw[:, 0:64]: E.tensor_copy(d, sp)),
                   waits=(P.now("t"),), inc="v", tag=f"wrw_e{g}")
            pb.free = P.now("v")
        P.dma("sync", self.rwt_s[:, :], rwt_stage[:, :], waits=(P.now("v"),),
              inc="d_rwS", tag="wrw_st")

        mats = []
        for ei in range(16):
            mats.append((self.rup_d[ei].rearrange("(c p) d -> p c d", p=128),
                         4, self.utp_s[ei], f"fu{ei}"))
            mats.append((self.rdn_d[ei].rearrange("(c p) e -> p c e", p=128),
                         8, self.dtp_s[ei], f"fd{ei}"))
        for si in range(2):
            mats.append((self.sup_d[si].rearrange("(c p) d -> p c d", p=128),
                         4, self.utp_s[16 + si], f"fu{16 + si}"))
            mats.append((self.sdn_d[si].rearrange("(c p) e -> p c e", p=128),
                         8, self.dtp_s[16 + si], f"fd{16 + si}"))
        self.ffn_mats = mats
        self.ffn_load_i = 0
        self.ffn_comp_i = 0
        self.ffn_ld_ms = {}

    def _ffn_load(self, mi):
        P = self.P
        src, nch, dst, tag = self.ffn_mats[mi]
        bi = mi % 2
        P.dma("gpsimd",
              self.fin[bi][:, :].rearrange("p (c d) -> p c d", c=nch), src,
              waits=(self.fin_free[bi].free,) if self.fin_free[bi].free else (),
              inc=f"d_f{bi}", tag=tag + "_ld")
        self.ffn_ld_ms[mi] = P.now(f"d_f{bi}")

    def _ffn_comp(self, mi):
        P = self.P
        src, nch, dst, tag = self.ffn_mats[mi]
        bi = mi % 2
        LD = self.ffn_ld_ms[mi]
        fr = 4096 // nch
        nblk = fr // 128
        ow = self.fout[bi]
        first = True
        for b in range(nblk):
            ngr = max(1, nch // 4)
            for g in range(ngr):
                k = (b * ngr + g) % 2
                pw = (self.psW, self.psW2)[k]
                pb = self.psw_free[k]
                span = min(4, nch)
                for q in range(span):
                    c_in = g * 4 + q
                    w = []
                    if q == 0:
                        w = [LD, self.CONSTS]
                        if pb.free:
                            w.append(pb.free)
                        if first and self.fout_free[bi].free:
                            w.append(self.fout_free[bi].free)
                        first = False
                    P.emit("tensor",
                           (lambda E, o=pw[:, q * 128:(q + 1) * 128],
                            i_=self.fin[bi][:, c_in * fr + b * 128:
                                            c_in * fr + b * 128 + 128]:
                            E.transpose(o, i_, self.identb[:, :])),
                           waits=tuple(w), inc="t" if q == span - 1 else None,
                           tag=tag + f"_t{b}{g}")
                P.emit("vector",
                       (lambda E, d=ow[:, b * (nch * 128) + g * 512:
                                       b * (nch * 128) + g * 512 + span * 128],
                        sp=pw[:, 0:span * 128]: E.tensor_copy(d, sp)),
                       waits=(P.now("t"),), inc="v", tag=tag + f"_e{b}{g}")
                pb.free = P.now("v")
        self.fin_free[bi].free = P.now("v")
        P.dma("sync", dst[:, :], ow[:, :], waits=(P.now("v"),),
              inc=f"d_fwst{bi}", tag=tag + "_st")
        self.fout_free[bi].free = P.now(f"d_fwst{bi}")

    def emit_ffn_prep(self, nmat, flush=False):
        n = len(self.ffn_mats)
        target = min(self.ffn_load_i + nmat, n)
        while (self.ffn_load_i < target
               or (flush and self.ffn_comp_i < n)):
            if (self.ffn_load_i < target
                    and self.ffn_load_i - self.ffn_comp_i < 2):
                self._ffn_load(self.ffn_load_i)
                self.ffn_load_i += 1
            elif self.ffn_comp_i < self.ffn_load_i:
                self._ffn_comp(self.ffn_comp_i)
                self.ffn_comp_i += 1
            else:
                break

    def phase_1a(self, sc):
        nc, P = self.nc, self.P
        x_sb = [self.sb(f"x_sb{j}", [128, 4096], F32, sc) for j in range(2)]
        xn_tok = self.sb("xn_tok", [128, 4096], BF16, sc)
        xn_f = self.sb("xn_f", [128, 4096], BF16, sc)
        proj = [self.sb(f"proj{j}", [128, 4096], BF16, sc) for j in range(3)]
        qrot = self.sb("qrot", [128, 4096], BF16, sc)
        krot = self.sb("krot", [128, 4096], BF16, sc)
        vtok = self.sb("vtok", [128, 4096], BF16, sc)
        sqn = self.sb("sqn", [128, 1024], BF16, sc)
        ttmp = self.sb("ttmp", [128, 512], F32, sc)
        psQ = self.ps("psQ", [128, 512], F32, sc)
        psQ2 = self.ps("psQ2", [128, 512], F32, sc)
        psSw = self.ps("psSw", [128, 512], F32, sc)
        psSw2 = self.ps("psSw2", [128, 512], F32, sc)
        psNorm = self.ps("psNorm", [16, 512], F32, sc)
        self.psNorm = psNorm
        self.psQ_ref = psQ

        x_free = [Buf(), Buf()]
        xnf_free = Buf()
        xntok_free = Buf()
        proj_free = [Buf(), Buf(), Buf()]
        rot_free = {"q": Buf(), "k": Buf(), "v": Buf()}
        psq_free = [Buf(), Buf()]
        self.psq_free_ref = psq_free
        pssw_free = [Buf(), Buf()]
        self.psnorm_free = Buf()
        cs_free = [Buf(), Buf()]
        sqn_free = [Buf(), Buf()]
        xn_ms = [None, None]
        csl_ms = [None, None]

        xl_ms = [None, None]

        def emit_x_load(ii):
            bi = ii % 2
            xb = x_sb[bi]
            P.dma("sync", xb[:, :].rearrange("p (u d) -> p u d", u=4),
                  self.xin[ii * TT:(ii + 1) * TT, :].rearrange(
                      "(u p) d -> p u d", p=128),
                  waits=(x_free[bi].free,) if x_free[bi].free else (),
                  inc=f"d_x{bi}", tag=f"x_ld{ii}")
            xl_ms[bi] = P.now(f"d_x{bi}")
            P.dma("sync", self.crep[bi][:, :],
                  self.cd["crep"][:, ii * TT:(ii + 1) * TT],
                  waits=(cs_free[bi].free,) if cs_free[bi].free else (),
                  inc=f"d_cs{bi}", tag=f"cs_c{ii}")
            P.dma("sync", self.srep[bi][:, :],
                  self.cd["srep"][:, ii * TT:(ii + 1) * TT],
                  inc=f"d_cs{bi}", tag=f"cs_s{ii}")
            csl_ms[bi] = P.now(f"d_cs{bi}")

        def emit_ln1(ii):
            bi = ii % 2
            xb = x_sb[bi]
            XL = xl_ms[bi]
            LN1 = self.ln_stats(lambda u: xb[:, u * 1024:(u + 1) * 1024],
                                XL, f"ln1_{ii}", extra_s2_wait=(self.CONSTS,))
            xw = [LN1]
            if xntok_free.free:
                xw.append(xntok_free.free)
            for u in range(NSUB):
                P.emit("scalar",
                       (lambda E, o=xn_tok[:, u * 1024:(u + 1) * 1024],
                        i_=xb[:, u * 1024:(u + 1) * 1024],
                        scv=self.stats[:, 20 + u:21 + u],
                        bv=self.stats[:, 24 + u:25 + u]:
                        E.activation(o, i_, AF.Identity, bias=bv, scale=scv)),
                       waits=tuple(xw) if u == 0 else (), inc="s",
                       tag=f"ln1n_{ii}_{u}")
            XN = P.now("s")
            self.stats_free.free = XN
            x_free[bi].free = XN
            xn_ms[bi] = XN

        emit_x_load(0)
        emit_ln1(0)
        for i in range(NT):
            bi = i % 2
            XN = xn_ms[bi]
            CSL = csl_ms[i % 2]

            tw = [XN]
            if xnf_free.free:
                tw.append(xnf_free.free)
            for c in range(8):
                k = c % 2
                pw = (psQ, psQ2)[k]
                pb = psq_free[k]
                for u in range(NSUB):
                    w = list(tw) if (c == 0 and u == 0) else []
                    if u == 0 and pb.free:
                        w.append(pb.free)
                    P.emit("tensor",
                           (lambda E, o=pw.bitcast(BF16)[:, u * 128:(u + 1) * 128],
                            i_=xn_tok[:, u * 1024 + c * 128:u * 1024 + c * 128 + 128]:
                            E.transpose(o, i_, self.identb[:, :])),
                           waits=tuple(w), inc="t" if u == 3 else None,
                           tag=f"xnt{i}_{c}{u}")
                P.emit("vector",
                       (lambda E, d=xn_f[:, c * 512:(c + 1) * 512],
                        sp=pw.bitcast(BF16)[:, 0:512]: E.tensor_copy(d, sp)),
                       waits=(P.now("t"),), inc="v", tag=f"xne{i}_{c}")
                pb.free = P.now("v")
            XNF = P.now("v")
            xntok_free.free = P.now("t")
            if i + 1 < NT:
                emit_x_load(i + 1)

            for pi, pn in enumerate("qkv"):
                wt = self.wtrg[pn]
                bcol = pi * 8
                projb = proj[pi]
                pj_w = [XNF, self.attnw_ready[pn]]
                if proj_free[pi].free:
                    pj_w.append(proj_free[pi].free)
                if pn in "qk":
                    rotb = qrot if pn == "q" else krot
                    rfree = rot_free[pn]
                    EV = {}
                    RP = {}
                    for step in range(10):
                        if step < 8:
                            oc = step
                            k = oc % 2
                            pw = (psQ, psQ2)[k]
                            pb = psq_free[k]
                            for dc in range(8):
                                w = list(pj_w) if (oc == 0 and dc == 0) else []
                                if dc == 0 and pb.free:
                                    w.append(pb.free)
                                P.emit("tensor",
                                       (lambda E, o=pw[:, :],
                                        l=wt[:, dc * 1024 + oc * 128:
                                             dc * 1024 + oc * 128 + 128],
                                        r=xn_f[:, dc * 512:(dc + 1) * 512],
                                        s0=(dc == 0), s1=(dc == 7):
                                        E.matmul(o, l, r, start=s0, stop=s1)),
                                       waits=tuple(w),
                                       inc="t" if dc == 7 else None,
                                       tag=f"{pn}mm{i}_{oc}{dc}")
                            pj_w = []
                            P.emit("scalar",
                                   (lambda E, o=projb[:, oc * 512:(oc + 1) * 512],
                                    sp=pw[:, :],
                                    bv=self.bqkv[:, bcol + oc:bcol + oc + 1]:
                                    E.activation(o, sp, AF.Identity, bias=bv)),
                                   waits=(P.now("t"),), inc="s",
                                   tag=f"{pn}ev{i}_{oc}")
                            pb.free = P.now("s")
                            EV[oc] = P.now("s")
                        if 1 <= step <= 8:
                            oc = step - 1
                            k = oc % 2
                            pw = (psSw, psSw2)[k]
                            pb = pssw_free[k]
                            w = [EV[oc], self.CONSTS]
                            if pb.free:
                                w.append(pb.free)
                            P.emit("tensor",
                                   (lambda E, o=pw[:, :],
                                    r=projb[:, oc * 512:(oc + 1) * 512]:
                                    E.matmul(o, self.pswap[:, :], r,
                                             start=True, stop=True)),
                                   waits=tuple(w), inc="t", tag=f"{pn}sw{i}_{oc}")
                            SW = P.now("t")
                            w2 = [SW, CSL]
                            if oc == 0 and rfree.free:
                                w2.append(rfree.free)
                            P.emit("vector",
                                   (lambda E, o=rotb[:, oc * 512:(oc + 1) * 512],
                                    a=projb[:, oc * 512:(oc + 1) * 512],
                                    cr=self.crep[i % 2][:, :]:
                                    E.tensor_mul(o, a, cr)),
                                   waits=tuple(w2), inc="v", tag=f"{pn}r1{i}_{oc}")
                            P.emit("vector",
                                   (lambda E, sp=pw[:, :],
                                    sr=self.srep[i % 2][:, :]:
                                    E.tensor_mul(ttmp[:, :], sp, sr)),
                                   waits=(P.now("v"),), inc="v",
                                   tag=f"{pn}r2{i}_{oc}")
                            pb.free = P.now("v")
                            P.emit("vector",
                                   (lambda E, o=rotb[:, oc * 512:(oc + 1) * 512]:
                                    E.tensor_add(o, o, ttmp[:, :])),
                                   waits=(P.now("v"),), inc="v",
                                   tag=f"{pn}r3{i}_{oc}")
                            RP[oc] = P.now("v")
                        if step >= 2:
                            oc = step - 2
                            k = oc % 2
                            w = [RP[oc]]
                            if oc == 0 and self.psnorm_free.free:
                                w.append(self.psnorm_free.free)
                            if sqn_free[k].free:
                                w.append(sqn_free[k].free)
                            P.emit("scalar",
                                   (lambda E, o=sqn[:, k * 512:(k + 1) * 512],
                                    i_=rotb[:, oc * 512:(oc + 1) * 512]:
                                    E.activation(o, i_, AF.Square)),
                                   waits=tuple(w), inc="s",
                                   tag=f"{pn}nsq{i}_{oc}")
                            P.emit("tensor",
                                   (lambda E, l=self.hsall[:, oc * 16:(oc + 1) * 16],
                                    r=sqn[:, k * 512:(k + 1) * 512],
                                    s0=(oc == 0), s1=(oc == 7):
                                    E.matmul(psNorm[:, :], l, r, start=s0, stop=s1,
                                             skip_group_check=True)),
                                   waits=(P.now("s"),), inc="t",
                                   tag=f"{pn}nmm{i}_{oc}")
                            sqn_free[k].free = P.now("t")
                    ROT = RP[7]
                    col = i if pn == "q" else 8 + i
                    P.emit("vector",
                           (lambda E, o=self.runmax[:, col:col + 1]:
                            E.reduce_max(o, psNorm[:, :], axis=AX)),
                           waits=(P.now("t"),), inc="v", tag=f"{pn}nrm{i}")
                    self.psnorm_free.free = P.now("v")
                    spsem = "d_spq" if pn == "q" else "d_spk"
                    P.dma("sync", (self.qf_s if pn == "q" else self.kf_s)[i][:, :],
                          rotb[:, :], waits=(ROT,), inc=spsem, tag=f"{pn}sp{i}")
                    rfree.free = P.now(spsem)
                    proj_free[pi].free = ROT
                    if pn == "k" and i + 1 < NT:
                        emit_ln1(i + 1)
                else:
                    for oc in range(8):
                        k = oc % 2
                        pw = (psQ, psQ2)[k]
                        pb = psq_free[k]
                        for dc in range(8):
                            w = list(pj_w) if (oc == 0 and dc == 0) else []
                            if dc == 0 and pb.free:
                                w.append(pb.free)
                            P.emit("tensor",
                                   (lambda E, o=pw[:, :],
                                    l=wt[:, dc * 1024 + oc * 128:
                                         dc * 1024 + oc * 128 + 128],
                                    r=xn_f[:, dc * 512:(dc + 1) * 512],
                                    s0=(dc == 0), s1=(dc == 7):
                                    E.matmul(o, l, r, start=s0, stop=s1)),
                                   waits=tuple(w), inc="t" if dc == 7 else None,
                                   tag=f"{pn}mm{i}_{oc}{dc}")
                        pj_w = []
                        P.emit("scalar",
                               (lambda E, o=projb[:, oc * 512:(oc + 1) * 512],
                                sp=pw[:, :],
                                bv=self.bqkv[:, bcol + oc:bcol + oc + 1]:
                                E.activation(o, sp, AF.Identity, bias=bv)),
                               waits=(P.now("t"),), inc="s", tag=f"{pn}ev{i}_{oc}")
                        pb.free = P.now("s")
                    PRJ = P.now("s")
                    vw = [PRJ]
                    if rot_free["v"].free:
                        vw.append(rot_free["v"].free)
                    for u in range(NSUB):
                        for half in range(2):
                            k = half
                            pw = (psSw, psSw2)[k]
                            pb = pssw_free[k]
                            for oq in range(4):
                                oc = half * 4 + oq
                                w = list(vw) if (u == 0 and half == 0 and oq == 0) else []
                                if oq == 0 and pb.free:
                                    w.append(pb.free)
                                P.emit("tensor",
                                       (lambda E,
                                        o=pw.bitcast(BF16)[:, oq * 128:(oq + 1) * 128],
                                        i_=projb[:, oc * 512 + u * 128:
                                                 oc * 512 + u * 128 + 128]:
                                        E.transpose(o, i_, self.identb[:, :])),
                                       waits=tuple(w),
                                       inc="t" if oq == 3 else None,
                                       tag=f"vt{i}_{u}{half}{oq}")
                            P.emit("vector",
                                   (lambda E, d=vtok[:, u * 1024 + half * 512:
                                                     u * 1024 + half * 512 + 512],
                                    sp=pw.bitcast(BF16)[:, 0:512]:
                                    E.tensor_copy(d, sp)),
                                   waits=(P.now("t"),), inc="v",
                                   tag=f"vte{i}_{u}{half}")
                            pb.free = P.now("v")
                        vw = []
                    VT = P.now("v")
                    P.dma("sync", self.vt_s[i][:, :], vtok[:, :], waits=(VT,),
                          inc="d_spv", tag=f"vsp{i}")
                    rot_free["v"].free = P.now("d_spv")
                    proj_free[pi].free = VT
            xnf_free.free = P.now("t")
            cs_free[bi].free = P.now("v")
            self.emit_ffn_prep(9 if i >= 4 else 4)
        self.emit_ffn_prep(len(self.ffn_mats), flush=True)
        self.x_free_1a = x_free

    def phase_15(self):
        P = self.P
        nw1, nw2, nw3 = self.nw1, self.nw2, self.nw3
        psNorm = self.psNorm
        P.emit("vector", lambda E: E.reduce_max(nw1[0:16, 0:1],
                                                self.runmax[:, 0:8], axis=AX),
               waits=(P.now("v"), P.now("t"), P.now("s")), inc="v", tag="ch_mq")
        P.emit("vector", lambda E: E.reduce_max(nw1[0:16, 1:2],
                                                self.runmax[:, 8:16], axis=AX),
               waits=(P.now("v"),), inc="v", tag="ch_mk")
        P.emit("vector", lambda E: E.tensor_mul(nw1[0:16, 2:3], nw1[0:16, 0:1],
                                                nw1[0:16, 1:2]),
               waits=(P.now("v"),), inc="v", tag="ch_p")
        P.emit("vector", lambda E: E.tensor_scalar_max(nw1[0:16, 2:3],
                                                       nw1[0:16, 2:3], 1e-12),
               waits=(P.now("v"),), inc="v", tag="ch_cl")
        self.newton_rsqrt(self.stats[0:16, 28:29], nw1[0:16, 2:3],
                          pre=(P.now("v"), self.stats_free.free), tag="ch_nw")
        P.emit("vector", lambda E: E.tensor_mul(nw1[0:16, 3:4], nw1[0:16, 2:3],
                                                self.stats[0:16, 28:29]),
               waits=(P.now("v"),), inc="v", tag="ch_s")
        P.emit("vector", lambda E: E.tensor_scalar_mul(nw1[0:16, 3:4],
                                                       nw1[0:16, 3:4], -0.125),
               waits=(P.now("v"),), inc="v", tag="ch_n")
        CH = P.now("v")
        P.emit("tensor", lambda E: E.transpose(psNorm[0:1, 0:16],
                                               nw1[0:16, 3:4],
                                               self.identf[0:16, 0:16]),
               waits=(CH, self.psnorm_free.free), inc="t", tag="ch_t")
        P.emit("vector", lambda E: E.tensor_copy(self.stats[0:1, 0:16],
                                                 psNorm[0:1, 0:16]),
               waits=(P.now("t"), self.stats_free.free), inc="v", tag="ch_e")
        psQ = self.psQ_ref
        pqf = self.psq_free_ref[0]
        P.emit("tensor", lambda E: E.matmul(psQ[:, 0:16], self.ones1[:, :],
                                            self.stats[0:1, 0:16],
                                            start=True, stop=True),
               waits=(P.now("v"), pqf.free), inc="t", tag="ch_b")
        P.emit("vector", lambda E: E.tensor_copy(self.bias_ch[:, :],
                                                 psQ[:, 0:16]),
               waits=(P.now("t"),), inc="v", tag="ch_be")
        P.emit("vector", lambda E: E.tensor_scalar_max(
            self.bias_ch[:, :], self.bias_ch[:, :], -30.0),
            waits=(P.now("v"),), inc="v", tag="ch_cap")
        for p_ in range(8):
            P.emit("vector",
                   (lambda E, o=self.bias_pr[:, p_:p_ + 1],
                    a=self.bias_ch[:, 2 * p_:2 * p_ + 1],
                    b2=self.bias_ch[:, 2 * p_ + 1:2 * p_ + 2]:
                    E.tensor_tensor(o, a, b2, op=ALU.min)),
                   waits=(P.now("v"),), inc="v", tag=f"ch_pr{p_}")
        self.psnorm_free.free = P.now("v")
        self.stats_free.free = P.now("v")
        if "dbg_bias" in self.taps:
            dbg_b = self.scratch("dbg_bias", [128, 16], F32)
            dbg_r = self.scratch("dbg_rmax", [16, 16], F32)
            P.dma("sync", dbg_b[:, :], self.bias_ch[:, :],
                  waits=(P.now("v"),), inc="d_dbg", tag="dbg_bias")
            P.dma("sync", dbg_r[:, :], self.runmax[:, :],
                  waits=(P.now("v"),), inc="d_dbg", tag="dbg_rmax")

    def phase_1b(self, sc):
        nc, P = self.nc, self.P
        wot = self.sb("wot_sb", [128, 8192], BF16, sc)
        q_sb = [self.sb(f"q_sb{j}", [128, 4096], BF16, sc) for j in range(2)]
        k_sb = [self.sb(f"k_sb{j}", [128, 4096], BF16, sc) for j in range(2)]
        v_sb = [self.sb(f"v_sb{j}", [128, 4096], BF16, sc) for j in range(2)]
        probs = self.sb("probs", [128, 4096], BF16, sc)
        attn = self.sb("attn", [128, 4096], BF16, sc)
        av32 = self.sb("av32", [128, 512], F32, sc)
        xr_sb = [self.sb(f"xr_sb{j}", [128, 4096], F32, sc) for j in range(2)]
        out1 = self.sb("out1", [128, 4096], F32, sc)
        x1t = self.sb("x1t", [128, 4096], F32, sc)
        selp_sb = self.sb("selp_sb", [128, 1024], F32, sc)
        mask_sb = self.sb("mask_sb", [128, 2048], BF16, sc)
        e4_sb = self.sb("e4_sb", [128, 64], BF16, sc)
        Ssb = self.sb("Ssb", [128, 512], F32, sc)
        recS = self.sb("recS", [128, 512], F32, sc)
        psA = self.ps("psA", [128, 2048], F32, sc)
        psAV = [self.ps(f"psAV{j}", [128, 512], F32, sc) for j in range(2)]
        psWo = self.ps("psWo", [128, 512], F32, sc)
        psWo2 = self.ps("psWo2", [128, 512], F32, sc)
        psSums = psWo
        zero128 = self.sb("zero128", [128, 128], BF16, sc)
        P.emit("vector", lambda E: E.memset(zero128[:, :], 0.0), inc="v",
               tag="zero_init")
        ZERO = P.now("v")

        if "dbg_attn" in self.taps:
            self.dbg_attn_s = self.scratch("dbg_attn", [NT, 128, 4096], BF16)
            self.dbg_s_s = self.scratch("dbg_s", [NT, 128, 512], F32)
        P.dma("sync", wot[:, :], self.wot_s[:, :],
              waits=(P.now("d_woS"),), inc="d_wo2", tag="wot_ld")
        WOT = P.now("d_wo2")
        P.dma("sync", selp_sb[:, :], self.cd["selp"][:, :], inc="d_c", tag="c_selp")
        P.dma("sync", mask_sb[:, :], self.cd["mask4"][:, :], inc="d_c", tag="c_mask")
        P.dma("sync", e4_sb[:, :], self.cd["e4"][:, :], inc="d_c", tag="c_e4")
        C1B = P.now("d_c")

        for hf in range(2):
            P.emit("tensor", (lambda E, r=self.bo_sb[:, hf * 512:(hf + 1) * 512]:
                              E.matmul(psWo[:, :], self.ones1[:, :], r,
                                       start=True, stop=True)),
                   waits=(self.CONSTS,) if hf == 0 else (P.now("v"),),
                   inc="t", tag=f"bo_b{hf}")
            P.emit("vector", (lambda E, d=self.bo_rep[:, hf * 512:(hf + 1) * 512]:
                              E.tensor_copy(d, psWo[:, :])),
                   waits=(P.now("t"),), inc="v", tag=f"bo_e{hf}")
        BOREP = P.now("v")
        pswo_free_init = BOREP

        SPILLS = [("d_spq", 16 * 8), ("d_spk", 16 * 8), ("d_spv", 16 * 8)]
        q_free = [Buf(), Buf()]
        k_free = [Buf(), Buf()]
        v_free = [Buf(), Buf()]
        xr_free = [Buf(), Buf()]
        probs_free = {g: Buf() for g in range(2)}
        ps_free = Buf()
        psav_free = [Buf(), Buf()]
        pswo_free = Buf()
        pswo_free.free = pswo_free_init
        pswo2_free = Buf()
        attn_free = Buf()
        out1_free = Buf()
        x1t_free = Buf()

        attn32 = self.sb("attn32", [128, 4096], F32, sc)
        attn32_free = Buf()
        pending_spill = []

        def flush_spill():
            while pending_spill:
                si, sX1 = pending_spill.pop(0)
                P.dma("sync", self.x1_s[si][:, :], x1t[:, :], waits=(sX1,),
                      inc="d_x1w", tag=f"x1sp{si}")
                x1t_free.free = P.now("d_x1w")
        ql_ms = [None, None]

        def emit_q_load(ii):
            qb2 = ii % 2
            P.dma("sync", q_sb[qb2][:, :], self.qf_s[ii][:, :],
                  waits=tuple((SPILLS if ii == 0 else [])
                              + ([q_free[qb2].free] if q_free[qb2].free else [])),
                  inc=f"d_q{qb2}", tag=f"q_ld{ii}")
            ql_ms[qb2] = P.now(f"d_q{qb2}")

        emit_q_load(0)
        for i in range(NT):
            qb = i % 2
            QL = ql_ms[qb]
            XR = None

            pend = []

            def emit_avsums(d):
                g, j, p = d["g"], d["j"], d["p"]
                for bb in range(2):
                    b = 2 * g + bb
                    first = (b == 0)
                    last = (b == 3)
                    for hh in range(2):
                        h = 2 * p + hh
                        w = [d["PRD"], d["VL"]] if (bb == 0 and hh == 0) else []
                        if first and bb == 0 and hh == 0 and d["avf"].free:
                            w.append(d["avf"].free)
                        P.emit("tensor",
                               (lambda E,
                                o=d["av"][64 * hh:64 * hh + 64, :],
                                l=v_sb[d["sl"]][:, b * 1024 + h * 64:
                                                b * 1024 + h * 64 + 64],
                                r=probs[:, d["pcol"] + hh * 1024 + bb * 512:
                                        d["pcol"] + hh * 1024 + bb * 512 + 512],
                                s0=first, s1=last, tp=(0, 64 * hh):
                                E.matmul(o, l, r, start=s0, stop=s1,
                                         tile_position=tp,
                                         skip_group_check=True)),
                               waits=tuple(w), inc="t",
                               tag=f"av{i}_{j}_{p}{g}{hh}{bb}")
                    for hh in range(2):
                        h = 2 * p + hh
                        P.emit("tensor",
                               (lambda E,
                                o=psSums[32 * (h % 4):32 * (h % 4) + 16, :],
                                l=e4_sb[:, (h // 4) * 16:(h // 4) * 16 + 16],
                                r=probs[:, d["pcol"] + hh * 1024 + bb * 512:
                                        d["pcol"] + hh * 1024 + bb * 512 + 512],
                                s1=d["slast"] and bb == 1 and g == 1,
                                tp=(0, 32 * (h % 4)):
                                E.matmul(o, l, r, start=False, stop=s1,
                                         tile_position=tp,
                                         skip_group_check=True)),
                               waits=(), inc="t",
                               tag=f"su{i}_{j}_{p}{g}{hh}{bb}")
                probs_free[d["pslot"]].free = P.now("t")
                if g == 1:
                    AVP = P.now("t")
                    aw = [AVP]
                    if d["j"] == 0 and attn32_free.free and p == 0:
                        aw.append(attn32_free.free)
                    if d["j"] == 0:
                        P.emit("vector",
                               (lambda E, dst=attn32[:, p * 512:(p + 1) * 512],
                                sp=d["av"][:, :]: E.tensor_copy(dst, sp)),
                               waits=tuple(aw), inc="v", tag=f"ac{i}_{j}_{p}")
                    else:
                        P.emit("vector",
                               (lambda E, dst=attn32[:, p * 512:(p + 1) * 512],
                                sp=d["av"][:, :]:
                                E.tensor_tensor(dst, sp, dst, op=ALU.add)),
                               waits=tuple(aw), inc="v", tag=f"ac{i}_{j}_{p}")
                    d["avf"].free = P.now("v")
                    if d["vlast"]:
                        v_free[d["sl"]].free = P.now("t")

            zw = [ZERO, C1B]
            if pswo_free.free:
                zw.append(pswo_free.free)
            P.emit("tensor",
                   (lambda E: E.matmul(psSums[:, :], zero128[:, :],
                                       mask_sb[:, 0:512], start=True,
                                       stop=False, skip_group_check=True)),
                   waits=tuple(zw), inc="t", tag=f"zs{i}")

            for j in range(i + 1):
                sl = j % 2
                P.dma("sync", k_sb[sl][:, :], self.kf_s[j][:, :],
                      waits=tuple((SPILLS if i == 0 else [])
                                  + ([k_free[sl].free] if k_free[sl].free else [])),
                      inc=f"d_k{sl}", tag=f"k_ld{i}_{j}")
                KL = P.now(f"d_k{sl}")
                P.dma("gpsimd", v_sb[sl][:, :], self.vt_s[j][:, :],
                      waits=(v_free[sl].free,) if v_free[sl].free else (),
                      inc=f"d_v{sl}", tag=f"v_ld{i}_{j}")
                VL = P.now(f"d_v{sl}")
                if j == min(1, i):
                    flush_spill()
                    P.dma("sync", xr_sb[qb][:, :].rearrange("p (u d) -> p u d", u=4),
                          self.xin[i * TT:(i + 1) * TT, :].rearrange(
                              "(u p) d -> p u d", p=128),
                          waits=(xr_free[qb].free,) if xr_free[qb].free else (),
                          inc=f"d_xr{qb}", tag=f"xr_ld{i}")
                    XR = P.now(f"d_xr{qb}")
                    if i + 1 < NT:
                        emit_q_load(i + 1)
                diag = (j == i)
                first_of_j = True
                for p in range(8):
                    av = psAV[p % 2]
                    avf = psav_free[p % 2]
                    for g in range(2):
                        for bb in range(2):
                            b = 2 * g + bb
                            for hh in range(2):
                                w = []
                                if first_of_j:
                                    w = [KL, QL]
                                    first_of_j = False
                                if bb == 0 and hh == 0 and ps_free.free:
                                    w.append(ps_free.free)
                                P.emit("tensor",
                                       (lambda E,
                                        o=psA[:, hh * 1024 + bb * 512:
                                              hh * 1024 + bb * 512 + 512],
                                        l=k_sb[sl][64 * hh:64 * hh + 64,
                                                   p * 512 + b * 128:
                                                   p * 512 + b * 128 + 128],
                                        r=q_sb[qb][64 * hh:64 * hh + 64,
                                                   p * 512:(p + 1) * 512]:
                                        E.matmul(o, l, r, start=True, stop=True)),
                                       waits=tuple(w), inc="t",
                                       tag=f"sc{i}_{j}_{p}{g}{hh}{bb}")
                        SCM = P.now("t")
                        pcol = g * 2048
                        ew = [SCM]
                        if probs_free[g].free:
                            ew.append(probs_free[g].free)
                        if i == 0 and j == 0 and p == 0:
                            ew.append(P.now("v"))
                        P.emit("scalar",
                               (lambda E, o=probs[:, pcol:pcol + 2048],
                                bv=self.bias_pr[:, p:p + 1]:
                                E.activation(o, psA[:, :], AF.Exp, bias=bv,
                                             scale=0.125)),
                               waits=tuple(ew), inc="s",
                               tag=f"ex{i}_{j}_{p}{g}")
                        EX = P.now("s")
                        ps_free.free = EX
                        if diag:
                            for hh in range(2):
                                for bb in range(2):
                                    b = 2 * g + bb
                                    P.emit("gpsimd",
                                           (lambda E,
                                            o=probs[:, pcol + hh * 1024 + bb * 512:
                                                    pcol + hh * 1024 + bb * 512 + 512],
                                            mk=mask_sb[:, b * 512:(b + 1) * 512]:
                                            E.tensor_mul(o, o, mk)),
                                           waits=(EX, C1B) if (hh == 0 and bb == 0) else (),
                                           inc="g", tag=f"mk{i}_{p}{g}{hh}{bb}")
                            PRD = P.now("g")
                        else:
                            PRD = EX
                        pend.append(dict(p=p, g=g, pcol=pcol, PRD=PRD, VL=VL,
                                         sl=sl, j=j, av=av, avf=avf, pslot=g,
                                         slast=(j == i), vlast=(p == 7)))
                        if len(pend) > 1:
                            emit_avsums(pend.pop(0))
                k_free[sl].free = P.now("t")
            while pend:
                emit_avsums(pend.pop(0))
            AVD = P.now("t")
            AVV = P.now("v")
            P.emit("vector", lambda E: E.tensor_scalar_max(
                Ssb[:, :], psSums[:, :], 1e-30),
                waits=(AVD,), inc="v", tag=f"S{i}")
            P.emit("vector", lambda E: E.reciprocal(recS[:, :], Ssb[:, :]),
                   waits=(P.now("v"),), inc="v", tag=f"Sr{i}")
            SREC = P.now("v")
            for p in range(8):
                w = [SREC]
                if pswo_free.free:
                    w.append(pswo_free.free)
                P.emit("tensor",
                       (lambda E, l=selp_sb[:, p * 128:(p + 1) * 128]:
                        E.matmul(psWo[:, :], l, recS[:, :],
                                 start=True, stop=True)),
                       waits=tuple(w), inc="t", tag=f"bc{i}_{p}")
                BC = P.now("t")
                aw = [BC, AVV]
                if attn_free.free and p == 0:
                    aw.append(attn_free.free)
                P.emit("vector",
                       (lambda E, o=attn[:, p * 512:(p + 1) * 512],
                        a32=attn32[:, p * 512:(p + 1) * 512]:
                        E.tensor_mul(o, a32, psWo[:, :])),
                       waits=tuple(aw), inc="v", tag=f"an{i}_{p}")
                pswo_free.free = P.now("v")
            attn32_free.free = P.now("v")
            ATTN = P.now("v")
            q_free[qb].free = P.now("t")

            if "dbg_attn" in self.taps:
                P.dma("sync", self.dbg_attn_s[i][:, :], attn[:, :],
                      waits=(ATTN,), inc="d_dbg", tag=f"dbga{i}")
                P.dma("sync", self.dbg_s_s[i][:, :], Ssb[:, :],
                      waits=(ATTN,), inc="d_dbg", tag=f"dbgs{i}")

            ww = [ATTN, WOT]
            if out1_free.free:
                ww.append(out1_free.free)
            for u in range(NSUB):
                for oh in range(2):
                    k2 = (u * 2 + oh) % 2
                    pwo = (psWo, psWo2)[k2]
                    pwf = (pswo_free, pswo2_free)[k2]
                    for dc in range(8):
                        w = list(ww) if (u == 0 and oh == 0 and dc == 0) else []
                        if dc == 0 and pwf.free:
                            w.append(pwf.free)
                        P.emit("tensor",
                               (lambda E, o=pwo[:, :],
                                l=attn[:, dc * 512 + u * 128:
                                        dc * 512 + u * 128 + 128],
                                r=wot[:, dc * 1024 + oh * 512:
                                      dc * 1024 + oh * 512 + 512],
                                s0=(dc == 0), s1=(dc == 7):
                                E.matmul(o, l, r, start=s0, stop=s1)),
                               waits=tuple(w), inc="t" if dc == 7 else None,
                               tag=f"wo{i}_{u}{oh}{dc}")
                    ww = []
                    P.emit("vector",
                           (lambda E, o=out1[:, u * 1024 + oh * 512:
                                             u * 1024 + oh * 512 + 512],
                            sp=pwo[:, :],
                            bo=self.bo_rep[:, oh * 512:(oh + 1) * 512]:
                            E.tensor_add(o, sp, bo)),
                           waits=(P.now("t"), BOREP), inc="v",
                           tag=f"woe{i}_{u}{oh}")
                    pwf.free = P.now("v")
            O1 = P.now("v")
            attn_free.free = P.now("t")

            LN2 = self.ln_stats(lambda u: out1[:, u * 1024:(u + 1) * 1024],
                                O1, f"ln2_{i}")
            xw = [LN2, XR]
            if x1t_free.free:
                xw.append(x1t_free.free)
            for u in range(NSUB):
                P.emit("vector",
                       (lambda E, o=x1t[:, u * 1024:(u + 1) * 1024],
                        i_=out1[:, u * 1024:(u + 1) * 1024],
                        m=self.stats[:, 8 + u:9 + u],
                        r=self.stats[:, 20 + u:21 + u]:
                        E.tensor_scalar(o, i_, m, r, op0=ALU.subtract,
                                        op1=ALU.mult)),
                       waits=tuple(xw) if u == 0 else (), inc="v",
                       tag=f"ln2n_{i}_{u}")
                P.emit("vector",
                       (lambda E, o=x1t[:, u * 1024:(u + 1) * 1024],
                        xs=xr_sb[qb][:, u * 1024:(u + 1) * 1024]:
                        E.tensor_add(o, o, xs)),
                       waits=(P.now("v"),), inc="v", tag=f"ln2a_{i}_{u}")
            X1 = P.now("v")
            self.stats_free.free = X1
            out1_free.free = X1
            xr_free[qb].free = X1
            pending_spill.append((i, X1))
        flush_spill()

    def phase_2(self, sc):
        nc, P = self.nc, self.P
        x1_sb = [self.sb(f"x1_sb{j}", [128, 4096], F32, sc) for j in range(2)]
        xn2_tok = self.sb("xn2_tok", [128, 4096], BF16, sc)
        xn2_f = [self.sb(f"xn2_f{j}", [128, 4096], BF16, sc) for j in range(2)]
        rwt = self.sb("rwt_sb", [128, 128], BF16, sc)
        selr_sb = self.sb("selr_sb", [16, 2048], BF16, sc)
        wu = [self.sb(f"wu{j}", [128, 4096], BF16, sc) for j in range(4)]
        wd = [self.sb(f"wd{j}", [128, 4096], BF16, sc) for j in range(4)]
        actg = [self.sb(f"actg{j}", [128, 2048], BF16, sc) for j in range(4)]
        acc = self.sb("acc", [128, 4096], F32, sc)
        outt = self.sb("outt", [128, 4096], F32, sc)
        G_sb = [self.sb(f"G_sb{j}", [128, 512], BF16, sc) for j in range(2)]
        pr_sb = self.sb("pr_sb", [128, 16], F32, sc)
        wk_sb = self.sb("wk_sb", [128, 16], F32, sc)
        mk_sb = self.sb("mk_sb", [128, 16], F32, sc)
        g16 = self.sb("g16", [128, 16], BF16, sc)
        gateT = [self.sb(f"gateT{j}", [16, 512], BF16, sc) for j in range(2)]
        rsum = self.sb("rsum", [128, 4], F32, sc)
        psUp = [self.ps(f"psUp{j}", [128, 512], F32, sc) for j in range(2)]
        psDn = [self.ps(f"psDn{j}", [128, 512], F32, sc) for j in range(2)]
        psR = self.ps("psR", [128, 16], F32, sc)
        psM = self.ps("psM", [128, 512], F32, sc)
        psL4 = self.ps("psL4", [128, 1024], F32, sc)

        P.dma("sync", rwt[:, :], self.rwt_s[:, :], waits=(P.now("d_rwS"),),
              inc="d_c", tag="rwt_ld")
        P.dma("sync", selr_sb[:, :], self.cd["selr"][:, :], inc="d_c",
              tag="c_selr")
        C2 = P.now("d_c")
        FW = [P.now("d_fwst0"), P.now("d_fwst1")]

        x1_free = [Buf(), Buf()]
        xn2t_free = Buf()
        xn2f_free = [Buf(), Buf()]
        wuslot_free = [Buf() for _ in range(4)]
        wdslot_free = [Buf() for _ in range(4)]
        actg_free = [Buf() for _ in range(4)]
        psup_free = [Buf(), Buf()]
        psdn_free = [Buf(), Buf()]
        psr_free = Buf()
        psm_free = Buf()
        psl4_free = Buf()
        g_free = [Buf(), Buf()]
        gateT_free = [Buf(), Buf()]
        acc_free = Buf()
        outt_free = Buf()
        xf_ms = [None, None]
        gatet_ms = [None, None]
        xn2_ms = [None, None]

        passes = []
        k0 = 0
        for n in PASSES:
            passes.append(list(range(NEXP))[k0:k0 + n])
            k0 += n

        def emit_prep_stats(ip):
            bi = ip % 2
            P.dma("sync", x1_sb[bi][:, :], self.x1_s[ip][:, :],
                  waits=tuple(x for x in (("d_x1w", 16 * (ip + 1)),
                                          x1_free[bi].free) if x),
                  inc=f"d_x1r{bi}", tag=f"x1_ld{ip}")
            XL = P.now(f"d_x1r{bi}")
            LN3 = self.ln_stats(lambda u: x1_sb[bi][:, u * 1024:(u + 1) * 1024],
                                XL, f"ln3_{ip}")
            xw = [LN3]
            if xn2t_free.free:
                xw.append(xn2t_free.free)
            for u in range(NSUB):
                P.emit("scalar",
                       (lambda E, o=xn2_tok[:, u * 1024:(u + 1) * 1024],
                        i_=x1_sb[bi][:, u * 1024:(u + 1) * 1024],
                        scv=self.stats[:, 20 + u:21 + u],
                        bv=self.stats[:, 24 + u:25 + u]:
                        E.activation(o, i_, AF.Identity, bias=bv, scale=scv)),
                       waits=tuple(xw) if u == 0 else (), inc="s",
                       tag=f"ln3n_{ip}_{u}")
            XN2 = P.now("s")
            self.stats_free.free = XN2
            xn2_ms[bi] = XN2

        def emit_prep_rest(ip):
            bi = ip % 2
            XN2 = xn2_ms[bi]
            tw = [XN2]
            if xn2f_free[bi].free:
                tw.append(xn2f_free[bi].free)
            for c in range(8):
                k = c % 2
                pw = psUp[k]
                for u in range(NSUB):
                    w = list(tw) if u == 0 else []
                    if u == 0 and psup_free[k].free:
                        w.append(psup_free[k].free)
                    P.emit("tensor",
                           (lambda E, o=pw.bitcast(BF16)[:, u * 128:u * 128 + 128],
                            i_=xn2_tok[:, u * 1024 + c * 128:u * 1024 + c * 128 + 128]:
                            E.transpose(o, i_, self.identb[:, :])),
                           waits=tuple(w), inc="t" if u == 3 else None,
                           tag=f"x2t{ip}_{c}{u}")
                tw = []
                P.emit("vector",
                       (lambda E, d=xn2_f[bi][:, c * 512:(c + 1) * 512],
                        sp=pw.bitcast(BF16)[:, 0:512]: E.tensor_copy(d, sp)),
                       waits=(P.now("t"),), inc="v", tag=f"x2e{ip}_{c}")
                psup_free[k].free = P.now("v")
            XF = P.now("v")
            xn2t_free.free = P.now("t")
            xf_ms[bi] = XF

            for u in range(NSUB):
                for dc in range(8):
                    w = [XF, C2] if dc == 0 else []
                    if dc == 0 and psr_free.free:
                        w.append(psr_free.free)
                    P.emit("tensor",
                           (lambda E, l=xn2_f[bi][:, dc * 512 + u * 128:
                                                  dc * 512 + u * 128 + 128],
                            r=rwt[:, dc * 16:(dc + 1) * 16],
                            s0=(dc == 0), s1=(dc == 7):
                            E.matmul(psR[:, :], l, r, start=s0, stop=s1)),
                           waits=tuple(w), inc="t" if dc == 7 else None,
                           tag=f"rt{ip}_{u}{dc}")
                RT = P.now("t")
                P.emit("vector", lambda E: E.reduce_max(self.nw1[:, 0:1],
                                                        psR[:, :], axis=AX),
                       waits=(RT,), inc="v", tag=f"rm{ip}_{u}")
                P.emit("vector", lambda E: E.tensor_scalar_mul(
                    self.nw1[:, 1:2], self.nw1[:, 0:1], -1.0),
                    waits=(P.now("v"),), inc="v", tag=f"rmn{ip}_{u}")
                NM = P.now("v")
                P.emit("scalar",
                       (lambda E, a=rsum[:, u:u + 1]:
                        E.activation(pr_sb[:, :], psR[:, :], AF.Exp,
                                     bias=self.nw1[:, 1:2], accum_out=a)),
                       waits=(NM,), inc="s", tag=f"re{ip}_{u}")
                psr_free.free = P.now("s")
                P.emit("vector", (lambda E, a=rsum[:, u:u + 1]:
                                  E.reciprocal(self.nw1[:, 2:3], a)),
                       waits=(P.now("s"),), inc="v", tag=f"rrc{ip}_{u}")
                P.emit("vector", lambda E: E.tensor_scalar_mul(
                    pr_sb[:, :], pr_sb[:, :], self.nw1[:, 2:3]),
                    waits=(P.now("v"),), inc="v", tag=f"rn{ip}_{u}")
                P.emit("vector", lambda E: E.tensor_copy(wk_sb[:, :], pr_sb[:, :]),
                       waits=(P.now("v"),), inc="v", tag=f"tk0{ip}_{u}")
                for it in range(3):
                    P.emit("vector", lambda E: E.reduce_max(self.nw1[:, 3:4],
                                                            wk_sb[:, :], axis=AX),
                           waits=(P.now("v"),), inc="v", tag=f"tkm{ip}_{u}{it}")
                    P.emit("vector", lambda E: E.tensor_scalar(
                        mk_sb[:, :], wk_sb[:, :], self.nw1[:, 3:4], -1e9,
                        op0=ALU.is_ge, op1=ALU.mult),
                        waits=(P.now("v"),), inc="v", tag=f"tka{ip}_{u}{it}")
                    P.emit("vector", lambda E: E.tensor_add(
                        wk_sb[:, :], wk_sb[:, :], mk_sb[:, :]),
                        waits=(P.now("v"),), inc="v", tag=f"tkb{ip}_{u}{it}")
                P.emit("vector", lambda E: E.reduce_max(self.nw1[:, 3:4],
                                                        wk_sb[:, :], axis=AX),
                       waits=(P.now("v"),), inc="v", tag=f"tkt{ip}_{u}")
                P.emit("vector", lambda E: E.tensor_scalar(
                    mk_sb[:, :], pr_sb[:, :], self.nw1[:, 3:4], None,
                    op0=ALU.is_ge), waits=(P.now("v"),), inc="v",
                    tag=f"tkc{ip}_{u}")
                P.emit("vector", lambda E: E.tensor_mul(
                    g16[:, :], pr_sb[:, :], mk_sb[:, :]),
                    waits=(P.now("v"),), inc="v", tag=f"tkg{ip}_{u}")
                GT = P.now("v")
                w = [GT]
                if u == 0 and psm_free.free:
                    w.append(psm_free.free)
                if u == 0 and gateT_free[bi].free:
                    w.append(gateT_free[bi].free)
                P.emit("tensor",
                       (lambda E, o=psM.bitcast(BF16)[0:16, u * 256:u * 256 + 128]:
                        E.transpose(o, g16[:, :], self.identb[:, :])),
                       waits=tuple(w), inc="t", tag=f"gt{ip}_{u}")
            for u in range(NSUB):
                P.emit("vector",
                       (lambda E, d=gateT[bi][:, u * 128:(u + 1) * 128],
                        sp=psM.bitcast(BF16)[0:16, u * 256:u * 256 + 128]:
                        E.tensor_copy(d, sp)),
                       waits=(P.now("t"),) if u == 0 else (), inc="v",
                       tag=f"gte{ip}_{u}")
            gatet_ms[bi] = P.now("v")
            psm_free.free = P.now("v")

        emit_prep_stats(0)
        emit_prep_rest(0)
        for i in range(self.nt2):
            bi = i % 2
            XF = xf_ms[bi]
            GATET = gatet_ms[bi]
            xnf = xn2_f[bi]
            gT = gateT[bi]

            pass_first = True
            for pi_, pl in enumerate(passes):
                WU = {}
                WD = {}
                for ei_, e in enumerate(pl):
                    slot = (i * NEXP + sum(PASSES[:pi_]) + ei_) % 4
                    P.dma("sync", wu[slot][:, :], self.utp_s[e][:, :],
                          waits=tuple((FW if (i == 0 and pi_ == 0 and ei_ == 0) else [])
                                      + ([wuslot_free[slot].free] if wuslot_free[slot].free else [])),
                          inc=f"d_fwu{slot}", tag=f"fu_ld{i}_{e}")
                    WU[ei_] = P.now(f"d_fwu{slot}")
                for ei_, e in enumerate(pl):
                    slot = (i * NEXP + sum(PASSES[:pi_]) + ei_) % 4
                    P.dma("sync", wd[slot][:, :], self.dtp_s[e][:, :],
                          waits=tuple([wdslot_free[slot].free]
                                      if wdslot_free[slot].free else []),
                          inc=f"d_fwd{slot}", tag=f"fd_ld{i}_{e}")
                    WD[ei_] = P.now(f"d_fwd{slot}")
                for ei_, e in enumerate(pl):
                    slot = (i * NEXP + sum(PASSES[:pi_]) + ei_) % 4
                    WL = WU[ei_]
                    aslot = slot
                    if e < 16:
                        w = [GATET, C2]
                        if psm_free.free:
                            w.append(psm_free.free)
                        P.emit("tensor",
                               (lambda E, l=selr_sb[:, e * 128:(e + 1) * 128],
                                r=gT[:, :]:
                                E.matmul(psM[:, :], l, r, start=True, stop=True)),
                               waits=tuple(w), inc="t", tag=f"gb{i}_{e}")
                        gslot = e % 2
                        gw = [P.now("t")]
                        if g_free[gslot].free:
                            gw.append(g_free[gslot].free)
                        P.emit("vector", (lambda E, d=G_sb[e % 2][:, :]:
                                          E.tensor_copy(d, psM[:, :])),
                               waits=tuple(gw), inc="v", tag=f"gbe{i}_{e}")
                        psm_free.free = P.now("v")
                        GB = P.now("v")
                    else:
                        GB = None
                    for ec in range(4):
                        k = ec % 2
                        pw = psUp[k]
                        for dc in range(8):
                            w = [WL, XF] if (ec == 0 and dc == 0) else []
                            if dc == 0 and psup_free[k].free:
                                w.append(psup_free[k].free)
                            P.emit("tensor",
                                   (lambda E, o=pw[:, :],
                                    l=wu[slot][:, dc * 512 + ec * 128:
                                               dc * 512 + ec * 128 + 128],
                                    r=xnf[:, dc * 512:(dc + 1) * 512],
                                    s0=(dc == 0), s1=(dc == 7):
                                    E.matmul(o, l, r, start=s0, stop=s1)),
                                   waits=tuple(w), inc="t" if dc == 7 else None,
                                   tag=f"up{i}_{e}_{ec}{dc}")
                        UPM = P.now("t")
                        gw = [UPM]
                        if ec == 0 and actg_free[aslot].free:
                            gw.append(actg_free[aslot].free)
                        P.emit("scalar",
                               (lambda E, o=actg[aslot][:, ec * 512:(ec + 1) * 512],
                                sp=pw[:, :]: E.activation(o, sp, AF.Gelu)),
                               waits=tuple(gw), inc="s", tag=f"gl{i}_{e}_{ec}")
                        psup_free[k].free = P.now("s")
                        if e < 16:
                            P.emit("gpsimd",
                                   (lambda E, o=actg[aslot][:, ec * 512:(ec + 1) * 512],
                                    gs=G_sb[e % 2][:, :]:
                                    E.tensor_mul(o, o, gs)),
                                   waits=(P.now("s"), GB), inc="g",
                                   tag=f"gm{i}_{e}_{ec}")
                    ACT = P.now("g") if e < 16 else P.now("s")
                    if e < 16:
                        g_free[e % 2].free = ACT
                    actg_free[aslot].ready = ACT
                    wuslot_free[slot].free = P.now("t")
                for c in range(8):
                    k = c % 2
                    pd = psDn[k]
                    nel = len(pl)
                    for ei_, e in enumerate(pl):
                        slot = (i * NEXP + sum(PASSES[:pi_]) + ei_) % 4
                        for ec in range(4):
                            first = (ei_ == 0 and ec == 0)
                            last = (ei_ == nel - 1 and ec == 3)
                            w = []
                            if ec == 0 and c == 0:
                                w.append(actg_free[slot].ready)
                                w.append(WD[ei_])
                            if first and psdn_free[k].free:
                                w.append(psdn_free[k].free)
                            P.emit("tensor",
                                   (lambda E, o=pd[:, :],
                                    l=wd[slot][:, ec * 1024 + c * 128:
                                               ec * 1024 + c * 128 + 128],
                                    r=actg[slot][:, ec * 512:(ec + 1) * 512],
                                    s0=first, s1=last:
                                    E.matmul(o, l, r, start=s0, stop=s1)),
                                   waits=tuple(w),
                                   inc="t" if (last or (c == 7 and ec == 3))
                                   else None,
                                   tag=f"dn{i}_{pi_}_{c}{e}{ec}")
                        if c == 7:
                            wdslot_free[slot].free = P.now("t")
                            actg_free[slot].free = P.now("t")
                    DN = P.now("t")
                    aw = [DN]
                    if pass_first and acc_free.free and c == 0:
                        aw.append(acc_free.free)
                    if pass_first:
                        P.emit("vector",
                               (lambda E, d=acc[:, c * 512:(c + 1) * 512],
                                sp=pd[:, :]: E.tensor_copy(d, sp)),
                               waits=tuple(aw), inc="v", tag=f"dne{i}_{pi_}_{c}")
                    else:
                        P.emit("vector",
                               (lambda E, d=acc[:, c * 512:(c + 1) * 512],
                                sp=pd[:, :]: E.tensor_tensor(d, sp, d, op=ALU.add)),
                               waits=tuple(aw), inc="v", tag=f"dna{i}_{pi_}_{c}")
                    psdn_free[k].free = P.now("v")
                if pass_first and i + 1 < self.nt2:
                    emit_prep_stats(i + 1)
                pass_first = False
            ACC = P.now("v")
            xn2f_free[bi].free = P.now("t")
            gateT_free[bi].free = P.now("t")

            if i + 1 < self.nt2:
                emit_prep_rest(i + 1)

            for u in range(NSUB):
                tw = [ACC] if u == 0 else []
                if psl4_free.free:
                    tw.append(psl4_free.free)
                for c in range(8):
                    w = list(tw) if c == 0 else []
                    P.emit("tensor",
                           (lambda E, o=psL4[:, c * 128:(c + 1) * 128],
                            i_=acc[:, c * 512 + u * 128:c * 512 + u * 128 + 128]:
                            E.transpose(o, i_, self.identf[:, :])),
                           waits=tuple(w), inc="t" if c == 7 else None,
                           tag=f"l4t{i}_{u}{c}")
                    tw = []
                L4T = P.now("t")
                ew = [L4T]
                if u == 0 and outt_free.free:
                    ew.append(outt_free.free)
                P.emit("vector",
                       (lambda E, o=outt[:, u * 1024:(u + 1) * 1024]:
                        E.tensor_copy(o, psL4[:, :])),
                       waits=tuple(ew), inc="v", tag=f"l4e{i}_{u}")
                psl4_free.free = P.now("v")
            LN4 = self.ln_stats(lambda u: outt[:, u * 1024:(u + 1) * 1024],
                                P.now("v"), f"ln4_{i}")
            for u in range(NSUB):
                P.emit("vector",
                       (lambda E, o=outt[:, u * 1024:(u + 1) * 1024],
                        m=self.stats[:, 8 + u:9 + u],
                        r=self.stats[:, 20 + u:21 + u]:
                        E.tensor_scalar(o, o, m, r,
                                        op0=ALU.subtract, op1=ALU.mult)),
                       waits=(LN4,) if u == 0 else (), inc="v",
                       tag=f"l4n{i}_{u}")
                P.emit("vector",
                       (lambda E, o=outt[:, u * 1024:(u + 1) * 1024],
                        xs=x1_sb[bi][:, u * 1024:(u + 1) * 1024]:
                        E.tensor_add(o, o, xs)),
                       waits=(P.now("v"),), inc="v", tag=f"l4a{i}_{u}")
            self.stats_free.free = P.now("v")
            OUT = P.now("v")
            x1_free[bi].free = OUT
            acc_free.free = P.now("t")
            P.dma("gpsimd", self.out_d[i * TT:(i + 1) * TT, :].rearrange(
                "(u p) d -> p u d", p=128),
                outt[:, :].rearrange("p (u d) -> p u d", u=4),
                waits=(OUT,), inc="d_o", tag=f"out{i}")
            outt_free.free = P.now("d_o")



_BUILD_CACHE = {}


def finalize_program(b):
    nc, P = b.nc, b.P
    with ExitStack() as st:
        for name in sorted(P.cnt):
            P.sem_handles[name] = st.enter_context(nc.semaphore(name))
        with nc.Block() as block:
            for eng in ENGS:
                ops = P.ops[eng]

                def runner(E, ops=ops):
                    for op in ops:
                        op(E)

                getattr(block, eng)(runner)
    b.finalized = True
    return b


def get_build(taps=(), phases=("W", "1a", "1b", "2")):
    key = (tuple(sorted(taps)), tuple(phases))
    if key not in _BUILD_CACHE:
        b = Build(taps=taps, phases=phases)
        b.build()
        finalize_program(b)
        _BUILD_CACHE[key] = b
    return _BUILD_CACHE[key]


def make_in_maps(inputs, consts):
    perm = _qk_perm()
    wq = np.ascontiguousarray(np.asarray(inputs["L0_attn_W_q"])[perm])
    wk = np.ascontiguousarray(np.asarray(inputs["L0_attn_W_k"])[perm])
    bq = np.asarray(inputs["L0_attn_b_q"])[perm]
    bk = np.asarray(inputs["L0_attn_b_k"])[perm]
    bv = np.asarray(inputs["L0_attn_b_v"])
    bqkv = np.zeros((128, 24), np.float32)
    for pi, b in enumerate((bq, bk, bv)):
        for oc in range(8):
            bqkv[:, pi * 8 + oc] = b[oc * 128:(oc + 1) * 128]
    shared = {
        "wq": wq, "wk": wk,
        "wv": np.asarray(inputs["L0_attn_W_v"]),
        "wo": np.asarray(inputs["L0_attn_W_o"]),
        "bqkv": bqkv,
        "bo_in": np.asarray(inputs["L0_attn_b_o"]).reshape(1, D),
        "rw_in": np.asarray(inputs["L0_ffn_router_w"]),
        "sup": np.asarray(inputs["L0_ffn_shared_up"]),
        "sdn": np.asarray(inputs["L0_ffn_shared_down"]),
        "rup": np.asarray(inputs["L0_ffn_routed_up"]),
        "rdn": np.asarray(inputs["L0_ffn_routed_down"]),
    }
    shared.update(consts)
    x = np.asarray(inputs["L0_attn_x"])
    maps = []
    for c in range(8):
        m = dict(shared)
        m["xin"] = np.ascontiguousarray(x[c])
        maps.append(m)
    return maps


def run_device(b, in_maps, core_ids):
    if not getattr(b, "finalized", False):
        finalize_program(b)
    return run_bass_kernel_spmd(b.nc, in_maps, core_ids=core_ids)


def kernel(**inputs):
    consts = host_constants()
    b = get_build()
    in_maps = make_in_maps(inputs, consts)
    res = run_device(b, in_maps, core_ids=list(range(8)))
    return np.stack([np.asarray(r["out"]) for r in res.results], axis=0)
